# version 09
# baseline (speedup 1.0000x reference)
import numpy as np
import ml_dtypes
from contextlib import ExitStack

import concourse.bass as bass
import concourse.mybir as mybir
from concourse.bass_utils import run_bass_kernel_spmd

F32 = mybir.dt.float32
BF16 = mybir.dt.bfloat16
I8 = mybir.dt.int8
AF = mybir.ActivationFunctionType
ALU = mybir.AluOpType
AX = mybir.AxisListType

T = 2048
D = 2048
NT = T // 128
ND = D // 128
M = 4096
INTER = 8192
NEG = -1e30
SB = -32.0

TIER_OWN = [256, 768, 1024]
HEAD_OUT = [256, 512, 768]
NHS = [4, 8, 12]
IN_OFF = [0, 256, 1024, 2048]
OUT_OFF = [0, 256, 768, 1536]

ENGINES = ("sync", "gpsimd", "tensor", "vector", "scalar")
import os
SYNC_DMA = bool(os.environ.get("KDBG_SYNCDMA"))

ALL_PHASES = ["P0", "LN1", "KMT", "VMT", "SIM", "OMEM", "QT", "CTX", "CMB",
              "LN2", "F1a", "F2a", "LN3", "B0", "B1", "B2", "BWO", "LN4",
              "F1b", "F2b"]


class Sem:
    def __init__(self, h):
        self.h = h
        self.n = 0


class Phase:
    def __init__(self, nc, name):
        self.nc = nc
        self.name = name
        self.q = {e: [] for e in ENGINES}

    def on(self, eng, fn):
        self.q[eng].append(fn)

    def emit(self):
        with self.nc.Block(self.name) as blk:
            for name in ENGINES:
                fns = self.q[name]
                if not fns:
                    continue
                def body(eng, fns=fns):
                    for f in fns:
                        f(eng)
                getattr(blk, name)(body)

    def dma(self, eng, dst, src, sem, **kw):
        sem.n += 16
        tgt = sem.n
        self.on(eng, lambda e, dst=dst, src=src, kw=kw:
                e.dma_start(dst, src, **kw).then_inc(sem.h, 16))
        if SYNC_DMA:
            self.on(eng, lambda e, s=sem.h, t=tgt: e.wait_ge(s, t))
        return tgt

    def wait(self, eng, sem, tgt):
        if tgt is None:
            return
        self.on(eng, lambda e, s=sem.h, t=tgt: e.wait_ge(s, t))

    def run(self, eng, fn, sem=None, k=1):
        if sem is None:
            self.on(eng, fn)
            return None
        sem.n += k
        tgt = sem.n
        self.on(eng, lambda e, fn=fn, s=sem.h, k=k: fn(e).then_inc(s, k))
        return tgt


def build_nc(debug=(), upto=None):
    nc = bass.Bass("TRN2", target_bir_lowering=False)
    phase_set = ALL_PHASES[:ALL_PHASES.index(upto) + 1] if upto else ALL_PHASES

    def inp(name, shape, dt):
        return nc.dram_tensor(name, list(shape), dt, kind="ExternalInput").ap()

    x_in = inp("x", (T, D), F32)
    wqT = inp("wqT", (D, D), F32)
    wkT = inp("wkT", (D, D), F32)
    wvT = inp("wvT", (D, D), F32)
    woT = inp("woT", (D, D), F32)
    wo_nat = inp("wo_nat", (D, D), F32)
    gwT_rs = inp("gwT_rs", (128, 16), F32)
    gb_bc = inp("gb_bc", (128, 1), F32)
    membT = inp("membT", (D, M), F32)
    WQd = inp("WQ", (D, 1536), F32)
    WKd = inp("WK", (D, 1536), F32)
    WVd = inp("WV", (D, 1536), F32)
    WOd = inp("WO", (1536, D), F32)
    FKd = {0: inp("FK0", (1792, 32), F32), 1: inp("FK1", (1024, 64), F32)}
    FVd = {0: inp("FV0", (1792, 32), F32), 1: inp("FV1", (1024, 64), F32)}
    PKbd = {0: inp("PK0bd", (32, 256), F32), 1: inp("PK1bd", (64, 512), F32)}
    PVbd = {0: inp("PV0bd", (32, 256), F32), 1: inp("PV1bd", (64, 512), F32)}
    wgR = inp("wgR", (16, 16, 128, 512), I8)
    wuR = inp("wuR", (16, 16, 128, 512), I8)
    wd_i8 = inp("wd", (INTER, D), I8)
    sg_rs = inp("sg_rs", (128, 64), F32)
    su_rs = inp("su_rs", (128, 64), F32)
    sdbc_d = inp("sdbc", (128, D), F32)
    ident_d = inp("ident", (128, 128), BF16)
    maskw_d = inp("maskw", (128, 384), F32)

    out_d = nc.dram_tensor("out", [T, D], F32, kind="ExternalOutput").ap()

    scr = {}
    def scratch(name, shape, dt):
        scr[name] = nc.dram_tensor("scr_" + name, list(shape), dt,
                                   kind="Internal").ap()
        return scr[name]

    xres = scratch("xres", (NT, 128, D), F32)
    KT = scratch("KT", (NT, ND, 128, 128), BF16)
    QT = scratch("QT", (NT, ND, 128, 128), BF16)
    Vh = scratch("Vh", (NT, 128, D), BF16)
    kmemT = scratch("kmemT", (8, ND, 128, 512), BF16)
    vmem2 = scratch("vmem2", (4, 32, 128, 512), BF16)
    simd = scratch("simd", (NT, 128, M), F32)
    omemp = scratch("omemp", (NT, 128, D), BF16)
    ctxT = scratch("ctxT", (NT, ND, 128, 128), BF16)
    fuse = scratch("fuse", (64, 128, T), BF16)
    ObT = scratch("ObT", (12, 128, T), BF16)
    QTb = scratch("QTb", (24, 64, T), BF16)
    KTb = scratch("KTb", (24, 64, T), BF16)
    xnloT = scratch("xnloT", (NT, ND, 128, 128), BF16)
    wklo = scratch("wklo", (ND, ND, 128, 128), BF16)
    kmemTlo = scratch("kmemTlo", (8, ND, 128, 512), BF16)

    dbg_outs = {}
    dbg_sbuf = []

    with ExitStack() as stk:
        E = stk.enter_context

        xnT = E(nc.sbuf_tensor("xnT", [128, ND * T], BF16))
        ident = E(nc.sbuf_tensor("c_ident", [128, 128], BF16))
        maskw = E(nc.sbuf_tensor("c_maskw", [128, 384], F32))
        sgc = E(nc.sbuf_tensor("c_sg", [128, 64], F32))
        suc = E(nc.sbuf_tensor("c_su", [128, 64], F32))
        gbb = E(nc.sbuf_tensor("c_gb", [128, 1], F32))
        gcomb = E(nc.sbuf_tensor("c_gcomb", [128, 16], BF16))
        gwcol = E(nc.sbuf_tensor("c_gwcol", [128, 16], BF16))
        fk_sb = {0: E(nc.sbuf_tensor("c_fk0", [128, 14 * 32], BF16)),
                 1: E(nc.sbuf_tensor("c_fk1", [128, 8 * 64], BF16))}
        fv_sb = {0: E(nc.sbuf_tensor("c_fv0", [128, 14 * 32], BF16)),
                 1: E(nc.sbuf_tensor("c_fv1", [128, 8 * 64], BF16))}
        pkbd = {0: E(nc.sbuf_tensor("c_pk0", [32, 256], BF16)),
                1: E(nc.sbuf_tensor("c_pk1", [64, 512], BF16))}
        pvbd = {0: E(nc.sbuf_tensor("c_pv0", [32, 256], BF16)),
                1: E(nc.sbuf_tensor("c_pv1", [64, 512], BF16))}
        cmisc = E(nc.sbuf_tensor("c_misc", [128, 4], F32))
        C_NB32, C_S128, C_S64 = 0, 1, 2

        sems = [Sem(E(nc.semaphore(f"s{i}"))) for i in range(16)]
        sL, sG, sSt, sPE, sV, sA = sems[0:6]
        SP = sems[6:16]
        global _dbg_sems
        _dbg_sems = sems

        if "P0" in phase_set:
            ph = Phase(nc, "P0")
            with ExitStack() as ps:
                wo_sb = ps.enter_context(nc.sbuf_tensor("p0_wo", [128, 1024], BF16))
                gps = [ps.enter_context(
                    nc.psum_tensor(f"p0_ps{j}", [128, 1], F32)) for j in range(8)]
                ph.dma("sync", ident[:], ident_d, sL)
                ph.dma("sync", maskw[:], maskw_d, sL)
                ph.dma("sync", sgc[:], sg_rs, sL)
                ph.dma("sync", suc[:], su_rs, sL)
                tc0 = ph.dma("sync", gbb[:], gb_bc, sL)
                tgw = ph.dma("gpsimd", gwcol[:], gwT_rs, sG)
                ph.run("vector", lambda v: v.memset(
                    cmisc[:, C_NB32:C_NB32 + 1], -32.0))
                ph.run("vector", lambda v: v.memset(
                    cmisc[:, C_S128:C_S128 + 1], 1.0 / float(np.sqrt(128.0))))
                ph.run("vector", lambda v: v.memset(
                    cmisc[:, C_S64:C_S64 + 1], 0.125))
                for tier, nh, nj, w in ((0, 4, 14, 32), (1, 8, 8, 64)):
                    for j in range(nj):
                        ph.dma("gpsimd", fk_sb[tier][:, j * w:(j + 1) * w],
                               FKd[tier][j * 128:(j + 1) * 128, :], sG)
                        ph.dma("gpsimd", fv_sb[tier][:, j * w:(j + 1) * w],
                               FVd[tier][j * 128:(j + 1) * 128, :], sG)
                    ph.dma("gpsimd", pkbd[tier][:], PKbd[tier], sG)
                    ph.dma("gpsimd", pvbd[tier][:], PVbd[tier], sG)
                tload = sG.n
                last_grp = None
                for jg in range(2):
                    for c in range(16):
                        ph.wait("gpsimd", sPE, last_grp)
                        tw = ph.dma("gpsimd", wo_sb[:],
                                    wo_nat[c * 128:(c + 1) * 128,
                                           jg * 1024:jg * 1024 + 1024], sG)
                        ph.wait("tensor", sG, tw)
                        if jg == 0 and c == 0:
                            ph.wait("tensor", sG, tload)
                        for j8 in range(8):
                            last_grp = ph.run(
                                "tensor", lambda t, c=c, j8=j8: t.matmul(
                                    gps[j8][:], wo_sb[:, j8 * 128:(j8 + 1) * 128],
                                    gwcol[:, c:c + 1],
                                    start=(c == 0), stop=(c == 15)),
                                sPE if j8 == 7 else None)
                    ph.wait("vector", sPE, last_grp)
                    for j8 in range(8):
                        ph.run("vector", lambda v, jg=jg, j8=j8: v.tensor_copy(
                            gcomb[:, jg * 8 + j8:jg * 8 + j8 + 1], gps[j8][:]), sV)
                ph.wait("vector", sL, tc0)
                ph.wait("vector", sG, tload)
            ph.emit()

        def ln_phase(name, src_tile, emit_lo=False):
            ph = Phase(nc, name)
            with ExitStack() as ps:
                EE = ps.enter_context
                xsb = EE(nc.sbuf_tensor(name + "_x", [128, 2 * D], F32))
                scr_a = EE(nc.sbuf_tensor(name + "_scr", [128, D], F32))
                stats = EE(nc.sbuf_tensor(name + "_st", [128, 8], F32))
                xn_tm = EE(nc.sbuf_tensor(name + "_xn", [128, 2 * D], BF16))
                if emit_lo:
                    xf = EE(nc.sbuf_tensor(name + "_xf", [128, D], F32))
                    lob = EE(nc.sbuf_tensor(name + "_lo", [128, 2 * D], BF16))
                    lostg = [EE(nc.sbuf_tensor(name + f"_ls{i}",
                                               [128, ND * 128], BF16))
                             for i in range(2)]
                tp = [EE(nc.psum_tensor(name + f"_tp{i}", [128, 128], BF16))
                      for i in range(2)]
                sX = [SP[0], SP[1]]
                sLo = [SP[2], SP[3]]
                SUM, SSQ, MEAN, MSQ, VAR, SD, RSTD = range(7)
                act_done = []
                tr_last = []
                xn_t = []
                ev_t = []
                lo_st = []
                for tt in range(NT):
                    buf = xsb[:, (tt % 2) * D:(tt % 2) * D + D]
                    nbuf = xn_tm[:, (tt % 2) * D:(tt % 2) * D + D]
                    if tt >= 2:
                        ph.wait("gpsimd", sA, act_done[tt - 2])
                        ph.wait("gpsimd", sV, xn_t[tt - 2])
                    tx = ph.dma("gpsimd", buf, src_tile(tt), sX[tt % 2])
                    ph.wait("scalar", sX[tt % 2], tx)
                    ph.run("scalar", lambda s, b=buf: s.activation(
                        scr_a[:], b, AF.Copy, accum_out=stats[:, SUM:SUM + 1]))
                    tsq = ph.run("scalar", lambda s, b=buf: s.activation(
                        scr_a[:], b, AF.Square, accum_out=stats[:, SSQ:SSQ + 1]),
                        sA)
                    act_done.append(tsq)
                    ph.wait("vector", sA, tsq)
                    ph.run("vector", lambda v: v.tensor_scalar_mul(
                        stats[:, MEAN:MEAN + 1], stats[:, SUM:SUM + 1], 1.0 / D))
                    ph.run("vector", lambda v: v.tensor_mul(
                        stats[:, MSQ:MSQ + 1], stats[:, MEAN:MEAN + 1],
                        stats[:, MEAN:MEAN + 1]))
                    ph.run("vector", lambda v: v.tensor_scalar(
                        stats[:, VAR:VAR + 1], stats[:, SSQ:SSQ + 1], 1.0 / D,
                        None, op0=ALU.mult))
                    ph.run("vector", lambda v: v.tensor_sub(
                        stats[:, VAR:VAR + 1], stats[:, VAR:VAR + 1],
                        stats[:, MSQ:MSQ + 1]))
                    tv = ph.run("vector", lambda v: v.tensor_scalar_add(
                        stats[:, VAR:VAR + 1], stats[:, VAR:VAR + 1], 1e-5), sV)
                    ph.wait("scalar", sV, tv)
                    tsd = ph.run("scalar", lambda s: s.activation(
                        stats[:, SD:SD + 1], stats[:, VAR:VAR + 1], AF.Sqrt), sA)
                    ph.wait("vector", sA, tsd)
                    trc = ph.run("vector", lambda v: v.reciprocal(
                        stats[:, RSTD:RSTD + 1], stats[:, SD:SD + 1]), sV)
                    ph.wait("vector", sV, trc)
                    if tt >= 2:
                        ph.wait("vector", sPE, tr_last[tt - 2])
                    if not emit_lo:
                        txn = ph.run("vector", lambda v, b=buf, nb=nbuf:
                                     v.tensor_scalar(
                                         nb, b, stats[:, MEAN:MEAN + 1],
                                         stats[:, RSTD:RSTD + 1],
                                         op0=ALU.subtract, op1=ALU.mult), sV)
                    else:
                        lbuf = lob[:, (tt % 2) * D:(tt % 2) * D + D]
                        ph.run("vector", lambda v, b=buf: v.tensor_scalar(
                            xf[:], b, stats[:, MEAN:MEAN + 1],
                            stats[:, RSTD:RSTD + 1],
                            op0=ALU.subtract, op1=ALU.mult))
                        ph.run("vector", lambda v, nb=nbuf:
                               v.tensor_copy(nb, xf[:]))
                        txn = ph.run("vector", lambda v, nb=nbuf, lb=lbuf:
                                     v.tensor_sub(lb, xf[:], nb), sV)
                    xn_t.append(txn)
                    ph.wait("tensor", sV, txn)
                    tpt = None
                    for j in range(ND):
                        pb = tp[j % 2]
                        if len(ev_t) >= 2:
                            ph.wait("tensor", sV, ev_t[-2])
                        tpt = ph.run("tensor", lambda t, j=j, pb=pb, nb=nbuf:
                                     t.transpose(pb[:], nb[:, j * 128:(j + 1) * 128],
                                                 ident[:]), sPE)
                        ph.wait("vector", sPE, tpt)
                        dst = xnT[:, j * T + tt * 128: j * T + tt * 128 + 128]
                        ev = ph.run("vector", lambda v, pb=pb, dst=dst:
                                    v.tensor_copy(dst, pb[:]), sV)
                        ev_t.append(ev)
                    if emit_lo:
                        ls = lostg[tt % 2]
                        if tt >= 2:
                            ph.wait("vector", sLo[tt % 2], lo_st[tt - 2])
                        for j in range(ND):
                            pb = tp[j % 2]
                            if len(ev_t) >= 2:
                                ph.wait("tensor", sV, ev_t[-2])
                            tpt = ph.run("tensor",
                                         lambda t, j=j, pb=pb, lb=lbuf:
                                         t.transpose(pb[:],
                                                     lb[:, j * 128:(j + 1) * 128],
                                                     ident[:]), sPE)
                            ph.wait("vector", sPE, tpt)
                            ev = ph.run("vector", lambda v, pb=pb, j=j, ls=ls:
                                        v.tensor_copy(
                                            ls[:, j * 128:(j + 1) * 128],
                                            pb[:]), sV)
                            ev_t.append(ev)
                        ph.wait("sync", sV, ev_t[-1])
                        tst = ph.dma("sync",
                                     xnloT[tt].rearrange("j p c -> p j c"),
                                     ls[:].rearrange("p (j c) -> p j c", c=128),
                                     sLo[tt % 2])
                        lo_st.append(tst)
                    tr_last.append(tpt)
                if emit_lo:
                    ph.wait("sync", sLo[0], sLo[0].n)
                    ph.wait("sync", sLo[1], sLo[1].n)
                if name + ":xnT" in debug:
                    ph.wait("sync", sV, ev_t[-1])
                    o = nc.dram_tensor("dbg_" + name + "_xnT", [128, ND * T],
                                       BF16, kind="ExternalOutput").ap()
                    dbg_outs[name + "_xnT"] = o
                    td = ph.dma("sync", o, xnT[:], sSt)
                    ph.wait("sync", sSt, td)
            ph.emit()

        if "LN1" in phase_set:
            ln_phase("LN1", lambda tt: x_in[tt * 128:(tt + 1) * 128, :],
                     emit_lo=True)

        def projT_phase(name, w_dram, dst_T, do_mem):
            ph = Phase(nc, name)
            with ExitStack() as ps:
                EE = ps.enter_context
                wsb = EE(nc.sbuf_tensor(name + "_w", [128, ND * D], BF16))
                stg = [EE(nc.sbuf_tensor(name + f"_s{i}", [128, 512], BF16))
                       for i in range(4)]
                pps = [EE(nc.psum_tensor(name + f"_p{i}", [128, 512], F32))
                       for i in range(2)]
                sW = SP[0]
                sM = SP[1]
                sWL = [SP[2], SP[3]]
                sSg = [SP[4], SP[5], SP[6], SP[7]]
                sWS = SP[8]
                if do_mem:
                    mstg = EE(nc.sbuf_tensor(name + "_mf", [128, ND * 256], F32))
                    kmh = EE(nc.sbuf_tensor(name + "_mh", [128, ND * 512], BF16))
                    kml = EE(nc.sbuf_tensor(name + "_ml", [128, ND * 512], BF16))
                    wstg = mstg[:, 0:D]
                    wlos = kml[:, 0:D]
                if not do_mem:
                    tw = 0
                    for j in range(ND):
                        tw = ph.dma("gpsimd", wsb[:, j * D:(j + 1) * D],
                                    w_dram[j * 128:(j + 1) * 128, :], sW)
                    ph.wait("tensor", sW, tw)
                else:
                    wl_prev = None
                    for j in range(ND):
                        twf = ph.dma("gpsimd", wstg[:],
                                     w_dram[j * 128:(j + 1) * 128, :], sW)
                        ph.wait("vector", sW, twf)
                        ph.run("vector", lambda v, j=j: v.tensor_copy(
                            wsb[:, j * D:(j + 1) * D], wstg[:]))
                        tlo = ph.run("vector", lambda v, j=j: v.tensor_sub(
                            wlos[:], wstg[:], wsb[:, j * D:(j + 1) * D]), sV)
                        ph.wait("sync", sV, tlo)
                        twl = ph.dma("sync",
                                     wklo[:, j].rearrange("o p c -> p o c"),
                                     wlos[:].rearrange("p (o c) -> p o c", c=128),
                                     sWS)
                        ph.wait("gpsimd", sWS, twl)
                        wl_prev = tlo
                    ph.wait("tensor", sV, wl_prev)
                ev_t, st_t = [], []

                def unit(lhs_list, rhs_list, dst, src_of):
                    u = len(ev_t)
                    pb = pps[u % 2]
                    sb = stg[u % 4]
                    if u >= 2:
                        ph.wait("tensor", sV, ev_t[u - 2])
                    n = len(lhs_list)
                    tpe = None
                    for i in range(n):
                        tpe = ph.run("tensor",
                                     lambda t, l=lhs_list[i], r=rhs_list[i], pb=pb,
                                     i=i, n=n:
                                     t.matmul(pb[:], l, r, start=(i == 0),
                                              stop=(i == n - 1)),
                                     sPE if i == n - 1 else None)
                    ph.wait("vector", sPE, tpe)
                    if u >= 4:
                        ph.wait("vector", sSg[u % 4], st_t[u - 4])
                    ev = ph.run("vector", lambda v, pb=pb, sb=sb:
                                v.tensor_copy(sb[:], pb[:]), sV)
                    ev_t.append(ev)
                    ph.wait("sync", sV, ev)
                    st = ph.dma("sync", dst, src_of(sb), sSg[u % 4])
                    st_t.append(st)
                    return tpe

                for ot in range(ND):
                    for tc in range(4):
                        lhs = [wsb[:, j * D + ot * 128: j * D + ot * 128 + 128]
                               for j in range(ND)]
                        rhs = [xnT[:, j * T + tc * 512: j * T + tc * 512 + 512]
                               for j in range(ND)]
                        dst = dst_T[tc * 4:(tc + 1) * 4, ot].rearrange(
                            "t p c -> p t c")
                        unit(lhs, rhs, dst,
                             lambda sb: sb.rearrange("p (t c) -> p t c", c=128))
                if do_mem:
                    wlr = EE(nc.sbuf_tensor(name + "_wlr", [128, 2 * ND * 128],
                                            BF16))
                    lstg = [EE(nc.sbuf_tensor(name + f"_lo{i}", [128, 512], BF16))
                            for i in range(4)]
                    mem_pe = []
                    wlr_t = {}
                    for mc in range(8):
                        if mc >= 1:
                            ph.wait("gpsimd", sPE, mem_pe[-1])
                        tsp = None
                        for jh in range(2):
                            tm = 0
                            for jj in range(8):
                                j = jh * 8 + jj
                                tm = ph.dma("gpsimd",
                                            mstg[:, jj * 512:(jj + 1) * 512],
                                            membT[j * 128:(j + 1) * 128,
                                                  mc * 512:(mc + 1) * 512], sM)
                            ph.wait("vector", sM, tm)
                            off = jh * 8 * 512
                            ph.run("vector", lambda v, off=off: v.tensor_copy(
                                kmh[:, off:off + 8 * 512], mstg[:]))
                            tsp = ph.run("vector", lambda v, off=off: v.tensor_sub(
                                kml[:, off:off + 8 * 512], mstg[:],
                                kmh[:, off:off + 8 * 512]), sV)
                            if jh == 0:
                                ph.wait("gpsimd", sV, tsp)
                        ph.wait("tensor", sV, tsp)
                        tpe = None
                        for ot in range(ND):
                            ring = (mc * ND + ot) % 2
                            gw = wlr_t.get((mc * ND + ot) - 2)
                            ph.wait("gpsimd", sPE, gw)
                            twl = ph.dma("gpsimd",
                                         wlr[:, ring * ND * 128:
                                             (ring + 1) * ND * 128].rearrange(
                                             "p (j c) -> p j c", c=128),
                                         wklo[ot].rearrange("j p c -> p j c"),
                                         sWL[ring])
                            u = len(ev_t)
                            pb = pps[u % 2]
                            sb = stg[u % 4]
                            lb = lstg[u % 4]
                            if u >= 2:
                                ph.wait("tensor", sV, ev_t[u - 2])
                            for j in range(ND):
                                ph.run("tensor", lambda t, j=j, ot=ot, pb=pb:
                                       t.matmul(
                                           pb[:],
                                           wsb[:, j * D + ot * 128:
                                               j * D + ot * 128 + 128],
                                           kmh[:, j * 512:(j + 1) * 512],
                                           start=(j == 0), stop=False))
                            ph.wait("tensor", sWL[ring], twl)
                            for j in range(ND):
                                ph.run("tensor", lambda t, j=j, ring=ring, pb=pb:
                                       t.matmul(
                                           pb[:],
                                           wlr[:, ring * ND * 128 + j * 128:
                                               ring * ND * 128 + j * 128 + 128],
                                           kmh[:, j * 512:(j + 1) * 512],
                                           start=False, stop=False))
                            tpe = None
                            for j in range(ND):
                                tpe = ph.run(
                                    "tensor", lambda t, j=j, ot=ot, pb=pb:
                                    t.matmul(
                                        pb[:],
                                        wsb[:, j * D + ot * 128:
                                            j * D + ot * 128 + 128],
                                        kml[:, j * 512:(j + 1) * 512],
                                        start=False, stop=(j == ND - 1)),
                                    sPE if j == ND - 1 else None)
                            wlr_t[mc * ND + ot] = tpe
                            ph.wait("vector", sPE, tpe)
                            if u >= 4:
                                ph.wait("vector", sSg[u % 4], st_t[u - 4])
                            ph.run("vector", lambda v, pb=pb, sb=sb:
                                   v.tensor_copy(sb[:], pb[:]))
                            ev = ph.run("vector", lambda v, pb=pb, sb=sb, lb=lb:
                                        v.tensor_sub(lb[:], pb[:], sb[:]), sV)
                            ev_t.append(ev)
                            ph.wait("sync", sV, ev)
                            st = ph.dma("sync", kmemT[mc, ot], sb[:], sSg[u % 4])
                            st = ph.dma("sync", kmemTlo[mc, ot], lb[:],
                                        sSg[u % 4])
                            st_t.append(st)
                        mem_pe.append(tpe)
                for ss in sSg:
                    ph.wait("sync", ss, ss.n)
                ph.wait("sync", sWS, sWS.n)
            ph.emit()

        if "KMT" in phase_set:
            projT_phase("KMT", wkT, KT, True)

        if "VMT" in phase_set:
            ph = Phase(nc, "VMT")
            with ExitStack() as ps:
                EE = ps.enter_context
                wsb = EE(nc.sbuf_tensor("VMT_w", [128, ND * D], BF16))
                stg = [EE(nc.sbuf_tensor(f"VMT_s{i}", [128, 512], BF16))
                       for i in range(4)]
                pps = [EE(nc.psum_tensor(f"VMT_p{i}", [128, 512], F32))
                       for i in range(2)]
                mts = EE(nc.sbuf_tensor("VMT_mt", [128, 2 * ND * 512], BF16))
                sW = SP[0]
                sMh = [SP[1], SP[2]]
                sSg = [SP[4], SP[5], SP[6], SP[7]]
                tw = 0
                for j in range(ND):
                    tw = ph.dma("gpsimd", wsb[:, j * D:(j + 1) * D],
                                wvT[j * 128:(j + 1) * 128, :], sW)
                ph.wait("tensor", sW, tw)
                ev_t, st_t = [], []

                def unit(lhs_list, rhs_list, dst):
                    u = len(ev_t)
                    pb = pps[u % 2]
                    sb = stg[u % 4]
                    if u >= 2:
                        ph.wait("tensor", sV, ev_t[u - 2])
                    n = len(lhs_list)
                    tpe = None
                    for i in range(n):
                        tpe = ph.run("tensor",
                                     lambda t, l=lhs_list[i], r=rhs_list[i], pb=pb,
                                     i=i, n=n:
                                     t.matmul(pb[:], l, r, start=(i == 0),
                                              stop=(i == n - 1)),
                                     sPE if i == n - 1 else None)
                    ph.wait("vector", sPE, tpe)
                    if u >= 4:
                        ph.wait("vector", sSg[u % 4], st_t[u - 4])
                    ev = ph.run("vector", lambda v, pb=pb, sb=sb:
                                v.tensor_copy(sb[:], pb[:]), sV)
                    ev_t.append(ev)
                    ph.wait("sync", sV, ev)
                    st = ph.dma("sync", dst, sb[:], sSg[u % 4])
                    st_t.append(st)
                    return tpe

                for tt in range(NT):
                    for oc in range(4):
                        lhs = [xnT[:, j * T + tt * 128: j * T + tt * 128 + 128]
                               for j in range(ND)]
                        rhs = [wsb[:, j * D + oc * 512: j * D + oc * 512 + 512]
                               for j in range(ND)]
                        unit(lhs, rhs, Vh[tt][:, oc * 512:(oc + 1) * 512])
                mem_pe = []
                for mc in range(8):
                    half = (mc % 2) * ND * 512
                    if mc >= 2:
                        ph.wait("gpsimd", sPE, mem_pe[mc - 2])
                    tm = 0
                    for j in range(ND):
                        tm = ph.dma("gpsimd",
                                    mts[:, half + j * 512: half + j * 512 + 512],
                                    membT[j * 128:(j + 1) * 128,
                                          mc * 512:(mc + 1) * 512], sMh[mc % 2])
                    ph.wait("tensor", sMh[mc % 2], tm)
                    tpe = None
                    for ml in range(4):
                        mt = mc * 4 + ml
                        for oc in range(4):
                            lhs = [mts[:, half + j * 512 + ml * 128:
                                       half + j * 512 + ml * 128 + 128]
                                   for j in range(ND)]
                            rhs = [wsb[:, j * D + oc * 512: j * D + oc * 512 + 512]
                                   for j in range(ND)]
                            tpe = unit(lhs, rhs, vmem2[oc, mt])
                    mem_pe.append(tpe)
                for ss in sSg:
                    ph.wait("sync", ss, ss.n)
            ph.emit()

        if "SIM" in phase_set:
            ph = Phase(nc, "SIM")
            with ExitStack() as ps:
                EE = ps.enter_context
                kms = EE(nc.sbuf_tensor("SIM_km", [128, 2 * ND * 512], BF16))
                kls = EE(nc.sbuf_tensor("SIM_kl", [128, 2 * ND * 512], BF16))
                xlr = EE(nc.sbuf_tensor("SIM_xl", [128, 2 * ND * 128], BF16))
                stg = [EE(nc.sbuf_tensor(f"SIM_s{i}", [128, 512], F32))
                       for i in range(4)]
                pps = [EE(nc.psum_tensor(f"SIM_p{i}", [128, 512], F32))
                       for i in range(2)]
                sMh = [SP[0], SP[1]]
                sXl = [SP[2], SP[3]]
                sSg = [SP[4], SP[5], SP[6], SP[7]]
                ev_t, st_t, mem_pe = [], [], []
                xlr_t = {}
                for mc in range(8):
                    half = (mc % 2) * ND * 512
                    if mc >= 2:
                        ph.wait("gpsimd", sPE, mem_pe[mc - 2])
                    ph.dma("gpsimd",
                           kms[:, half:half + ND * 512].rearrange(
                               "p (j c) -> p j c", c=512),
                           kmemT[mc].rearrange("j p c -> p j c"), sMh[mc % 2])
                    tm = ph.dma("gpsimd",
                                kls[:, half:half + ND * 512].rearrange(
                                    "p (j c) -> p j c", c=512),
                                kmemTlo[mc].rearrange("j p c -> p j c"),
                                sMh[mc % 2])
                    ph.wait("tensor", sMh[mc % 2], tm)
                    tpe = None
                    for tt in range(NT):
                        idx = mc * NT + tt
                        ring = idx % 2
                        gw = xlr_t.get(idx - 2)
                        ph.wait("gpsimd", sPE, gw)
                        txl = ph.dma("gpsimd",
                                     xlr[:, ring * ND * 128:
                                         (ring + 1) * ND * 128].rearrange(
                                         "p (j c) -> p j c", c=128),
                                     xnloT[tt].rearrange("j p c -> p j c"),
                                     sXl[ring])
                        u = len(ev_t)
                        pb = pps[u % 2]
                        sb = stg[u % 4]
                        if u >= 2:
                            ph.wait("tensor", sV, ev_t[u - 2])
                        for j in range(ND):
                            ph.run("tensor",
                                   lambda t, j=j, tt=tt, pb=pb, half=half:
                                   t.matmul(pb[:],
                                            xnT[:, j * T + tt * 128:
                                                j * T + tt * 128 + 128],
                                            kms[:, half + j * 512:
                                                half + j * 512 + 512],
                                            start=(j == 0), stop=False))
                        ph.wait("tensor", sXl[ring], txl)
                        for j in range(ND):
                            ph.run("tensor",
                                   lambda t, j=j, ring=ring, pb=pb, half=half:
                                   t.matmul(pb[:],
                                            xlr[:, ring * ND * 128 + j * 128:
                                                ring * ND * 128 + j * 128 + 128],
                                            kms[:, half + j * 512:
                                                half + j * 512 + 512],
                                            start=False, stop=False))
                        for j in range(ND):
                            tpe = ph.run(
                                "tensor",
                                lambda t, j=j, tt=tt, pb=pb, half=half:
                                t.matmul(pb[:],
                                         xnT[:, j * T + tt * 128:
                                             j * T + tt * 128 + 128],
                                         kls[:, half + j * 512:
                                             half + j * 512 + 512],
                                         start=False, stop=(j == ND - 1)),
                                sPE if j == ND - 1 else None)
                        xlr_t[idx] = tpe
                        ph.wait("vector", sPE, tpe)
                        if u >= 4:
                            ph.wait("vector", sSg[u % 4], st_t[u - 4])
                        ev = ph.run("vector", lambda v, pb=pb, sb=sb:
                                    v.tensor_scalar_mul(
                                        sb[:], pb[:],
                                        1.0 / float(np.sqrt(D))), sV)
                        ev_t.append(ev)
                        ph.wait("sync", sV, ev)
                        st = ph.dma("sync", simd[tt][:, mc * 512:(mc + 1) * 512],
                                    sb[:], sSg[u % 4])
                        st_t.append(st)
                    mem_pe.append(tpe)
                for ss in sSg:
                    ph.wait("sync", ss, ss.n)
            ph.emit()

        if "OMEM" in phase_set:
            ph = Phase(nc, "OMEM")
            GT = 2
            with ExitStack() as ps:
                EE = ps.enter_context
                simg = EE(nc.sbuf_tensor("OM_sim", [128, GT * M], F32))
                wfs = EE(nc.sbuf_tensor("OM_wf", [128, GT * M], BF16))
                wft = EE(nc.sbuf_tensor("OM_wft", [128, GT * M], BF16))
                wexp = EE(nc.sbuf_tensor("OM_we", [128, 2 * M], BF16))
                tk = EE(nc.sbuf_tensor("OM_tk", [128, 40], F32))
                vms = EE(nc.sbuf_tensor("OM_vm", [128, 2 * 16 * 512], BF16))
                stg = [EE(nc.sbuf_tensor(f"OM_s{i}", [128, 512], BF16))
                       for i in range(4)]
                tp = [EE(nc.psum_tensor(f"OM_tp{i}", [128, 128], BF16))
                      for i in range(2)]
                om = [EE(nc.psum_tensor(f"OM_p{i}", [128, 512], F32))
                      for i in range(GT)]
                sSim = SP[0]
                sVm = [SP[1], SP[2]]
                sSg = [SP[4], SP[5], SP[6], SP[7]]
                TA, TB, T8, NT1, Z8, LNZ, B2, TT8, E8 = (
                    0, 8, 16, 24, 25, 26, 27, 28, 32)
                st_t = []
                prev_group_ev = None
                half_pe = {}
                twf_all = []
                g_tr_last = []
                for g in range(NT // GT):
                    if g >= 1:
                        ph.wait("gpsimd", sPE, g_tr_last[g - 1])
                        ph.wait("gpsimd", sV, twf_all[-1])
                    tload = ph.dma("gpsimd",
                                   simg[:].rearrange("p (t c) -> p t c", c=M),
                                   simd[g * GT:(g + 1) * GT].rearrange(
                                       "t p c -> p t c"), sSim)
                    tr_ev = []
                    tpt = None
                    for tl in range(GT):
                        sl = simg[:, tl * M:(tl + 1) * M]
                        wf = wfs[:, tl * M:(tl + 1) * M]
                        if tl == 0:
                            ph.wait("vector", sSim, tload)
                        ph.run("vector", lambda v, sl=sl: v.max(
                            out=tk[:, TA:TA + 8], in_=sl[:, :M // 2]))
                        thm = ph.run("vector", lambda v, sl=sl: v.max(
                            out=tk[:, TB:TB + 8], in_=sl[:, M // 2:]), sV)
                        ph.wait("vector", sV, thm)
                        tmax = ph.run("vector", lambda v: v.max(
                            out=tk[:, T8:T8 + 8], in_=tk[:, TA:TA + 16]), sV)
                        ph.wait("vector", sV, tmax)
                        ph.run("vector", lambda v: v.tensor_scalar_mul(
                            tk[:, NT1:NT1 + 1], tk[:, T8:T8 + 1], -1.0))
                        tnt = ph.run("vector", lambda v: v.tensor_copy(
                            tk[:, TT8:TT8 + 1], tk[:, T8 + 7:T8 + 8]), sV)
                        ph.wait("scalar", sV, tnt)
                        tz = ph.run("scalar", lambda s: s.activation(
                            tk[:, E8:E8 + 8], tk[:, T8:T8 + 8], AF.Exp,
                            bias=tk[:, NT1:NT1 + 1],
                            accum_out=tk[:, Z8:Z8 + 1]), sA)
                        ph.wait("scalar", sA, tz)
                        tln = ph.run("scalar", lambda s: s.activation(
                            tk[:, LNZ:LNZ + 1], tk[:, Z8:Z8 + 1], AF.Ln), sA)
                        ph.wait("vector", sA, tln)
                        tb2 = ph.run("vector", lambda v: v.tensor_scalar(
                            tk[:, B2:B2 + 1], tk[:, LNZ:LNZ + 1],
                            tk[:, T8:T8 + 1], -1.0,
                            op0=ALU.add, op1=ALU.mult), sV)
                        ph.wait("scalar", sV, tb2)
                        we = wexp[:, (tl % 2) * M:(tl % 2) * M + M]
                        if len(twf_all) >= 2:
                            ph.wait("scalar", sV, twf_all[-2])
                        twe = ph.run("scalar", lambda s, sl=sl, we=we: s.activation(
                            we, sl, AF.Exp, bias=tk[:, B2:B2 + 1]), sA)
                        ph.wait("vector", sA, twe)
                        twf = ph.run("vector", lambda v, sl=sl, we=we, wf=wf:
                                     v.scalar_tensor_tensor(
                                         wf, sl, tk[:, TT8:TT8 + 1], we,
                                         op0=ALU.is_ge, op1=ALU.mult), sV)
                        twf_all.append(twf)
                        ph.wait("tensor", sV, twf)
                        for mb in range(32):
                            pb = tp[mb % 2]
                            if len(tr_ev) >= 2:
                                ph.wait("tensor", sV, tr_ev[-2])
                            tpt = ph.run("tensor", lambda t, wf=wf, mb=mb, pb=pb:
                                         t.transpose(
                                             pb[:], wf[:, mb * 128:(mb + 1) * 128],
                                             ident[:]), sPE)
                            ph.wait("vector", sPE, tpt)
                            dst = wft[:, tl * M + mb * 128:tl * M + mb * 128 + 128]
                            ev = ph.run("vector", lambda v, pb=pb, dst=dst:
                                        v.tensor_copy(dst, pb[:]), sV)
                            tr_ev.append(ev)
                    g_tr_last.append(tpt)
                    if g == 0 and "OM:wft" in debug:
                        o = nc.dram_tensor("dbg_OM_wft", [128, GT * M], BF16,
                                           kind="ExternalOutput").ap()
                        dbg_outs["OM_wft"] = o
                        ph.wait("sync", sV, tr_ev[-1])
                        tdw = ph.dma("sync", o, wft[:], sSt)
                        ph.wait("sync", sSt, tdw)
                    for oc in range(4):
                        for hf in range(2):
                            half = hf * 16 * 512
                            prev = half_pe.get((oc - 1, hf) if oc > 0
                                               else (3, hf))
                            ph.wait("gpsimd", sPE, prev)
                            tvm = ph.dma("gpsimd",
                                         vms[:, half:half + 16 * 512].rearrange(
                                             "p (m c) -> p m c", c=512),
                                         vmem2[oc, hf * 16:(hf + 1) * 16].rearrange(
                                             "m p c -> p m c"), sVm[hf])
                            ph.wait("tensor", sVm[hf], tvm)
                            if hf == 0:
                                ph.wait("tensor", sV, tr_ev[-1])
                                ph.wait("tensor", sV, prev_group_ev)
                            tmm = None
                            for tl in range(GT):
                                for ml in range(16):
                                    mb = hf * 16 + ml
                                    tmm = ph.run(
                                        "tensor",
                                        lambda t, tl=tl, ml=ml, mb=mb, half=half:
                                        t.matmul(
                                            om[tl][:],
                                            wft[:, tl * M + mb * 128:
                                                tl * M + mb * 128 + 128],
                                            vms[:, half + ml * 512:
                                                half + ml * 512 + 512],
                                            start=(mb == 0), stop=(mb == 31)),
                                        sPE if ml == 15 else None)
                            half_pe[(oc, hf)] = tmm
                        evs = []
                        for tl in range(GT):
                            u = len(st_t)
                            sb = stg[u % 4]
                            ph.wait("vector", sPE, half_pe[(oc, 1)] - (GT - 1 - tl))
                            if u >= 4:
                                ph.wait("vector", sSg[u % 4], st_t[u - 4])
                            ev = ph.run("vector", lambda v, tl=tl, sb=sb:
                                        v.tensor_copy(sb[:], om[tl][:]), sV)
                            evs.append(ev)
                            ph.wait("sync", sV, ev)
                            st = ph.dma("sync",
                                        omemp[g * GT + tl][:,
                                                           oc * 512:(oc + 1) * 512],
                                        sb[:], sSg[u % 4])
                            st_t.append(st)
                        prev_group_ev = evs[-1]
                for ss in sSg:
                    ph.wait("sync", ss, ss.n)
            ph.emit()

        if "QT" in phase_set:
            projT_phase("QT", wqT, QT, False)

        if "CTX" in phase_set:
            ph = Phase(nc, "CTX")
            with ExitStack() as ps:
                EE = ps.enter_context
                qsb = EE(nc.sbuf_tensor("CT_q", [128, 2 * D], BF16))
                ktr = EE(nc.sbuf_tensor("CT_kt", [128, 4 * D], BF16))
                vr = EE(nc.sbuf_tensor("CT_v", [128, 4 * D], BF16))
                psb = EE(nc.sbuf_tensor("CT_p", [128, 4 * 384], BF16))
                ptb = EE(nc.sbuf_tensor("CT_pt", [128, 4 * 384], BF16))
                zz = EE(nc.sbuf_tensor("CT_z", [128, 8], F32))
                cst = EE(nc.sbuf_tensor("CT_c", [128, 2 * D], BF16))
                sc = [EE(nc.psum_tensor(f"CT_sc{i}", [128, 384], F32))
                      for i in range(4)]
                tp = [EE(nc.psum_tensor(f"CT_tp{i}", [128, 384], BF16))
                      for i in range(2)]
                otp = [EE(nc.psum_tensor(f"CT_ot{i}", [128, 128], F32))
                       for i in range(2)]
                st_t = []
                exp_t = []
                score_last = []
                av_last = []
                otev_t = []
                trp_last = []
                tp_last = {}
                sTl = [SP[0], SP[1], SP[2], SP[3]]
                sCs = [SP[4], SP[5]]
                tl_hist = {}
                un = 0
                for tt in range(NT):
                    if tt >= 4:
                        ph.wait("gpsimd", sPE, av_last[tt - 2])
                    if tt >= 2:
                        ph.wait("gpsimd", sPE, score_last[tt - 2])
                    ph.dma("gpsimd",
                           ktr[:, (tt % 4) * D:(tt % 4) * D + D].rearrange(
                               "p (j c) -> p j c", c=128),
                           KT[tt].rearrange("j p c -> p j c"), sTl[tt % 4])
                    ph.dma("gpsimd", vr[:, (tt % 4) * D:(tt % 4) * D + D],
                           Vh[tt], sTl[tt % 4])
                    tld = ph.dma("gpsimd",
                                 qsb[:, (tt % 2) * D:(tt % 2) * D + D].rearrange(
                                     "p (j c) -> p j c", c=128),
                                 QT[tt].rearrange("j p c -> p j c"),
                                 sTl[tt % 4])
                    tl_hist[tt] = tld
                    nb = min(tt + 1, 3)
                    kts = list(range(tt - nb + 1, tt + 1))
                    q = qsb[:, (tt % 2) * D:(tt % 2) * D + D]
                    if tt >= 2:
                        ph.wait("scalar", sCs[tt % 2], st_t[tt - 2])
                    for h in range(16):
                        pb = sc[un % 4]
                        if un >= 4:
                            ph.wait("tensor", sA, exp_t[un - 4])
                        if h == 0:
                            ph.wait("tensor", sTl[tt % 4], tld)
                            if tt >= 1:
                                ph.wait("tensor", sTl[(tt - 1) % 4],
                                        tl_hist[tt - 1])
                            if tt >= 2:
                                ph.wait("tensor", sTl[(tt - 2) % 4],
                                        tl_hist[tt - 2])
                        tsc = None
                        for bi, kt in enumerate(kts):
                            tsc = ph.run(
                                "tensor", lambda t, h=h, bi=bi, kt=kt, pb=pb, q=q:
                                t.matmul(pb[:, bi * 128:(bi + 1) * 128],
                                         q[:, h * 128:(h + 1) * 128],
                                         ktr[:, (kt % 4) * D + h * 128:
                                             (kt % 4) * D + h * 128 + 128],
                                         start=True, stop=True),
                                sPE if bi == nb - 1 else None)
                        if h == 15:
                            score_last.append(tsc)
                        ph.wait("vector", sPE, tsc)
                        tm = ph.run("vector", lambda v, pb=pb, nb=nb:
                                    v.tensor_tensor(
                                        pb[:, 0:nb * 128], pb[:, 0:nb * 128],
                                        maskw[:, (3 - nb) * 128:384], op=ALU.add),
                                    sV)
                        ph.wait("scalar", sV, tm)
                        zc = zz[:, un % 8:un % 8 + 1]
                        pcur = psb[:, (un % 4) * 384:(un % 4) * 384 + nb * 128]
                        if un >= 4:
                            ph.wait("scalar", sPE, trp_last[un - 4])
                        te = ph.run("scalar", lambda s, pb=pb, nb=nb, zc=zc,
                                    pcur=pcur:
                                    s.activation(pcur, pb[:, 0:nb * 128], AF.Exp,
                                                 bias=cmisc[:, C_NB32:C_NB32 + 1],
                                                 scale=cmisc[:, C_S128:C_S128 + 1],
                                                 accum_out=zc), sA)
                        exp_t.append(te)
                        ph.wait("vector", sA, te)
                        trc = ph.run("vector", lambda v, zc=zc:
                                     v.reciprocal(zc, zc), sV)
                        ph.wait("vector", sV, trc)
                        tps = ph.run("vector", lambda v, zc=zc, pcur=pcur:
                                     v.tensor_scalar_mul(pcur, pcur, zc), sV)
                        ph.wait("tensor", sV, tps)
                        pt = tp[un % 2]
                        g = tp_last.get(un % 2)
                        if g:
                            ph.wait("tensor", g[1], g[0])
                        tpt = None
                        for bi in range(nb):
                            tpt = ph.run(
                                "tensor", lambda t, bi=bi, pt=pt, un=un, pcur=pcur:
                                t.transpose(
                                    pt[:, bi * 128:(bi + 1) * 128],
                                    pcur[:, bi * 128:(bi + 1) * 128],
                                    ident[:]), sPE if bi == nb - 1 else None)
                        eng, sem = (("vector", sV) if (un % 2 == 0)
                                    else ("scalar", sA))
                        ph.wait(eng, sPE, tpt)
                        dst = ptb[:, (un % 4) * 384:(un % 4) * 384 + nb * 128]
                        if eng == "vector":
                            ev = ph.run("vector", lambda v, pt=pt, dst=dst, nb=nb:
                                        v.tensor_copy(dst, pt[:, 0:nb * 128]), sem)
                        else:
                            ev = ph.run("scalar", lambda s, pt=pt, dst=dst, nb=nb:
                                        s.copy(dst, pt[:, 0:nb * 128]), sem)
                        tp_last[un % 2] = (ev, sem)
                        trp_last.append(tpt)
                        po = otp[un % 2]
                        if un >= 2:
                            ph.wait("tensor", sA, otev_t[un - 2])
                        ph.wait("tensor", sem, ev)
                        tav = None
                        for bi, kt in enumerate(kts):
                            tav = ph.run(
                                "tensor", lambda t, bi=bi, kt=kt, h=h, po=po,
                                un=un:
                                t.matmul(po[:],
                                         vr[:, (kt % 4) * D + h * 128:
                                            (kt % 4) * D + h * 128 + 128],
                                         ptb[:, (un % 4) * 384 + bi * 128:
                                             (un % 4) * 384 + bi * 128 + 128],
                                         start=(bi == 0), stop=(bi == nb - 1)),
                                sPE if bi == nb - 1 else None)
                        if h == 15:
                            av_last.append(tav)
                        ph.wait("scalar", sPE, tav)
                        dst = cst[:, (tt % 2) * D + h * 128:
                                  (tt % 2) * D + h * 128 + 128]
                        tev = ph.run("scalar", lambda s, po=po, dst=dst:
                                     s.copy(dst, po[:]), sA)
                        otev_t.append(tev)
                        un += 1
                    ph.wait("sync", sA, otev_t[-1])
                    st = ph.dma("sync", ctxT[tt].rearrange("j p c -> p j c"),
                                cst[:, (tt % 2) * D:(tt % 2) * D + D].rearrange(
                                    "p (j c) -> p j c", c=128), sCs[tt % 2])
                    st_t.append(st)
                ph.wait("sync", sCs[0], sCs[0].n)
                ph.wait("sync", sCs[1], sCs[1].n)
            ph.emit()

        if "CMB" in phase_set:
            ph = Phase(nc, "CMB")
            with ExitStack() as ps:
                EE = ps.enter_context
                ctxa = xnT
                omt = EE(nc.sbuf_tensor("CM_om", [128, ND * T], BF16))
                otile = EE(nc.sbuf_tensor("CM_ot", [128, 2 * D], BF16))
                wos = EE(nc.sbuf_tensor("CM_wo", [128, 2 * ND * 512], BF16))
                gall = EE(nc.sbuf_tensor("CM_g", [128, 16], F32))
                ols = [EE(nc.sbuf_tensor(f"CM_ol{i}", [128, 512], F32))
                       for i in range(2)]
                tmp = EE(nc.sbuf_tensor("CM_tmp", [128, 512], F32))
                xsb = [EE(nc.sbuf_tensor(f"CM_x{i}", [128, 512], F32))
                       for i in range(2)]
                xnw = [EE(nc.sbuf_tensor(f"CM_xn{i}", [128, 512], F32))
                       for i in range(2)]
                tp = [EE(nc.psum_tensor(f"CM_tp{i}", [128, 128], BF16))
                      for i in range(2)]
                gps = [EE(nc.psum_tensor(f"CM_gp{i}", [128, 1], F32))
                       for i in range(2)]
                pso = [EE(nc.psum_tensor(f"CM_po{i}", [128, 512], F32))
                       for i in range(2)]
                psm = [EE(nc.psum_tensor(f"CM_pm{i}", [128, 512], F32))
                       for i in range(2)]
                sCtx = SP[0]
                sOm = [SP[1], SP[2]]
                sWo = [SP[3], SP[8]]
                sXl2 = [SP[4], SP[5]]
                sYs = [SP[6], SP[7]]
                xnT_v = xnT.rearrange("p (j c) -> p j c", c=T)
                tctx = 0
                for tt in range(NT):
                    tctx = ph.dma("gpsimd",
                                  xnT_v[:, :, tt * 128:tt * 128 + 128],
                                  ctxT[tt].rearrange("j p c -> p j c"), sCtx)
                trv = []
                om_tr_last = []
                for tt in range(NT):
                    ob = otile[:, (tt % 2) * D:(tt % 2) * D + D]
                    if tt >= 2:
                        ph.wait("gpsimd", sPE, om_tr_last[tt - 2])
                    tl = ph.dma("gpsimd", ob, omemp[tt], sOm[tt % 2])
                    ph.wait("tensor", sOm[tt % 2], tl)
                    tpt = None
                    for j in range(ND):
                        pb = tp[j % 2]
                        if len(trv) >= 2:
                            ph.wait("tensor", sV, trv[-2])
                        tpt = ph.run("tensor", lambda t, j=j, pb=pb, ob=ob:
                                     t.transpose(pb[:], ob[:, j * 128:(j + 1) * 128],
                                                 ident[:]), sPE)
                        ph.wait("vector", sPE, tpt)
                        ev = ph.run("vector", lambda v, pb=pb, j=j, tt=tt:
                                    v.tensor_copy(
                                        omt[:, j * T + tt * 128:
                                            j * T + tt * 128 + 128], pb[:]), sV)
                        trv.append(ev)
                    om_tr_last.append(tpt)
                ph.wait("tensor", sCtx, tctx)
                gev = []
                for tt in range(NT):
                    gp = gps[tt % 2]
                    if tt >= 2:
                        ph.wait("tensor", sA, gev[tt - 2])
                    tg = None
                    for j in range(ND):
                        tg = ph.run("tensor", lambda t, j=j, tt=tt, gp=gp:
                                    t.matmul(gp[:],
                                             ctxa[:, j * T + tt * 128:
                                                  j * T + tt * 128 + 128],
                                             gcomb[:, j:j + 1],
                                             start=(j == 0), stop=(j == ND - 1)),
                                    sPE if j == ND - 1 else None)
                    ph.wait("scalar", sPE, tg)
                    ge = ph.run("scalar", lambda s, gp=gp, tt=tt: s.activation(
                        gall[:, tt:tt + 1], gp[:], AF.Sigmoid, bias=gbb[:]), sA)
                    gev.append(ge)
                ph.wait("tensor", sA, gev[-1])
                st_t, ev_t = [], []
                wo_pe = {}
                for oc in range(4):
                    half = (oc % 2) * ND * 512
                    if oc >= 2:
                        ph.wait("gpsimd", sPE, wo_pe[oc - 2])
                    twl = ph.dma("gpsimd",
                                 wos[:, half:half + ND * 512].rearrange(
                                     "p (j c) -> p j c", c=512),
                                 woT[:, oc * 512:(oc + 1) * 512].rearrange(
                                     "(j p) c -> p j c", p=128), sWo[oc % 2])
                    ph.wait("tensor", sWo[oc % 2], twl)
                    tpm = None
                    for tt in range(NT):
                        u = len(st_t)
                        po, pm = pso[u % 2], psm[u % 2]
                        if u >= 2:
                            ph.wait("tensor", sV, ev_t[u - 2])
                        for j in range(ND):
                            ph.run("tensor", lambda t, j=j, tt=tt, po=po, half=half:
                                   t.matmul(po[:],
                                            ctxa[:, j * T + tt * 128:
                                                 j * T + tt * 128 + 128],
                                            wos[:, half + j * 512:
                                                half + j * 512 + 512],
                                            start=(j == 0), stop=(j == ND - 1)))
                        for j in range(ND):
                            tpm = ph.run(
                                "tensor", lambda t, j=j, tt=tt, pm=pm, half=half:
                                t.matmul(pm[:],
                                         omt[:, j * T + tt * 128:
                                             j * T + tt * 128 + 128],
                                         wos[:, half + j * 512:
                                             half + j * 512 + 512],
                                         start=(j == 0), stop=(j == ND - 1)),
                                sPE if j == ND - 1 else None)
                        tx = ph.dma("sync", xsb[u % 2][:],
                                    x_in[tt * 128:(tt + 1) * 128,
                                         oc * 512:(oc + 1) * 512], sXl2[u % 2])
                        ph.wait("scalar", sPE, tpm)
                        if u >= 2:
                            ph.wait("scalar", sV, ev_t[u - 2])
                        tol = ph.run("scalar", lambda s, po=po, u=u:
                                     s.copy(ols[u % 2][:], po[:]), sA)
                        ph.wait("vector", sA, tol)
                        ph.run("vector", lambda v, pm=pm, u=u, tt=tt:
                               v.scalar_tensor_tensor(
                                   tmp[:], pm[:], gall[:, tt:tt + 1],
                                   ols[u % 2][:], op0=ALU.mult, op1=ALU.add))
                        ph.wait("vector", sXl2[u % 2], tx)
                        if u >= 2:
                            ph.wait("vector", sYs[u % 2], st_t[u - 2])
                        ev = ph.run("vector", lambda v, u=u:
                                    v.tensor_add(xnw[u % 2][:], tmp[:],
                                                 xsb[u % 2][:]), sV)
                        ev_t.append(ev)
                        ph.wait("sync", sV, ev)
                        st = ph.dma("sync", xres[tt][:, oc * 512:(oc + 1) * 512],
                                    xnw[u % 2][:], sYs[u % 2])
                        st_t.append(st)
                    wo_pe[oc] = tpm
                ph.wait("sync", sYs[0], sYs[0].n)
                ph.wait("sync", sYs[1], sYs[1].n)
            ph.emit()

        if "LN2" in phase_set:
            ln_phase("LN2", lambda tt: xres[tt])

        def f1_phase(name):
            ph = Phase(nc, name)
            with ExitStack() as ps:
                EE = ps.enter_context
                wg_sb = EE(nc.sbuf_tensor(name + "_wg", [128, 2 * ND * 512], BF16))
                wu_sb = EE(nc.sbuf_tensor(name + "_wu", [128, 2 * ND * 512], BF16))
                hg = [EE(nc.sbuf_tensor(name + f"_hg{i}", [128, 512], BF16))
                      for i in range(2)]
                fst = [EE(nc.sbuf_tensor(name + f"_f{i}", [128, 512], BF16))
                       for i in range(4)]
                psg = [EE(nc.psum_tensor(name + f"_pg{i}", [128, 512], F32))
                       for i in range(2)]
                psu = [EE(nc.psum_tensor(name + f"_pu{i}", [128, 512], F32))
                       for i in range(2)]
                st_t, ev_t, grp_pe = [], [], {}
                hg_done = []
                sWh = [SP[0], SP[1]]
                sSg = [SP[4], SP[5], SP[6], SP[7]]
                for ibg in range(16):
                    half = (ibg % 2) * ND * 512
                    if ibg >= 2:
                        ph.wait("gpsimd", sPE, grp_pe[ibg - 2])
                    ph.dma("gpsimd",
                           wg_sb[:, half:half + ND * 512].rearrange(
                               "p (j c) -> p j c", c=512),
                           wgR[ibg].rearrange("j p c -> p j c"), sWh[ibg % 2])
                    twl = ph.dma("gpsimd",
                                 wu_sb[:, half:half + ND * 512].rearrange(
                                     "p (j c) -> p j c", c=512),
                                 wuR[ibg].rearrange("j p c -> p j c"),
                                 sWh[ibg % 2])
                    ph.wait("tensor", sWh[ibg % 2], twl)
                    tpu = None
                    for ibl in range(4):
                        ib = ibg * 4 + ibl
                        for tc in range(4):
                            u = len(st_t)
                            pg, pu = psg[u % 2], psu[u % 2]
                            if u >= 2:
                                ph.wait("tensor", sA, hg_done[u - 2])
                                ph.wait("tensor", sV, ev_t[u - 2])
                            for j in range(ND):
                                ph.run("tensor",
                                       lambda t, j=j, ibl=ibl, tc=tc, pg=pg,
                                       half=half:
                                       t.matmul(
                                           pg[:],
                                           wg_sb[:, half + j * 512 + ibl * 128:
                                                 half + j * 512 + ibl * 128 + 128],
                                           xnT[:, j * T + tc * 512:
                                               j * T + tc * 512 + 512],
                                           start=(j == 0), stop=(j == ND - 1)))
                            for j in range(ND):
                                tpu = ph.run(
                                    "tensor",
                                    lambda t, j=j, ibl=ibl, tc=tc, pu=pu, half=half:
                                    t.matmul(
                                        pu[:],
                                        wu_sb[:, half + j * 512 + ibl * 128:
                                              half + j * 512 + ibl * 128 + 128],
                                        xnT[:, j * T + tc * 512:
                                            j * T + tc * 512 + 512],
                                        start=(j == 0), stop=(j == ND - 1)),
                                    sPE if j == ND - 1 else None)
                            ph.wait("scalar", sPE, tpu)
                            if u >= 2:
                                ph.wait("scalar", sV, ev_t[u - 2])
                            th = ph.run("scalar", lambda s, pg=pg, u=u, ib=ib:
                                        s.activation(hg[u % 2][:], pg[:], AF.Silu,
                                                     scale=sgc[:, ib:ib + 1]), sA)
                            hg_done.append(th)
                            ph.wait("vector", sA, th)
                            if u >= 4:
                                ph.wait("vector", sSg[u % 4], st_t[u - 4])
                            sb = fst[u % 4]
                            ev = ph.run("vector",
                                        lambda v, pu=pu, u=u, ib=ib, sb=sb:
                                        v.scalar_tensor_tensor(
                                            sb[:], pu[:], suc[:, ib:ib + 1],
                                            hg[u % 2][:], op0=ALU.mult,
                                            op1=ALU.mult), sV)
                            ev_t.append(ev)
                            ph.wait("sync", sV, ev)
                            st = ph.dma("sync",
                                        fuse[ib][:, tc * 512:(tc + 1) * 512],
                                        sb[:], sSg[u % 4])
                            st_t.append(st)
                    grp_pe[ibg] = tpu
                for ss in sSg:
                    ph.wait("sync", ss, ss.n)
            ph.emit()

        def f2_phase(name, dst_of):
            ph = Phase(nc, name)
            with ExitStack() as ps:
                EE = ps.enter_context
                wd_sb = EE(nc.sbuf_tensor(name + "_wd", [128, 64 * 512], BF16))
                sdbc = EE(nc.sbuf_tensor(name + "_sd", [128, D], F32))
                tsd = ph.dma("gpsimd", sdbc[:], sdbc_d, sG)
                ph.wait("vector", sG, tsd)
                fsb = [EE(nc.sbuf_tensor(name + f"_fs{i}", [128, 4 * 512], BF16))
                       for i in range(2)]
                xsb = [EE(nc.sbuf_tensor(name + f"_x{i}", [128, 512], F32))
                       for i in range(2)]
                tmp = EE(nc.sbuf_tensor(name + "_tmp", [128, 512], F32))
                ynw = [EE(nc.sbuf_tensor(name + f"_y{i}", [128, 512], F32))
                       for i in range(2)]
                psy = [EE(nc.psum_tensor(name + f"_py{i}", [128, 512], F32))
                       for i in range(8)]
                sWd = [SP[0], SP[1]]
                sFs = [SP[2], SP[3]]
                sXl2 = [SP[4], SP[5]]
                sYs = [SP[6], SP[7]]
                st_t = []
                ev_by_unit = {}
                fs_pe = {}
                wd_pe = {}
                nchunk = 0
                for oc in range(4):
                    if oc >= 1:
                        ph.wait("gpsimd", sPE, wd_pe.get(oc - 1))
                    twd = 0
                    for j8 in range(8):
                        twd = ph.dma(
                            "gpsimd",
                            wd_sb[:, j8 * 8 * 512:(j8 + 1) * 8 * 512].rearrange(
                                "p (j c) -> p j c", c=512),
                            wd_i8[j8 * 1024:(j8 + 1) * 1024,
                                  oc * 512:(oc + 1) * 512].rearrange(
                                      "(j p) c -> p j c", p=128), sWd[oc % 2])
                    ph.wait("tensor", sWd[oc % 2], twd)
                    for g4 in range(4):
                        pset = [psy[(g4 % 2) * 4 + i] for i in range(4)]
                        uid = oc * 4 + g4
                        if uid >= 2:
                            ph.wait("tensor", sV, ev_by_unit[uid - 2])
                        tmm = None
                        for ib4 in range(16):
                            fb = fsb[ib4 % 2]
                            ph.wait("gpsimd", sPE, fs_pe.get(nchunk - 2))
                            tf = ph.dma(
                                "gpsimd",
                                fb[:].rearrange("p (b c) -> p b c", c=512),
                                fuse[ib4 * 4:(ib4 + 1) * 4, :,
                                     g4 * 512:(g4 + 1) * 512].rearrange(
                                         "b p c -> p b c"), sFs[ib4 % 2])
                            ph.wait("tensor", sFs[ib4 % 2], tf)
                            for bl in range(4):
                                ib = ib4 * 4 + bl
                                for tl in range(4):
                                    tmm = ph.run(
                                        "tensor",
                                        lambda t, fb=fb, bl=bl, ib=ib, tl=tl,
                                        pset=pset:
                                        t.matmul(
                                            pset[tl][:],
                                            fb[:, bl * 512 + tl * 128:
                                               bl * 512 + tl * 128 + 128],
                                            wd_sb[:, ib * 512:(ib + 1) * 512],
                                            start=(ib == 0), stop=(ib == 63)),
                                        sPE if (bl == 3 and tl == 3) else None)
                            fs_pe[nchunk] = tmm
                            nchunk += 1
                        wd_pe[oc] = tmm
                        lastev = None
                        for tl in range(4):
                            tt = g4 * 4 + tl
                            u = len(st_t)
                            tx = ph.dma("sync", xsb[u % 2][:],
                                        xres[tt][:, oc * 512:(oc + 1) * 512],
                                        sXl2[u % 2])
                            ph.wait("vector", sPE, tmm)
                            ph.run("vector", lambda v, pset=pset, tl=tl, oc=oc:
                                   v.tensor_mul(tmp[:], pset[tl][:],
                                                sdbc[:, oc * 512:(oc + 1) * 512]))
                            ph.wait("vector", sXl2[u % 2], tx)
                            if u >= 2:
                                ph.wait("vector", sYs[u % 2], st_t[u - 2])
                            ev = ph.run("vector", lambda v, u=u:
                                        v.tensor_add(ynw[u % 2][:], tmp[:],
                                                     xsb[u % 2][:]), sV)
                            lastev = ev
                            ph.wait("sync", sV, ev)
                            st = ph.dma("sync", dst_of(tt, oc), ynw[u % 2][:],
                                        sYs[u % 2])
                            st_t.append(st)
                        ev_by_unit[uid] = lastev
                ph.wait("sync", sYs[0], sYs[0].n)
                ph.wait("sync", sYs[1], sYs[1].n)
            ph.emit()

        if "F1a" in phase_set:
            f1_phase("F1a")
        if "F2a" in phase_set:
            f2_phase("F2a", lambda tt, oc: xres[tt][:, oc * 512:(oc + 1) * 512])
        if "LN3" in phase_set:
            ln_phase("LN3", lambda tt: xres[tt])

        def btier(tier):
            nh = NHS[tier]
            dim = nh * 64
            ndt = IN_OFF[tier + 1] // 128
            o0 = OUT_OFF[tier]
            rbase = o0 // 128
            has_fb = tier < 2
            w = nh * 8
            with ExitStack() as ts:
                TE = ts.enter_context
                vts = TE(nc.sbuf_tensor(f"B{tier}_v", [128, NT * dim], BF16))
                h0 = o0 // 64
                if has_fb:
                    cT = TE(nc.sbuf_tensor(f"B{tier}_cT", [64, 2 * T], BF16))
                    ph = Phase(nc, f"B{tier}c")
                    with ExitStack() as ps:
                        EE = ps.enter_context
                        pck = [EE(nc.psum_tensor(f"B{tier}c_k{i}", [128, w], F32))
                               for i in range(2)]
                        pcv = [EE(nc.psum_tensor(f"B{tier}c_v{i}", [128, w], F32))
                               for i in range(2)]
                        tpc = [EE(nc.psum_tensor(f"B{tier}c_t{i}", [128, 128], BF16))
                               for i in range(2)]
                        csb = [EE(nc.sbuf_tensor(f"B{tier}c_s{i}", [128, 2 * w],
                                                 BF16)) for i in range(2)]
                        nhi = 14 if tier == 0 else 8
                        j0 = 2 if tier == 0 else 8
                        fkc, fvc = fk_sb[tier], fv_sb[tier]
                        ev_t, tr_ev, c_tr_last = [], [], []
                        for tt in range(NT):
                            pk_, pv_ = pck[tt % 2], pcv[tt % 2]
                            sb = csb[tt % 2]
                            if tt >= 2:
                                ph.wait("tensor", sV, ev_t[tt - 2])
                                ph.wait("vector", sPE, c_tr_last[tt - 2])
                            tpe = None
                            for jj in range(nhi):
                                j = j0 + jj
                                ph.run("tensor", lambda t, j=j, jj=jj, tt=tt,
                                       pk_=pk_:
                                       t.matmul(pk_[:],
                                                xnT[:, j * T + tt * 128:
                                                    j * T + tt * 128 + 128],
                                                fkc[:, jj * w:(jj + 1) * w],
                                                start=(jj == 0),
                                                stop=(jj == nhi - 1)))
                            for jj in range(nhi):
                                j = j0 + jj
                                tpe = ph.run("tensor", lambda t, j=j, jj=jj, tt=tt,
                                             pv_=pv_:
                                             t.matmul(pv_[:],
                                                      xnT[:, j * T + tt * 128:
                                                          j * T + tt * 128 + 128],
                                                      fvc[:, jj * w:(jj + 1) * w],
                                                      start=(jj == 0),
                                                      stop=(jj == nhi - 1)),
                                             sPE if jj == nhi - 1 else None)
                            ph.wait("vector", sPE, tpe)
                            ph.run("vector", lambda v, pk_=pk_, sb=sb:
                                   v.tensor_copy(sb[:, 0:w], pk_[:]))
                            ev = ph.run("vector", lambda v, pv_=pv_, sb=sb:
                                        v.tensor_copy(sb[:, w:2 * w], pv_[:]), sV)
                            ev_t.append(ev)
                            ph.wait("tensor", sV, ev)
                            tpt = None
                            for kv in range(2):
                                pt = tpc[kv]
                                if len(tr_ev) >= 2:
                                    ph.wait("tensor", sV, tr_ev[-2])
                                src = sb[:, kv * w:(kv + 1) * w]
                                tpt = ph.run("tensor",
                                             lambda t, src=src, pt=pt:
                                             t.transpose(pt[0:w, :], src,
                                                         ident[:]), sPE)
                                ph.wait("vector", sPE, tpt)
                                dstc = cT[0:w, kv * T + tt * 128:
                                          kv * T + tt * 128 + 128]
                                ev2 = ph.run("vector",
                                             lambda v, pt=pt, dstc=dstc:
                                             v.tensor_copy(dstc, pt[0:w, :]), sV)
                                tr_ev.append(ev2)
                            c_tr_last.append(tpt)
                    ph.emit()
                if "Bsub=c" in debug:
                    return

                ph = Phase(nc, f"B{tier}qkv")
                with ExitStack() as ps:
                    EE = ps.enter_context
                    wcol = EE(nc.sbuf_tensor(f"B{tier}_wc", [128, ndt * 64], BF16))
                    wvc = EE(nc.sbuf_tensor(
                        f"B{tier}_wvc",
                        [128, ndt * (512 if tier != 2 else 256)], BF16))
                    pps = [EE(nc.psum_tensor(f"B{tier}q_p{i}", [128, 512], F32))
                           for i in range(2)]
                    ppv = [EE(nc.psum_tensor(f"B{tier}v_p{i}", [128, 512], F32))
                           for i in range(2)]
                    qstg = [EE(nc.sbuf_tensor(f"B{tier}_qs{i}", [64, 512], BF16))
                            for i in range(4)]
                    sWc = SP[0]
                    sWv = SP[1]
                    sQs = [SP[4], SP[5], SP[6], SP[7]]
                    ev_t = []
                    st_t = []
                    wld = []

                    def head_units(w_dram, hh, dst_dram, with_fb):
                        col0 = o0 + hh * 64
                        if wld:
                            ph.wait("gpsimd", sPE, wld[-1])
                        twl = ph.dma("gpsimd",
                                     wcol[:, 0:ndt * 64].rearrange(
                                         "p (j c) -> p j c", c=64),
                                     w_dram[0:ndt * 128,
                                            col0:col0 + 64].rearrange(
                                         "(j p) c -> p j c", p=128), sWc)
                        ph.wait("tensor", sWc, twl)
                        tpe = None
                        for tc in range(4):
                            u = len(ev_t)
                            pb = pps[u % 2]
                            if u >= 2:
                                ph.wait("tensor", sA, ev_t[u - 2])
                            for j in range(ndt):
                                last = (j == ndt - 1 and not with_fb)
                                tpe = ph.run(
                                    "tensor", lambda t, j=j, tc=tc, pb=pb,
                                    last=last:
                                    t.matmul(pb[0:64, :],
                                             wcol[:, j * 64:(j + 1) * 64],
                                             xnT[:, j * T + tc * 512:
                                                 j * T + tc * 512 + 512],
                                             start=(j == 0), stop=last),
                                    sPE if last else None)
                            if with_fb:
                                tpe = ph.run(
                                    "tensor", lambda t, hh=hh, tc=tc, pb=pb:
                                    t.matmul(pb[0:64, :],
                                             pkbd[tier][0:w,
                                                        hh * 64:hh * 64 + 64],
                                             cT[0:w, tc * 512:tc * 512 + 512],
                                             start=False, stop=True), sPE)
                            ph.wait("vector", sPE, tpe)
                            u2 = len(st_t)
                            sb2 = qstg[u2 % 4]
                            if u2 >= 4:
                                ph.wait("vector", sQs[u2 % 4], st_t[u2 - 4])
                            if "Bsub=noev" not in debug:
                                ph.wait("scalar", sPE, tpe)
                                ev = ph.run("scalar", lambda s, pb=pb, sb2=sb2:
                                            s.copy(sb2[:], pb[0:64, :]), sA)
                                ev_t.append(ev)
                                ph.wait("sync", sA, ev)
                            if "Bsub=noev" in debug or "Bsub=nost" in debug:
                                st_t.append(sSt.n)
                            else:
                                st = ph.dma(
                                    "sync",
                                    dst_dram[h0 + hh][:,
                                                      tc * 512:tc * 512 + 512],
                                    sb2[:], sQs[u2 % 4])
                                st_t.append(st)
                        wld.append(tpe)

                    nqh = nh if "Bsub=q1" not in debug else 1
                    for hh in range(nqh):
                        head_units(WQd, hh, QTb, False)
                        if "Bsub=qonly" not in debug and "Bsub=q1" not in debug:
                            head_units(WKd, hh, KTb, has_fb)
                    chunks = {0: [(0, 256)], 1: [(0, 512)],
                              2: [(0, 256), (256, 256), (512, 256)]}[tier]
                    if any(x in debug for x in ("Bsub=qonly", "Bsub=q1", "Bsub=noV")):
                        chunks = []
                    for (c0, wch) in chunks:
                        if wld:
                            ph.wait("gpsimd", sPE, wld[-1])
                        twl = ph.dma("gpsimd",
                                     wvc[:, 0:ndt * wch].rearrange(
                                         "p (j c) -> p j c", c=wch),
                                     WVd[0:ndt * 128,
                                         o0 + c0:o0 + c0 + wch].rearrange(
                                         "(j p) c -> p j c", p=128), sWv)
                        ph.wait("tensor", sWv, twl)
                        tpe = None
                        for tt in range(NT):
                            u = len(ev_t)
                            pb = ppv[u % 2]
                            if u >= 2:
                                ph.wait("tensor", sV, ev_t[u - 2])
                            for j in range(ndt):
                                last = (j == ndt - 1 and not has_fb)
                                tpe = ph.run(
                                    "tensor", lambda t, j=j, tt=tt, pb=pb,
                                    wch=wch, last=last:
                                    t.matmul(pb[:, 0:wch],
                                             xnT[:, j * T + tt * 128:
                                                 j * T + tt * 128 + 128],
                                             wvc[:, j * wch:(j + 1) * wch],
                                             start=(j == 0), stop=last),
                                    sPE if last else None)
                            if has_fb:
                                tpe = ph.run(
                                    "tensor", lambda t, tt=tt, pb=pb, c0=c0,
                                    wch=wch:
                                    t.matmul(pb[:, 0:wch],
                                             cT[0:w, T + tt * 128:
                                                T + tt * 128 + 128],
                                             pvbd[tier][0:w, c0:c0 + wch],
                                             start=False, stop=True), sPE)
                            ph.wait("vector", sPE, tpe)
                            ev = ph.run("vector", lambda v, pb=pb, tt=tt, c0=c0,
                                        wch=wch:
                                        v.tensor_copy(
                                            vts[:, tt * dim + c0:
                                                tt * dim + c0 + wch],
                                            pb[:, 0:wch]), sV)
                            ev_t.append(ev)
                        wld.append(tpe)
                    for ss in sQs:
                        ph.wait("sync", ss, ss.n)
                ph.emit()
                if "Bsub=qkv" in debug:
                    return

                ph = Phase(nc, f"B{tier}att")
                with ExitStack() as ps:
                    EE = ps.enter_context
                    psb = EE(nc.sbuf_tensor(f"B{tier}_P", [128, 4 * 512], BF16))
                    ptb = EE(nc.sbuf_tensor(f"B{tier}_PT", [128, 2 * 512], BF16))
                    zz = EE(nc.sbuf_tensor(f"B{tier}_z", [128, 12], F32))
                    obh = [EE(nc.sbuf_tensor(f"B{tier}_oh{i}", [128, NT * 64],
                                             BF16)) for i in range(2)]
                    obsT = EE(nc.sbuf_tensor(f"B{tier}_obT", [64, T], BF16))
                    qkr = EE(nc.sbuf_tensor(f"B{tier}_qk", [64, 2 * 2 * T], BF16))
                    scp = [EE(nc.psum_tensor(f"B{tier}_sc{i}", [128, 512], F32))
                           for i in range(4)]
                    tp = [EE(nc.psum_tensor(f"B{tier}_tp{i}", [128, 512], BF16))
                          for i in range(2)]
                    otp = [EE(nc.psum_tensor(f"B{tier}_ot{i}", [128, 64], F32))
                           for i in range(2)]
                    un = 0
                    gchunk = 0
                    chunk_exp = []
                    chunk_tpt = {}
                    zred_t = []
                    otev_t = []
                    tp_last = {}
                    st_t = []
                    hd_tr = []
                    h_ring_last = []
                    sQk = [SP[0], SP[1]]
                    sOb = SP[2]
                    last_tsc = None
                    for hh in range(nh):
                        slot = (hh % 2) * 2 * T
                        if hh >= 2:
                            ph.wait("gpsimd", sPE, h_ring_last[hh - 2])
                        ph.dma("gpsimd", qkr[:, slot:slot + T],
                               QTb[h0 + hh], sQk[hh % 2])
                        tldq = ph.dma("gpsimd", qkr[:, slot + T:slot + 2 * T],
                                      KTb[h0 + hh], sQk[hh % 2])
                        ph.wait("tensor", sQk[hh % 2], tldq)
                        ob = obh[hh % 2]
                        if hh >= 2:
                            ph.wait("scalar", sPE, hd_tr[hh - 2])
                        for i in range(NT):
                            nk = i + 1
                            nkc = (nk * 128 + 511) // 512
                            po = otp[un % 2]
                            if un >= 2:
                                ph.wait("tensor", sA, otev_t[un - 2])
                                ph.wait("scalar", sV, zred_t[un - 2])
                            sc_t = []
                            for kc in range(nkc):
                                wv_ = min(512, nk * 128 - kc * 512)
                                pb = scp[gchunk % 4]
                                if gchunk >= 4:
                                    ph.wait("tensor", sA, chunk_exp[gchunk - 4])
                                tsc = ph.run(
                                    "tensor",
                                    lambda t, kc=kc, wv_=wv_, slot=slot, i=i,
                                    pb=pb:
                                    t.matmul(
                                        pb[:, 0:wv_],
                                        qkr[0:64, slot + i * 128:
                                            slot + i * 128 + 128],
                                        qkr[0:64, slot + T + kc * 512:
                                            slot + T + kc * 512 + wv_],
                                        start=True, stop=True), sPE)
                                last_tsc = tsc
                                diag = (i * 128) // 512 == kc
                                tm = None
                                if diag:
                                    ph.wait("vector", sPE, tsc)
                                    off = (i * 128) % 512
                                    tm = ph.run(
                                        "vector", lambda v, pb=pb, off=off:
                                        v.tensor_tensor(
                                            pb[:, off:off + 128],
                                            pb[:, off:off + 128],
                                            maskw[:, 256:384], op=ALU.add), sV)
                                    ph.wait("scalar", sV, tm)
                                else:
                                    ph.wait("scalar", sPE, tsc)
                                if gchunk >= 4:
                                    ph.wait("scalar", sPE,
                                            chunk_tpt[gchunk - 4])
                                pcur = psb[:, (gchunk % 4) * 512:
                                           (gchunk % 4) * 512 + wv_]
                                zc = zz[:, (un % 2) * 4 + kc:
                                        (un % 2) * 4 + kc + 1]
                                te = ph.run(
                                    "scalar", lambda s, pb=pb, wv_=wv_, zc=zc,
                                    pcur=pcur:
                                    s.activation(
                                        pcur, pb[:, 0:wv_], AF.Exp,
                                        bias=cmisc[:, C_NB32:C_NB32 + 1],
                                        scale=cmisc[:, C_S64:C_S64 + 1],
                                        accum_out=zc), sA)
                                chunk_exp.append(te)
                                sc_t.append((gchunk, te, wv_))
                                gchunk += 1
                            for (gc, te, wv_) in sc_t:
                                nbl = (wv_ + 127) // 128
                                ph.wait("tensor", sA, te)
                                pt = tp[gc % 2]
                                gg = tp_last.get(gc % 2)
                                if gg:
                                    ph.wait("tensor", gg[1], gg[0])
                                tpt = None
                                for bi in range(nbl):
                                    tpt = ph.run(
                                        "tensor",
                                        lambda t, pt=pt, gc=gc, bi=bi:
                                        t.transpose(
                                            pt[:, bi * 128:(bi + 1) * 128],
                                            psb[:, (gc % 4) * 512 + bi * 128:
                                                (gc % 4) * 512 + bi * 128 + 128],
                                            ident[:]),
                                        sPE if bi == nbl - 1 else None)
                                chunk_tpt[gc] = tpt
                                eng, sem = (("vector", sV)
                                            if gc % 2 == 0
                                            else ("scalar", sA))
                                ph.wait(eng, sPE, tpt)
                                dst = ptb[:, (gc % 2) * 512:
                                          (gc % 2) * 512 + nbl * 128]
                                if eng == "vector":
                                    ev = ph.run("vector",
                                                lambda v, pt=pt, dst=dst, nbl=nbl:
                                                v.tensor_copy(
                                                    dst, pt[:, 0:nbl * 128]), sem)
                                else:
                                    ev = ph.run("scalar",
                                                lambda s, pt=pt, dst=dst, nbl=nbl:
                                                s.copy(dst, pt[:, 0:nbl * 128]),
                                                sem)
                                tp_last[gc % 2] = (ev, sem)
                                ph.wait("tensor", sem, ev)
                                tav = None
                                for bi in range(nbl):
                                    kt = (gc - sc_t[0][0]) * 4 + bi
                                    last = (gc == sc_t[-1][0] and bi == nbl - 1)
                                    tav = ph.run(
                                        "tensor",
                                        lambda t, kt=kt, hh=hh, po=po, gc=gc,
                                        bi=bi, first=(gc == sc_t[0][0]
                                                      and bi == 0), last=last:
                                        t.matmul(
                                            po[:],
                                            ptb[:, (gc % 2) * 512 + bi * 128:
                                                (gc % 2) * 512 + bi * 128 + 128],
                                            vts[:, kt * dim + hh * 64:
                                                kt * dim + hh * 64 + 64],
                                            start=first, stop=last),
                                        sPE if (last or bi == nbl - 1) else None)
                            Zc = 8 + un % 2
                            RZc = 10 + un % 2
                            ph.wait("vector", sA, chunk_exp[-1])
                            ph.run("vector", lambda v, nkc=nkc, un=un, Zc=Zc:
                                   v.reduce_sum(
                                       zz[:, Zc:Zc + 1],
                                       zz[:, (un % 2) * 4:(un % 2) * 4 + nkc],
                                       axis=AX.X))
                            trc = ph.run("vector", lambda v, Zc=Zc, RZc=RZc:
                                         v.reciprocal(
                                             zz[:, RZc:RZc + 1],
                                             zz[:, Zc:Zc + 1]), sV)
                            zred_t.append(trc)
                            ph.wait("scalar", sV, trc)
                            ph.wait("scalar", sPE, tav)
                            dst = ob[:, i * 64:(i + 1) * 64]
                            tev = ph.run(
                                "scalar", lambda s, po=po, dst=dst, RZc=RZc:
                                s.activation(dst, po[:], AF.Copy,
                                             scale=zz[:, RZc:RZc + 1]), sA)
                            otev_t.append(tev)
                            un += 1
                        h_ring_last.append(last_tsc)
                        ph.wait("tensor", sA, otev_t[-1])
                        if st_t:
                            pass
                        tails = []
                        for tg in range(0, NT, 4):
                            pt = tp[(tg // 4) % 2]
                            gg = tp_last.get((tg // 4) % 2)
                            if gg:
                                ph.wait("tensor", gg[1], gg[0])
                            tpt = None
                            for tt2 in range(tg, tg + 4):
                                tpt = ph.run(
                                    "tensor", lambda t, pt=pt, ob=ob, tt2=tt2,
                                    tg=tg:
                                    t.transpose(
                                        pt[0:64, (tt2 - tg) * 128:
                                           (tt2 - tg) * 128 + 128],
                                        ob[:, tt2 * 64:(tt2 + 1) * 64],
                                        ident[:]), sPE if tt2 == tg + 3 else None)
                            if tg == 0 and st_t:
                                ph.wait("vector", sOb, st_t[-1])
                            ph.wait("vector", sPE, tpt)
                            ev = ph.run(
                                "vector", lambda v, pt=pt, tg=tg:
                                v.tensor_copy(
                                    obsT[0:64, tg * 128:(tg + 4) * 128],
                                    pt[0:64, 0:512]), sV)
                            tp_last[(tg // 4) % 2] = (ev, sV)
                            tails.append(ev)
                        hd_tr.append(tpt)
                        ph.wait("sync", sV, tails[-1])
                        st = ph.dma("sync",
                                    ObT[rbase + hh // 2][(hh % 2) * 64:
                                                         (hh % 2) * 64 + 64, :],
                                    obsT[0:64, :], sOb)
                        st_t.append(st)
                    ph.wait("sync", sOb, sOb.n)
                ph.emit()

        for tier, pn in ((0, "B0"), (1, "B1"), (2, "B2")):
            if pn in phase_set:
                btier(tier)

        if "BWO" in phase_set:
            ph = Phase(nc, "BWO")
            with ExitStack() as ps:
                EE = ps.enter_context
                oball = EE(nc.sbuf_tensor("BW_ob", [128, 12 * T], BF16))
                wos2 = EE(nc.sbuf_tensor("BW_w", [128, 2 * 12 * 512], BF16))
                xsb = [EE(nc.sbuf_tensor(f"BW_x{i}", [128, 512], F32))
                       for i in range(2)]
                ynw = [EE(nc.sbuf_tensor(f"BW_y{i}", [128, 512], F32))
                       for i in range(2)]
                psy = [EE(nc.psum_tensor(f"BW_p{i}", [128, 512], F32))
                       for i in range(2)]
                sObl = SP[0]
                sWo2 = [SP[1], SP[2]]
                sXl2 = [SP[4], SP[5]]
                sYs = [SP[6], SP[7]]
                tob = 0
                for rt in range(12):
                    tob = ph.dma("gpsimd", oball[:, rt * T:(rt + 1) * T],
                                 ObT[rt], sObl)
                ph.wait("tensor", sObl, tob)
                nrc_by_oc = [6, 6, 12, 12]
                ev_t, st_t, wo_pe = [], [], []
                for oc in range(4):
                    nrc = nrc_by_oc[oc]
                    half = (oc % 2) * 12 * 512
                    if oc >= 2:
                        ph.wait("gpsimd", sPE, wo_pe[oc - 2])
                    twl = ph.dma("gpsimd",
                                 wos2[:, half:half + nrc * 512].rearrange(
                                     "p (j c) -> p j c", c=512),
                                 WOd[0:nrc * 128, oc * 512:(oc + 1) * 512]
                                 .rearrange("(j p) c -> p j c", p=128),
                                 sWo2[oc % 2])
                    ph.wait("tensor", sWo2[oc % 2], twl)
                    tpe = None
                    for tt in range(NT):
                        u = len(ev_t)
                        pb = psy[u % 2]
                        if u >= 2:
                            ph.wait("tensor", sV, ev_t[u - 2])
                        for r in range(nrc):
                            tpe = ph.run(
                                "tensor", lambda t, r=r, tt=tt, pb=pb, half=half:
                                t.matmul(pb[:],
                                         oball[:, r * T + tt * 128:
                                               r * T + tt * 128 + 128],
                                         wos2[:, half + r * 512:
                                              half + r * 512 + 512],
                                         start=(r == 0), stop=(r == nrc - 1)),
                                sPE if r == nrc - 1 else None)
                        tx = ph.dma("sync", xsb[u % 2][:],
                                    xres[tt][:, oc * 512:(oc + 1) * 512],
                                    sXl2[u % 2])
                        ph.wait("vector", sPE, tpe)
                        ph.wait("vector", sXl2[u % 2], tx)
                        if u >= 2:
                            ph.wait("vector", sYs[u % 2], st_t[u - 2])
                        ev = ph.run("vector", lambda v, pb=pb, u=u:
                                    v.tensor_add(ynw[u % 2][:], pb[:],
                                                 xsb[u % 2][:]), sV)
                        ev_t.append(ev)
                        ph.wait("sync", sV, ev)
                        st = ph.dma("sync", xres[tt][:, oc * 512:(oc + 1) * 512],
                                    ynw[u % 2][:], sYs[u % 2])
                        st_t.append(st)
                    wo_pe.append(tpe)
                ph.wait("sync", sYs[0], sYs[0].n)
                ph.wait("sync", sYs[1], sYs[1].n)
            ph.emit()

        if "LN4" in phase_set:
            ln_phase("LN4", lambda tt: xres[tt])
        if "F1b" in phase_set:
            f1_phase("F1b")
        if "F2b" in phase_set:
            f2_phase("F2b", lambda tt, oc: out_d[tt * 128:(tt + 1) * 128,
                                                 oc * 512:(oc + 1) * 512])

        dnames = [n for n in debug if ":" not in n and n in scr]
        if dnames:
            ph = Phase(nc, "DBG")
            td = 0
            for name in dnames:
                s = scr[name]
                o = nc.dram_tensor("dbg_" + name, list(s.shape), s.dtype,
                                   kind="ExternalOutput").ap()
                dbg_outs[name] = o
                td = ph.dma("sync", o, s, sSt)
            ph.wait("sync", sSt, td)
            ph.emit()

    import os
    if os.environ.get("KDBG_SEMS"):
        names = ["sL", "sG", "sSt", "sPE", "sV", "sA"]
        for i, sm in enumerate(_dbg_sems):
            print(f"SEMTOTAL {names[i] if i < 6 else i}: {sm.n}")
    return nc, dbg_outs


def _blockdiag(P):
    P = np.asarray(P, np.float32)
    nh = P.shape[0]
    out = np.zeros((nh * 8, nh * 64), np.float32)
    for h in range(nh):
        out[h * 8:(h + 1) * 8, h * 64:(h + 1) * 64] = P[h]
    return out


def _prep_core_inputs(b, I):
    f32 = np.float32
    mask0 = np.where(np.arange(128)[None, :] > np.arange(128)[:, None],
                     0.0, NEG).astype(f32)
    mask2 = np.where(np.arange(128)[None, :] <= np.arange(128)[:, None],
                     0.0, NEG).astype(f32)
    maskw = np.concatenate([mask0, np.zeros((128, 128), f32), mask2], axis=1)
    d = {
        "x": I["L0_attn_a_hidden_states"][b],
        "wqT": I["L0_attn_a_wq"].T,
        "wkT": I["L0_attn_a_wk"].T,
        "wvT": I["L0_attn_a_wv"].T,
        "woT": I["L0_attn_a_wo"].T,
        "wo_nat": I["L0_attn_a_wo"],
        "gwT_rs": I["L0_attn_a_gate_w"].reshape(16, 128).T,
        "gb_bc": np.tile(np.asarray(I["L0_attn_a_gate_b"]).reshape(1, 1),
                         (128, 1)).astype(f32),
        "membT": I["L0_attn_a_mem_bank"].T,
        "WQ": I["L0_attn_b_W_Q"], "WK": I["L0_attn_b_W_K"],
        "WV": I["L0_attn_b_W_V"], "WO": I["L0_attn_b_W_O"],
        "FK0": I["L0_attn_b_FK0"], "FV0": I["L0_attn_b_FV0"],
        "FK1": I["L0_attn_b_FK1"], "FV1": I["L0_attn_b_FV1"],
        "PK0bd": _blockdiag(I["L0_attn_b_PK0"]),
        "PV0bd": _blockdiag(I["L0_attn_b_PV0"]),
        "PK1bd": _blockdiag(I["L0_attn_b_PK1"]),
        "PV1bd": _blockdiag(I["L0_attn_b_PV1"]),
        "wgR": np.asarray(I["L0_ffn_w_gate"]).reshape(
            16, 128, 16, 512).transpose(2, 0, 1, 3),
        "wuR": np.asarray(I["L0_ffn_w_up"]).reshape(
            16, 128, 16, 512).transpose(2, 0, 1, 3),
        "wd": I["L0_ffn_w_down"],
        "sg_rs": np.asarray(I["L0_ffn_s_gate"]).reshape(64, 128).T,
        "su_rs": np.asarray(I["L0_ffn_s_up"]).reshape(64, 128).T,
        "sdbc": np.tile(np.asarray(I["L0_ffn_s_down"]).reshape(1, D), (128, 1)),
        "ident": np.eye(128, dtype=ml_dtypes.bfloat16),
        "maskw": maskw,
    }
    return {k: np.ascontiguousarray(v) for k, v in d.items()}


def kernel(**inputs):
    nc, dbg = build_nc()
    in_maps = [_prep_core_inputs(b, inputs) for b in range(8)]
    res = run_bass_kernel_spmd(nc, in_maps, core_ids=list(range(8)))
    return np.stack([np.asarray(res.results[b]["out"]) for b in range(8)], axis=0)


# version 10
# speedup vs baseline: 1.0287x; 1.0287x over previous
import numpy as np
import ml_dtypes
from contextlib import ExitStack

import concourse.bass as bass
import concourse.mybir as mybir
from concourse.bass_utils import run_bass_kernel_spmd

F32 = mybir.dt.float32
BF16 = mybir.dt.bfloat16
I8 = mybir.dt.int8
AF = mybir.ActivationFunctionType
ALU = mybir.AluOpType
AX = mybir.AxisListType

T = 2048
D = 2048
NT = T // 128
ND = D // 128
M = 4096
INTER = 8192
NEG = -1e30
SB = -32.0

TIER_OWN = [256, 768, 1024]
HEAD_OUT = [256, 512, 768]
NHS = [4, 8, 12]
IN_OFF = [0, 256, 1024, 2048]
OUT_OFF = [0, 256, 768, 1536]

ENGINES = ("sync", "gpsimd", "tensor", "vector", "scalar")
import os
SYNC_DMA = bool(os.environ.get("KDBG_SYNCDMA"))

ALL_PHASES = ["P0", "LN1", "KMT", "VMT", "SIM", "OMEM", "QT", "CTX", "CMB",
              "LN2", "F1a", "F2a", "LN3", "B0", "B1", "B2", "BWO", "LN4",
              "F1b", "F2b"]


class Sem:
    def __init__(self, h):
        self.h = h
        self.n = 0


class Phase:
    def __init__(self, nc, name):
        self.nc = nc
        self.name = name
        self.q = {e: [] for e in ENGINES}

    def on(self, eng, fn):
        self.q[eng].append(fn)

    def emit(self):
        with self.nc.Block(self.name) as blk:
            for name in ENGINES:
                fns = self.q[name]
                if not fns:
                    continue
                def body(eng, fns=fns):
                    for f in fns:
                        f(eng)
                getattr(blk, name)(body)

    def dma(self, eng, dst, src, sem, **kw):
        sem.n += 16
        tgt = sem.n
        self.on(eng, lambda e, dst=dst, src=src, kw=kw:
                e.dma_start(dst, src, **kw).then_inc(sem.h, 16))
        if SYNC_DMA:
            self.on(eng, lambda e, s=sem.h, t=tgt: e.wait_ge(s, t))
        return tgt

    def wait(self, eng, sem, tgt):
        if tgt is None:
            return
        self.on(eng, lambda e, s=sem.h, t=tgt: e.wait_ge(s, t))

    def run(self, eng, fn, sem=None, k=1):
        if sem is None:
            self.on(eng, fn)
            return None
        sem.n += k
        tgt = sem.n
        self.on(eng, lambda e, fn=fn, s=sem.h, k=k: fn(e).then_inc(s, k))
        return tgt


def build_nc(debug=(), upto=None):
    nc = bass.Bass("TRN2", target_bir_lowering=False)
    phase_set = ALL_PHASES[:ALL_PHASES.index(upto) + 1] if upto else ALL_PHASES

    def inp(name, shape, dt):
        return nc.dram_tensor(name, list(shape), dt, kind="ExternalInput").ap()

    x_in = inp("x", (T, D), F32)
    wqT = inp("wqT", (D, D), F32)
    wkT = inp("wkT", (D, D), F32)
    wvT = inp("wvT", (D, D), F32)
    woT = inp("woT", (D, D), F32)
    wo_nat = inp("wo_nat", (D, D), F32)
    gwT_rs = inp("gwT_rs", (128, 16), F32)
    gb_bc = inp("gb_bc", (128, 1), F32)
    membT = inp("membT", (D, M), F32)
    WQd = inp("WQ", (D, 1536), F32)
    WKd = inp("WK", (D, 1536), F32)
    WVd = inp("WV", (D, 1536), F32)
    WOd = inp("WO", (1536, D), F32)
    FKd = {0: inp("FK0", (1792, 32), F32), 1: inp("FK1", (1024, 64), F32)}
    FVd = {0: inp("FV0", (1792, 32), F32), 1: inp("FV1", (1024, 64), F32)}
    PKbd = {0: inp("PK0bd", (32, 256), F32), 1: inp("PK1bd", (64, 512), F32)}
    PVbd = {0: inp("PV0bd", (32, 256), F32), 1: inp("PV1bd", (64, 512), F32)}
    wgR = inp("wgR", (16, 16, 128, 512), I8)
    wuR = inp("wuR", (16, 16, 128, 512), I8)
    wd_i8 = inp("wd", (INTER, D), I8)
    sg_rs = inp("sg_rs", (128, 64), F32)
    su_rs = inp("su_rs", (128, 64), F32)
    sdbc_d = inp("sdbc", (128, D), F32)
    ident_d = inp("ident", (128, 128), BF16)
    maskw_d = inp("maskw", (128, 384), F32)

    out_d = nc.dram_tensor("out", [T, D], F32, kind="ExternalOutput").ap()

    scr = {}
    def scratch(name, shape, dt):
        scr[name] = nc.dram_tensor("scr_" + name, list(shape), dt,
                                   kind="Internal").ap()
        return scr[name]

    xres = scratch("xres", (NT, 128, D), F32)
    KT = scratch("KT", (NT, ND, 128, 128), BF16)
    QT = scratch("QT", (NT, ND, 128, 128), BF16)
    Vh = scratch("Vh", (NT, 128, D), BF16)
    kmemT = scratch("kmemT", (8, ND, 128, 512), BF16)
    vmem2 = scratch("vmem2", (4, 32, 128, 512), BF16)
    simd = scratch("simd", (NT, 128, M), F32)
    omemp = scratch("omemp", (NT, 128, D), BF16)
    ctxT = scratch("ctxT", (NT, ND, 128, 128), BF16)
    fuse = scratch("fuse", (64, 128, T), BF16)
    ObT = scratch("ObT", (12, 128, T), BF16)
    QTb = scratch("QTb", (24, 64, T), BF16)
    KTb = scratch("KTb", (24, 64, T), BF16)
    xnloT = scratch("xnloT", (NT, ND, 128, 128), BF16)
    wklo = scratch("wklo", (ND, ND, 128, 128), BF16)
    kmemTlo = scratch("kmemTlo", (8, ND, 128, 512), BF16)

    dbg_outs = {}
    dbg_sbuf = []

    with ExitStack() as stk:
        E = stk.enter_context

        xnT = E(nc.sbuf_tensor("xnT", [128, ND * T], BF16))
        ident = E(nc.sbuf_tensor("c_ident", [128, 128], BF16))
        maskw = E(nc.sbuf_tensor("c_maskw", [128, 384], F32))
        sgc = E(nc.sbuf_tensor("c_sg", [128, 64], F32))
        suc = E(nc.sbuf_tensor("c_su", [128, 64], F32))
        gbb = E(nc.sbuf_tensor("c_gb", [128, 1], F32))
        gcomb = E(nc.sbuf_tensor("c_gcomb", [128, 16], BF16))
        gwcol = E(nc.sbuf_tensor("c_gwcol", [128, 16], BF16))
        fk_sb = {0: E(nc.sbuf_tensor("c_fk0", [128, 14 * 32], BF16)),
                 1: E(nc.sbuf_tensor("c_fk1", [128, 8 * 64], BF16))}
        fv_sb = {0: E(nc.sbuf_tensor("c_fv0", [128, 14 * 32], BF16)),
                 1: E(nc.sbuf_tensor("c_fv1", [128, 8 * 64], BF16))}
        pkbd = {0: E(nc.sbuf_tensor("c_pk0", [32, 256], BF16)),
                1: E(nc.sbuf_tensor("c_pk1", [64, 512], BF16))}
        pvbd = {0: E(nc.sbuf_tensor("c_pv0", [32, 256], BF16)),
                1: E(nc.sbuf_tensor("c_pv1", [64, 512], BF16))}
        cmisc = E(nc.sbuf_tensor("c_misc", [128, 4], F32))
        C_NB32, C_S128, C_S64 = 0, 1, 2

        sems = [Sem(E(nc.semaphore(f"s{i}"))) for i in range(16)]
        sL, sG, sSt, sPE, sV, sA = sems[0:6]
        SP = sems[6:16]
        global _dbg_sems
        _dbg_sems = sems

        if "P0" in phase_set:
            ph = Phase(nc, "P0")
            with ExitStack() as ps:
                wo_sb = ps.enter_context(nc.sbuf_tensor("p0_wo", [128, 1024], BF16))
                gps = [ps.enter_context(
                    nc.psum_tensor(f"p0_ps{j}", [128, 1], F32)) for j in range(8)]
                ph.dma("sync", ident[:], ident_d, sL)
                ph.dma("sync", maskw[:], maskw_d, sL)
                ph.dma("sync", sgc[:], sg_rs, sL)
                ph.dma("sync", suc[:], su_rs, sL)
                tc0 = ph.dma("sync", gbb[:], gb_bc, sL)
                tgw = ph.dma("gpsimd", gwcol[:], gwT_rs, sG)
                ph.run("vector", lambda v: v.memset(
                    cmisc[:, C_NB32:C_NB32 + 1], -32.0))
                ph.run("vector", lambda v: v.memset(
                    cmisc[:, C_S128:C_S128 + 1], 1.0 / float(np.sqrt(128.0))))
                ph.run("vector", lambda v: v.memset(
                    cmisc[:, C_S64:C_S64 + 1], 0.125))
                for tier, nh, nj, w in ((0, 4, 14, 32), (1, 8, 8, 64)):
                    for j in range(nj):
                        ph.dma("gpsimd", fk_sb[tier][:, j * w:(j + 1) * w],
                               FKd[tier][j * 128:(j + 1) * 128, :], sG)
                        ph.dma("gpsimd", fv_sb[tier][:, j * w:(j + 1) * w],
                               FVd[tier][j * 128:(j + 1) * 128, :], sG)
                    ph.dma("gpsimd", pkbd[tier][:], PKbd[tier], sG)
                    ph.dma("gpsimd", pvbd[tier][:], PVbd[tier], sG)
                tload = sG.n
                last_grp = None
                for jg in range(2):
                    for c in range(16):
                        ph.wait("gpsimd", sPE, last_grp)
                        tw = ph.dma("gpsimd", wo_sb[:],
                                    wo_nat[c * 128:(c + 1) * 128,
                                           jg * 1024:jg * 1024 + 1024], sG)
                        ph.wait("tensor", sG, tw)
                        if jg == 0 and c == 0:
                            ph.wait("tensor", sG, tload)
                        for j8 in range(8):
                            last_grp = ph.run(
                                "tensor", lambda t, c=c, j8=j8: t.matmul(
                                    gps[j8][:], wo_sb[:, j8 * 128:(j8 + 1) * 128],
                                    gwcol[:, c:c + 1],
                                    start=(c == 0), stop=(c == 15)),
                                sPE if j8 == 7 else None)
                    ph.wait("vector", sPE, last_grp)
                    for j8 in range(8):
                        ph.run("vector", lambda v, jg=jg, j8=j8: v.tensor_copy(
                            gcomb[:, jg * 8 + j8:jg * 8 + j8 + 1], gps[j8][:]), sV)
                ph.wait("vector", sL, tc0)
                ph.wait("vector", sG, tload)
            ph.emit()

        def ln_phase(name, src_tile, emit_lo=False):
            ph = Phase(nc, name)
            with ExitStack() as ps:
                EE = ps.enter_context
                xsb = EE(nc.sbuf_tensor(name + "_x", [128, 2 * D], F32))
                scr_a = EE(nc.sbuf_tensor(name + "_scr", [128, D], F32))
                stats = EE(nc.sbuf_tensor(name + "_st", [128, 8], F32))
                xn_tm = EE(nc.sbuf_tensor(name + "_xn", [128, 2 * D], BF16))
                if emit_lo:
                    xf = EE(nc.sbuf_tensor(name + "_xf", [128, D], F32))
                    lob = EE(nc.sbuf_tensor(name + "_lo", [128, 2 * D], BF16))
                    lostg = [EE(nc.sbuf_tensor(name + f"_ls{i}",
                                               [128, ND * 128], BF16))
                             for i in range(2)]
                tp = [EE(nc.psum_tensor(name + f"_tp{i}", [128, 128], BF16))
                      for i in range(2)]
                sX = [SP[0], SP[1]]
                sLo = [SP[2], SP[3]]
                SUM, SSQ, MEAN, MSQ, VAR, SD, RSTD = range(7)
                act_done = []
                tr_last = []
                xn_t = []
                ev_t = []
                lo_st = []
                for tt in range(NT):
                    buf = xsb[:, (tt % 2) * D:(tt % 2) * D + D]
                    nbuf = xn_tm[:, (tt % 2) * D:(tt % 2) * D + D]
                    if tt >= 2:
                        ph.wait("gpsimd", sA, act_done[tt - 2])
                        ph.wait("gpsimd", sV, xn_t[tt - 2])
                    tx = ph.dma("gpsimd", buf, src_tile(tt), sX[tt % 2])
                    ph.wait("scalar", sX[tt % 2], tx)
                    ph.run("scalar", lambda s, b=buf: s.activation(
                        scr_a[:], b, AF.Copy, accum_out=stats[:, SUM:SUM + 1]))
                    tsq = ph.run("scalar", lambda s, b=buf: s.activation(
                        scr_a[:], b, AF.Square, accum_out=stats[:, SSQ:SSQ + 1]),
                        sA)
                    act_done.append(tsq)
                    ph.wait("vector", sA, tsq)
                    ph.run("vector", lambda v: v.tensor_scalar_mul(
                        stats[:, MEAN:MEAN + 1], stats[:, SUM:SUM + 1], 1.0 / D))
                    ph.run("vector", lambda v: v.tensor_mul(
                        stats[:, MSQ:MSQ + 1], stats[:, MEAN:MEAN + 1],
                        stats[:, MEAN:MEAN + 1]))
                    ph.run("vector", lambda v: v.tensor_scalar(
                        stats[:, VAR:VAR + 1], stats[:, SSQ:SSQ + 1], 1.0 / D,
                        None, op0=ALU.mult))
                    ph.run("vector", lambda v: v.tensor_sub(
                        stats[:, VAR:VAR + 1], stats[:, VAR:VAR + 1],
                        stats[:, MSQ:MSQ + 1]))
                    tv = ph.run("vector", lambda v: v.tensor_scalar_add(
                        stats[:, VAR:VAR + 1], stats[:, VAR:VAR + 1], 1e-5), sV)
                    ph.wait("scalar", sV, tv)
                    tsd = ph.run("scalar", lambda s: s.activation(
                        stats[:, SD:SD + 1], stats[:, VAR:VAR + 1], AF.Sqrt), sA)
                    ph.wait("vector", sA, tsd)
                    trc = ph.run("vector", lambda v: v.reciprocal(
                        stats[:, RSTD:RSTD + 1], stats[:, SD:SD + 1]), sV)
                    ph.wait("vector", sV, trc)
                    if tt >= 2:
                        ph.wait("vector", sPE, tr_last[tt - 2])
                    if not emit_lo:
                        txn = ph.run("vector", lambda v, b=buf, nb=nbuf:
                                     v.tensor_scalar(
                                         nb, b, stats[:, MEAN:MEAN + 1],
                                         stats[:, RSTD:RSTD + 1],
                                         op0=ALU.subtract, op1=ALU.mult), sV)
                    else:
                        lbuf = lob[:, (tt % 2) * D:(tt % 2) * D + D]
                        ph.run("vector", lambda v, b=buf: v.tensor_scalar(
                            xf[:], b, stats[:, MEAN:MEAN + 1],
                            stats[:, RSTD:RSTD + 1],
                            op0=ALU.subtract, op1=ALU.mult))
                        ph.run("vector", lambda v, nb=nbuf:
                               v.tensor_copy(nb, xf[:]))
                        txn = ph.run("vector", lambda v, nb=nbuf, lb=lbuf:
                                     v.tensor_sub(lb, xf[:], nb), sV)
                    xn_t.append(txn)
                    ph.wait("tensor", sV, txn)
                    tpt = None
                    for j in range(ND):
                        pb = tp[j % 2]
                        if len(ev_t) >= 2:
                            ph.wait("tensor", sV, ev_t[-2])
                        tpt = ph.run("tensor", lambda t, j=j, pb=pb, nb=nbuf:
                                     t.transpose(pb[:], nb[:, j * 128:(j + 1) * 128],
                                                 ident[:]), sPE)
                        ph.wait("vector", sPE, tpt)
                        dst = xnT[:, j * T + tt * 128: j * T + tt * 128 + 128]
                        ev = ph.run("vector", lambda v, pb=pb, dst=dst:
                                    v.tensor_copy(dst, pb[:]), sV)
                        ev_t.append(ev)
                    if emit_lo:
                        ls = lostg[tt % 2]
                        if tt >= 2:
                            ph.wait("vector", sLo[tt % 2], lo_st[tt - 2])
                        for j in range(ND):
                            pb = tp[j % 2]
                            if len(ev_t) >= 2:
                                ph.wait("tensor", sV, ev_t[-2])
                            tpt = ph.run("tensor",
                                         lambda t, j=j, pb=pb, lb=lbuf:
                                         t.transpose(pb[:],
                                                     lb[:, j * 128:(j + 1) * 128],
                                                     ident[:]), sPE)
                            ph.wait("vector", sPE, tpt)
                            ev = ph.run("vector", lambda v, pb=pb, j=j, ls=ls:
                                        v.tensor_copy(
                                            ls[:, j * 128:(j + 1) * 128],
                                            pb[:]), sV)
                            ev_t.append(ev)
                        ph.wait("sync", sV, ev_t[-1])
                        tst = ph.dma("sync",
                                     xnloT[tt].rearrange("j p c -> p j c"),
                                     ls[:].rearrange("p (j c) -> p j c", c=128),
                                     sLo[tt % 2])
                        lo_st.append(tst)
                    tr_last.append(tpt)
                if emit_lo:
                    ph.wait("sync", sLo[0], sLo[0].n)
                    ph.wait("sync", sLo[1], sLo[1].n)
                if name + ":xnT" in debug:
                    ph.wait("sync", sV, ev_t[-1])
                    o = nc.dram_tensor("dbg_" + name + "_xnT", [128, ND * T],
                                       BF16, kind="ExternalOutput").ap()
                    dbg_outs[name + "_xnT"] = o
                    td = ph.dma("sync", o, xnT[:], sSt)
                    ph.wait("sync", sSt, td)
            ph.emit()

        if "LN1" in phase_set:
            ln_phase("LN1", lambda tt: x_in[tt * 128:(tt + 1) * 128, :],
                     emit_lo=True)

        def projT_phase(name, w_dram, dst_T, do_mem):
            ph = Phase(nc, name)
            with ExitStack() as ps:
                EE = ps.enter_context
                wsb = EE(nc.sbuf_tensor(name + "_w", [128, ND * D], BF16))
                stg = [EE(nc.sbuf_tensor(name + f"_s{i}", [128, 512], BF16))
                       for i in range(4)]
                pps = [EE(nc.psum_tensor(name + f"_p{i}", [128, 512], F32))
                       for i in range(2)]
                sW = SP[0]
                sM = SP[1]
                sWL = [SP[2], SP[3]]
                sSg = [SP[4], SP[5], SP[6], SP[7]]
                sWS = SP[8]
                if do_mem:
                    mstg = EE(nc.sbuf_tensor(name + "_mf", [128, ND * 256], F32))
                    kmh = EE(nc.sbuf_tensor(name + "_mh", [128, ND * 512], BF16))
                    kml = EE(nc.sbuf_tensor(name + "_ml", [128, ND * 512], BF16))
                    wstg = mstg[:, 0:D]
                    wlos = kml[:, 0:D]
                if not do_mem:
                    tw = 0
                    for j in range(ND):
                        tw = ph.dma("gpsimd", wsb[:, j * D:(j + 1) * D],
                                    w_dram[j * 128:(j + 1) * 128, :], sW)
                    ph.wait("tensor", sW, tw)
                else:
                    wl_prev = None
                    for j in range(ND):
                        twf = ph.dma("gpsimd", wstg[:],
                                     w_dram[j * 128:(j + 1) * 128, :], sW)
                        ph.wait("vector", sW, twf)
                        ph.run("vector", lambda v, j=j: v.tensor_copy(
                            wsb[:, j * D:(j + 1) * D], wstg[:]))
                        tlo = ph.run("vector", lambda v, j=j: v.tensor_sub(
                            wlos[:], wstg[:], wsb[:, j * D:(j + 1) * D]), sV)
                        ph.wait("sync", sV, tlo)
                        twl = ph.dma("sync",
                                     wklo[:, j].rearrange("o p c -> p o c"),
                                     wlos[:].rearrange("p (o c) -> p o c", c=128),
                                     sWS)
                        ph.wait("gpsimd", sWS, twl)
                        wl_prev = tlo
                    ph.wait("tensor", sV, wl_prev)
                ev_t, st_t = [], []

                def unit(lhs_list, rhs_list, dst, src_of):
                    u = len(ev_t)
                    pb = pps[u % 2]
                    sb = stg[u % 4]
                    if u >= 2:
                        ph.wait("tensor", sV, ev_t[u - 2])
                    n = len(lhs_list)
                    tpe = None
                    for i in range(n):
                        tpe = ph.run("tensor",
                                     lambda t, l=lhs_list[i], r=rhs_list[i], pb=pb,
                                     i=i, n=n:
                                     t.matmul(pb[:], l, r, start=(i == 0),
                                              stop=(i == n - 1)),
                                     sPE if i == n - 1 else None)
                    ph.wait("vector", sPE, tpe)
                    if u >= 4:
                        ph.wait("vector", sSg[u % 4], st_t[u - 4])
                    ev = ph.run("vector", lambda v, pb=pb, sb=sb:
                                v.tensor_copy(sb[:], pb[:]), sV)
                    ev_t.append(ev)
                    ph.wait("sync", sV, ev)
                    st = ph.dma("sync", dst, src_of(sb), sSg[u % 4])
                    st_t.append(st)
                    return tpe

                for ot in range(ND):
                    for tc in range(4):
                        lhs = [wsb[:, j * D + ot * 128: j * D + ot * 128 + 128]
                               for j in range(ND)]
                        rhs = [xnT[:, j * T + tc * 512: j * T + tc * 512 + 512]
                               for j in range(ND)]
                        dst = dst_T[tc * 4:(tc + 1) * 4, ot].rearrange(
                            "t p c -> p t c")
                        unit(lhs, rhs, dst,
                             lambda sb: sb.rearrange("p (t c) -> p t c", c=128))
                if do_mem:
                    wlr = EE(nc.sbuf_tensor(name + "_wlr", [128, 2 * ND * 128],
                                            BF16))
                    lstg = [EE(nc.sbuf_tensor(name + f"_lo{i}", [128, 512], BF16))
                            for i in range(4)]
                    mem_pe = []
                    wlr_t = {}
                    for mc in range(8):
                        if mc >= 1:
                            ph.wait("gpsimd", sPE, mem_pe[-1])
                        tsp = None
                        for jh in range(2):
                            tm = 0
                            for jj in range(8):
                                j = jh * 8 + jj
                                tm = ph.dma("gpsimd",
                                            mstg[:, jj * 512:(jj + 1) * 512],
                                            membT[j * 128:(j + 1) * 128,
                                                  mc * 512:(mc + 1) * 512], sM)
                            ph.wait("vector", sM, tm)
                            off = jh * 8 * 512
                            ph.run("vector", lambda v, off=off: v.tensor_copy(
                                kmh[:, off:off + 8 * 512], mstg[:]))
                            tsp = ph.run("vector", lambda v, off=off: v.tensor_sub(
                                kml[:, off:off + 8 * 512], mstg[:],
                                kmh[:, off:off + 8 * 512]), sV)
                            if jh == 0:
                                ph.wait("gpsimd", sV, tsp)
                        ph.wait("tensor", sV, tsp)
                        tpe = None
                        for ot in range(ND):
                            ring = (mc * ND + ot) % 2
                            gw = wlr_t.get((mc * ND + ot) - 2)
                            ph.wait("gpsimd", sPE, gw)
                            twl = ph.dma("gpsimd",
                                         wlr[:, ring * ND * 128:
                                             (ring + 1) * ND * 128].rearrange(
                                             "p (j c) -> p j c", c=128),
                                         wklo[ot].rearrange("j p c -> p j c"),
                                         sWL[ring])
                            u = len(ev_t)
                            pb = pps[u % 2]
                            sb = stg[u % 4]
                            lb = lstg[u % 4]
                            if u >= 2:
                                ph.wait("tensor", sV, ev_t[u - 2])
                            for j in range(ND):
                                ph.run("tensor", lambda t, j=j, ot=ot, pb=pb:
                                       t.matmul(
                                           pb[:],
                                           wsb[:, j * D + ot * 128:
                                               j * D + ot * 128 + 128],
                                           kmh[:, j * 512:(j + 1) * 512],
                                           start=(j == 0), stop=False))
                            ph.wait("tensor", sWL[ring], twl)
                            for j in range(ND):
                                ph.run("tensor", lambda t, j=j, ring=ring, pb=pb:
                                       t.matmul(
                                           pb[:],
                                           wlr[:, ring * ND * 128 + j * 128:
                                               ring * ND * 128 + j * 128 + 128],
                                           kmh[:, j * 512:(j + 1) * 512],
                                           start=False, stop=False))
                            tpe = None
                            for j in range(ND):
                                tpe = ph.run(
                                    "tensor", lambda t, j=j, ot=ot, pb=pb:
                                    t.matmul(
                                        pb[:],
                                        wsb[:, j * D + ot * 128:
                                            j * D + ot * 128 + 128],
                                        kml[:, j * 512:(j + 1) * 512],
                                        start=False, stop=(j == ND - 1)),
                                    sPE if j == ND - 1 else None)
                            wlr_t[mc * ND + ot] = tpe
                            ph.wait("vector", sPE, tpe)
                            if u >= 4:
                                ph.wait("vector", sSg[u % 4], st_t[u - 4])
                            ph.run("vector", lambda v, pb=pb, sb=sb:
                                   v.tensor_copy(sb[:], pb[:]))
                            ev = ph.run("vector", lambda v, pb=pb, sb=sb, lb=lb:
                                        v.tensor_sub(lb[:], pb[:], sb[:]), sV)
                            ev_t.append(ev)
                            ph.wait("sync", sV, ev)
                            st = ph.dma("sync", kmemT[mc, ot], sb[:], sSg[u % 4])
                            st = ph.dma("sync", kmemTlo[mc, ot], lb[:],
                                        sSg[u % 4])
                            st_t.append(st)
                        mem_pe.append(tpe)
                for ss in sSg:
                    ph.wait("sync", ss, ss.n)
                ph.wait("sync", sWS, sWS.n)
            ph.emit()

        if "KMT" in phase_set:
            projT_phase("KMT", wkT, KT, True)

        if "VMT" in phase_set:
            ph = Phase(nc, "VMT")
            with ExitStack() as ps:
                EE = ps.enter_context
                wsb = EE(nc.sbuf_tensor("VMT_w", [128, ND * D], BF16))
                stg = [EE(nc.sbuf_tensor(f"VMT_s{i}", [128, 512], BF16))
                       for i in range(4)]
                pps = [EE(nc.psum_tensor(f"VMT_p{i}", [128, 512], F32))
                       for i in range(2)]
                mts = EE(nc.sbuf_tensor("VMT_mt", [128, 2 * ND * 512], BF16))
                sW = SP[0]
                sMh = [SP[1], SP[2]]
                sSg = [SP[4], SP[5], SP[6], SP[7]]
                tw = 0
                for j in range(ND):
                    tw = ph.dma("gpsimd", wsb[:, j * D:(j + 1) * D],
                                wvT[j * 128:(j + 1) * 128, :], sW)
                ph.wait("tensor", sW, tw)
                ev_t, st_t = [], []

                def unit(lhs_list, rhs_list, dst):
                    u = len(ev_t)
                    pb = pps[u % 2]
                    sb = stg[u % 4]
                    if u >= 2:
                        ph.wait("tensor", sV, ev_t[u - 2])
                    n = len(lhs_list)
                    tpe = None
                    for i in range(n):
                        tpe = ph.run("tensor",
                                     lambda t, l=lhs_list[i], r=rhs_list[i], pb=pb,
                                     i=i, n=n:
                                     t.matmul(pb[:], l, r, start=(i == 0),
                                              stop=(i == n - 1)),
                                     sPE if i == n - 1 else None)
                    ph.wait("vector", sPE, tpe)
                    if u >= 4:
                        ph.wait("vector", sSg[u % 4], st_t[u - 4])
                    ev = ph.run("vector", lambda v, pb=pb, sb=sb:
                                v.tensor_copy(sb[:], pb[:]), sV)
                    ev_t.append(ev)
                    ph.wait("sync", sV, ev)
                    st = ph.dma("sync", dst, sb[:], sSg[u % 4])
                    st_t.append(st)
                    return tpe

                for tt in range(NT):
                    for oc in range(4):
                        lhs = [xnT[:, j * T + tt * 128: j * T + tt * 128 + 128]
                               for j in range(ND)]
                        rhs = [wsb[:, j * D + oc * 512: j * D + oc * 512 + 512]
                               for j in range(ND)]
                        unit(lhs, rhs, Vh[tt][:, oc * 512:(oc + 1) * 512])
                mem_pe = []
                for mc in range(8):
                    half = (mc % 2) * ND * 512
                    if mc >= 2:
                        ph.wait("gpsimd", sPE, mem_pe[mc - 2])
                    tm = 0
                    for j in range(ND):
                        tm = ph.dma("gpsimd",
                                    mts[:, half + j * 512: half + j * 512 + 512],
                                    membT[j * 128:(j + 1) * 128,
                                          mc * 512:(mc + 1) * 512], sMh[mc % 2])
                    ph.wait("tensor", sMh[mc % 2], tm)
                    tpe = None
                    for ml in range(4):
                        mt = mc * 4 + ml
                        for oc in range(4):
                            lhs = [mts[:, half + j * 512 + ml * 128:
                                       half + j * 512 + ml * 128 + 128]
                                   for j in range(ND)]
                            rhs = [wsb[:, j * D + oc * 512: j * D + oc * 512 + 512]
                                   for j in range(ND)]
                            tpe = unit(lhs, rhs, vmem2[oc, mt])
                    mem_pe.append(tpe)
                for ss in sSg:
                    ph.wait("sync", ss, ss.n)
            ph.emit()

        if "SIM" in phase_set:
            ph = Phase(nc, "SIM")
            with ExitStack() as ps:
                EE = ps.enter_context
                kms = EE(nc.sbuf_tensor("SIM_km", [128, 2 * ND * 512], BF16))
                kls = EE(nc.sbuf_tensor("SIM_kl", [128, 2 * ND * 512], BF16))
                xlr = EE(nc.sbuf_tensor("SIM_xl", [128, 2 * ND * 128], BF16))
                stg = [EE(nc.sbuf_tensor(f"SIM_s{i}", [128, 512], F32))
                       for i in range(4)]
                pps = [EE(nc.psum_tensor(f"SIM_p{i}", [128, 512], F32))
                       for i in range(2)]
                sMh = [SP[0], SP[1]]
                sXl = [SP[2], SP[3]]
                sSg = [SP[4], SP[5], SP[6], SP[7]]
                ev_t, st_t, mem_pe = [], [], []
                xlr_t = {}
                for mc in range(8):
                    half = (mc % 2) * ND * 512
                    if mc >= 2:
                        ph.wait("gpsimd", sPE, mem_pe[mc - 2])
                    ph.dma("gpsimd",
                           kms[:, half:half + ND * 512].rearrange(
                               "p (j c) -> p j c", c=512),
                           kmemT[mc].rearrange("j p c -> p j c"), sMh[mc % 2])
                    tm = ph.dma("gpsimd",
                                kls[:, half:half + ND * 512].rearrange(
                                    "p (j c) -> p j c", c=512),
                                kmemTlo[mc].rearrange("j p c -> p j c"),
                                sMh[mc % 2])
                    ph.wait("tensor", sMh[mc % 2], tm)
                    tpe = None
                    for tt in range(NT):
                        idx = mc * NT + tt
                        ring = idx % 2
                        gw = xlr_t.get(idx - 2)
                        ph.wait("gpsimd", sPE, gw)
                        txl = ph.dma("gpsimd",
                                     xlr[:, ring * ND * 128:
                                         (ring + 1) * ND * 128].rearrange(
                                         "p (j c) -> p j c", c=128),
                                     xnloT[tt].rearrange("j p c -> p j c"),
                                     sXl[ring])
                        u = len(ev_t)
                        pb = pps[u % 2]
                        sb = stg[u % 4]
                        if u >= 2:
                            ph.wait("tensor", sV, ev_t[u - 2])
                        for j in range(ND):
                            ph.run("tensor",
                                   lambda t, j=j, tt=tt, pb=pb, half=half:
                                   t.matmul(pb[:],
                                            xnT[:, j * T + tt * 128:
                                                j * T + tt * 128 + 128],
                                            kms[:, half + j * 512:
                                                half + j * 512 + 512],
                                            start=(j == 0), stop=False))
                        ph.wait("tensor", sXl[ring], txl)
                        for j in range(ND):
                            ph.run("tensor",
                                   lambda t, j=j, ring=ring, pb=pb, half=half:
                                   t.matmul(pb[:],
                                            xlr[:, ring * ND * 128 + j * 128:
                                                ring * ND * 128 + j * 128 + 128],
                                            kms[:, half + j * 512:
                                                half + j * 512 + 512],
                                            start=False, stop=False))
                        for j in range(ND):
                            tpe = ph.run(
                                "tensor",
                                lambda t, j=j, tt=tt, pb=pb, half=half:
                                t.matmul(pb[:],
                                         xnT[:, j * T + tt * 128:
                                             j * T + tt * 128 + 128],
                                         kls[:, half + j * 512:
                                             half + j * 512 + 512],
                                         start=False, stop=(j == ND - 1)),
                                sPE if j == ND - 1 else None)
                        xlr_t[idx] = tpe
                        ph.wait("vector", sPE, tpe)
                        if u >= 4:
                            ph.wait("vector", sSg[u % 4], st_t[u - 4])
                        ev = ph.run("vector", lambda v, pb=pb, sb=sb:
                                    v.tensor_scalar_mul(
                                        sb[:], pb[:],
                                        1.0 / float(np.sqrt(D))), sV)
                        ev_t.append(ev)
                        ph.wait("sync", sV, ev)
                        st = ph.dma("sync", simd[tt][:, mc * 512:(mc + 1) * 512],
                                    sb[:], sSg[u % 4])
                        st_t.append(st)
                    mem_pe.append(tpe)
                for ss in sSg:
                    ph.wait("sync", ss, ss.n)
            ph.emit()

        if "OMEM" in phase_set:
            ph = Phase(nc, "OMEM")
            GT = 2
            with ExitStack() as ps:
                EE = ps.enter_context
                simg = EE(nc.sbuf_tensor("OM_sim", [128, GT * M], F32))
                wfs = EE(nc.sbuf_tensor("OM_wf", [128, GT * M], BF16))
                wft = EE(nc.sbuf_tensor("OM_wft", [128, GT * M], BF16))
                wexp = EE(nc.sbuf_tensor("OM_we", [128, 2 * M], BF16))
                tk = EE(nc.sbuf_tensor("OM_tk", [128, 40], F32))
                vms = EE(nc.sbuf_tensor("OM_vm", [128, 2 * 16 * 512], BF16))
                stg = [EE(nc.sbuf_tensor(f"OM_s{i}", [128, 512], BF16))
                       for i in range(4)]
                tp = [EE(nc.psum_tensor(f"OM_tp{i}", [128, 128], BF16))
                      for i in range(2)]
                om = [EE(nc.psum_tensor(f"OM_p{i}", [128, 512], F32))
                      for i in range(GT)]
                sSim = SP[0]
                sVm = [SP[1], SP[2]]
                sSg = [SP[4], SP[5], SP[6], SP[7]]
                TA, TB, T8, NT1, Z8, LNZ, B2, TT8, E8 = (
                    0, 8, 16, 24, 25, 26, 27, 28, 32)
                st_t = []
                prev_group_ev = None
                half_pe = {}
                twf_all = []
                g_tr_last = []
                for g in range(NT // GT):
                    if g >= 1:
                        ph.wait("gpsimd", sPE, g_tr_last[g - 1])
                        ph.wait("gpsimd", sV, twf_all[-1])
                    tload = ph.dma("gpsimd",
                                   simg[:].rearrange("p (t c) -> p t c", c=M),
                                   simd[g * GT:(g + 1) * GT].rearrange(
                                       "t p c -> p t c"), sSim)
                    tr_ev = []
                    tpt = None
                    for tl in range(GT):
                        sl = simg[:, tl * M:(tl + 1) * M]
                        wf = wfs[:, tl * M:(tl + 1) * M]
                        if tl == 0:
                            ph.wait("vector", sSim, tload)
                        ph.run("vector", lambda v, sl=sl: v.max(
                            out=tk[:, TA:TA + 8], in_=sl[:, :M // 2]))
                        thm = ph.run("vector", lambda v, sl=sl: v.max(
                            out=tk[:, TB:TB + 8], in_=sl[:, M // 2:]), sV)
                        ph.wait("vector", sV, thm)
                        tmax = ph.run("vector", lambda v: v.max(
                            out=tk[:, T8:T8 + 8], in_=tk[:, TA:TA + 16]), sV)
                        ph.wait("vector", sV, tmax)
                        ph.run("vector", lambda v: v.tensor_scalar_mul(
                            tk[:, NT1:NT1 + 1], tk[:, T8:T8 + 1], -1.0))
                        tnt = ph.run("vector", lambda v: v.tensor_copy(
                            tk[:, TT8:TT8 + 1], tk[:, T8 + 7:T8 + 8]), sV)
                        ph.wait("scalar", sV, tnt)
                        tz = ph.run("scalar", lambda s: s.activation(
                            tk[:, E8:E8 + 8], tk[:, T8:T8 + 8], AF.Exp,
                            bias=tk[:, NT1:NT1 + 1],
                            accum_out=tk[:, Z8:Z8 + 1]), sA)
                        ph.wait("scalar", sA, tz)
                        tln = ph.run("scalar", lambda s: s.activation(
                            tk[:, LNZ:LNZ + 1], tk[:, Z8:Z8 + 1], AF.Ln), sA)
                        ph.wait("vector", sA, tln)
                        tb2 = ph.run("vector", lambda v: v.tensor_scalar(
                            tk[:, B2:B2 + 1], tk[:, LNZ:LNZ + 1],
                            tk[:, T8:T8 + 1], -1.0,
                            op0=ALU.add, op1=ALU.mult), sV)
                        ph.wait("scalar", sV, tb2)
                        we = wexp[:, (tl % 2) * M:(tl % 2) * M + M]
                        if len(twf_all) >= 2:
                            ph.wait("scalar", sV, twf_all[-2])
                        twe = ph.run("scalar", lambda s, sl=sl, we=we: s.activation(
                            we, sl, AF.Exp, bias=tk[:, B2:B2 + 1]), sA)
                        ph.wait("vector", sA, twe)
                        twf = ph.run("vector", lambda v, sl=sl, we=we, wf=wf:
                                     v.scalar_tensor_tensor(
                                         wf, sl, tk[:, TT8:TT8 + 1], we,
                                         op0=ALU.is_ge, op1=ALU.mult), sV)
                        twf_all.append(twf)
                        ph.wait("tensor", sV, twf)
                        for mb in range(32):
                            pb = tp[mb % 2]
                            if len(tr_ev) >= 2:
                                ph.wait("tensor", sV, tr_ev[-2])
                            tpt = ph.run("tensor", lambda t, wf=wf, mb=mb, pb=pb:
                                         t.transpose(
                                             pb[:], wf[:, mb * 128:(mb + 1) * 128],
                                             ident[:]), sPE)
                            ph.wait("vector", sPE, tpt)
                            dst = wft[:, tl * M + mb * 128:tl * M + mb * 128 + 128]
                            ev = ph.run("vector", lambda v, pb=pb, dst=dst:
                                        v.tensor_copy(dst, pb[:]), sV)
                            tr_ev.append(ev)
                    g_tr_last.append(tpt)
                    if g == 0 and "OM:wft" in debug:
                        o = nc.dram_tensor("dbg_OM_wft", [128, GT * M], BF16,
                                           kind="ExternalOutput").ap()
                        dbg_outs["OM_wft"] = o
                        ph.wait("sync", sV, tr_ev[-1])
                        tdw = ph.dma("sync", o, wft[:], sSt)
                        ph.wait("sync", sSt, tdw)
                    for oc in range(4):
                        for hf in range(2):
                            half = hf * 16 * 512
                            prev = half_pe.get((oc - 1, hf) if oc > 0
                                               else (3, hf))
                            ph.wait("gpsimd", sPE, prev)
                            tvm = ph.dma("gpsimd",
                                         vms[:, half:half + 16 * 512].rearrange(
                                             "p (m c) -> p m c", c=512),
                                         vmem2[oc, hf * 16:(hf + 1) * 16].rearrange(
                                             "m p c -> p m c"), sVm[hf])
                            ph.wait("tensor", sVm[hf], tvm)
                            if hf == 0:
                                ph.wait("tensor", sV, tr_ev[-1])
                                ph.wait("tensor", sV, prev_group_ev)
                            tmm = None
                            for tl in range(GT):
                                for ml in range(16):
                                    mb = hf * 16 + ml
                                    tmm = ph.run(
                                        "tensor",
                                        lambda t, tl=tl, ml=ml, mb=mb, half=half:
                                        t.matmul(
                                            om[tl][:],
                                            wft[:, tl * M + mb * 128:
                                                tl * M + mb * 128 + 128],
                                            vms[:, half + ml * 512:
                                                half + ml * 512 + 512],
                                            start=(mb == 0), stop=(mb == 31)),
                                        sPE if ml == 15 else None)
                            half_pe[(oc, hf)] = tmm
                        evs = []
                        for tl in range(GT):
                            u = len(st_t)
                            sb = stg[u % 4]
                            ph.wait("vector", sPE, half_pe[(oc, 1)] - (GT - 1 - tl))
                            if u >= 4:
                                ph.wait("vector", sSg[u % 4], st_t[u - 4])
                            ev = ph.run("vector", lambda v, tl=tl, sb=sb:
                                        v.tensor_copy(sb[:], om[tl][:]), sV)
                            evs.append(ev)
                            ph.wait("sync", sV, ev)
                            st = ph.dma("sync",
                                        omemp[g * GT + tl][:,
                                                           oc * 512:(oc + 1) * 512],
                                        sb[:], sSg[u % 4])
                            st_t.append(st)
                        prev_group_ev = evs[-1]
                for ss in sSg:
                    ph.wait("sync", ss, ss.n)
            ph.emit()

        if "QT" in phase_set:
            projT_phase("QT", wqT, QT, False)

        if "CTX" in phase_set:
            ph = Phase(nc, "CTX")
            with ExitStack() as ps:
                EE = ps.enter_context
                qsb = EE(nc.sbuf_tensor("CT_q", [128, 2 * D], BF16))
                ktr = EE(nc.sbuf_tensor("CT_kt", [128, 4 * D], BF16))
                vr = EE(nc.sbuf_tensor("CT_v", [128, 4 * D], BF16))
                psb = EE(nc.sbuf_tensor("CT_p", [128, 4 * 384], BF16))
                ptb = EE(nc.sbuf_tensor("CT_pt", [128, 4 * 384], BF16))
                zz = EE(nc.sbuf_tensor("CT_z", [128, 16], F32))
                ctm = EE(nc.sbuf_tensor("CT_c", [128, 2 * D], BF16))
                cstT = EE(nc.sbuf_tensor("CT_ct", [128, 2 * D], BF16))
                sc = [EE(nc.psum_tensor(f"CT_sc{i}", [128, 384], F32))
                      for i in range(4)]
                tp = [EE(nc.psum_tensor(f"CT_tp{i}", [128, 384], BF16))
                      for i in range(2)]
                otp = [EE(nc.psum_tensor(f"CT_ot{i}", [128, 128], F32))
                       for i in range(2)]
                st_t = []
                exp_t = []
                score_last = []
                av_last = []
                otev_t = []
                trp_last = []
                tp_last = {}
                zr_t = []
                tail_last = []
                tail_pe = []
                sTl = [SP[0], SP[1], SP[2], SP[3]]
                sCs = [SP[4], SP[5]]
                tl_hist = {}
                un = 0
                for tt in range(NT):
                    if tt >= 4:
                        ph.wait("gpsimd", sPE, av_last[tt - 2])
                    if tt >= 2:
                        ph.wait("gpsimd", sPE, score_last[tt - 2])
                    ph.dma("gpsimd",
                           ktr[:, (tt % 4) * D:(tt % 4) * D + D].rearrange(
                               "p (j c) -> p j c", c=128),
                           KT[tt].rearrange("j p c -> p j c"), sTl[tt % 4])
                    ph.dma("gpsimd", vr[:, (tt % 4) * D:(tt % 4) * D + D],
                           Vh[tt], sTl[tt % 4])
                    tld = ph.dma("gpsimd",
                                 qsb[:, (tt % 2) * D:(tt % 2) * D + D].rearrange(
                                     "p (j c) -> p j c", c=128),
                                 QT[tt].rearrange("j p c -> p j c"),
                                 sTl[tt % 4])
                    tl_hist[tt] = tld
                    nb = min(tt + 1, 3)
                    kts = list(range(tt - nb + 1, tt + 1))
                    q = qsb[:, (tt % 2) * D:(tt % 2) * D + D]
                    for h in range(16):
                        pb = sc[un % 4]
                        if un >= 4:
                            ph.wait("tensor", sA, exp_t[un - 4])
                        if h == 0:
                            ph.wait("tensor", sTl[tt % 4], tld)
                        tsc = None
                        for bi, kt in enumerate(kts):
                            tsc = ph.run(
                                "tensor", lambda t, h=h, bi=bi, kt=kt, pb=pb, q=q:
                                t.matmul(pb[:, bi * 128:(bi + 1) * 128],
                                         q[:, h * 128:(h + 1) * 128],
                                         ktr[:, (kt % 4) * D + h * 128:
                                             (kt % 4) * D + h * 128 + 128],
                                         start=True, stop=True),
                                sPE if bi == nb - 1 else None)
                        if h == 15:
                            score_last.append(tsc)
                        ph.wait("vector", sPE, tsc)
                        tm = ph.run("vector", lambda v, pb=pb, nb=nb:
                                    v.tensor_tensor(
                                        pb[:, 0:nb * 128], pb[:, 0:nb * 128],
                                        maskw[:, (3 - nb) * 128:384], op=ALU.add),
                                    sV)
                        ph.wait("scalar", sV, tm)
                        zc = zz[:, un % 8:un % 8 + 1]
                        pcur = psb[:, (un % 4) * 384:(un % 4) * 384 + nb * 128]
                        if un >= 4:
                            ph.wait("scalar", sPE, trp_last[un - 4])
                            ph.wait("scalar", sV, zr_t[un - 4])
                        te = ph.run("scalar", lambda s, pb=pb, nb=nb, zc=zc,
                                    pcur=pcur:
                                    s.activation(pcur, pb[:, 0:nb * 128], AF.Exp,
                                                 bias=cmisc[:, C_NB32:C_NB32 + 1],
                                                 scale=cmisc[:, C_S128:C_S128 + 1],
                                                 accum_out=zc), sA)
                        exp_t.append(te)
                        ph.wait("vector", sA, te)
                        rzc = zz[:, 8 + un % 8:8 + un % 8 + 1]
                        trc = ph.run("vector", lambda v, zc=zc, rzc=rzc:
                                     v.reciprocal(rzc, zc), sV)
                        ph.wait("vector", sV, trc)
                        zr_t.append(trc)
                        pt = tp[un % 2]
                        g2 = tp_last.get(un % 2)
                        if g2:
                            ph.wait("tensor", g2[1], g2[0])
                        ph.wait("tensor", sA, te)
                        tpt = None
                        for bi in range(nb):
                            tpt = ph.run(
                                "tensor", lambda t, bi=bi, pt=pt, pcur=pcur:
                                t.transpose(
                                    pt[:, bi * 128:(bi + 1) * 128],
                                    pcur[:, bi * 128:(bi + 1) * 128],
                                    ident[:]), sPE if bi == nb - 1 else None)
                        eng, sem = (("vector", sV) if (un % 2 == 0)
                                    else ("scalar", sA))
                        ph.wait(eng, sPE, tpt)
                        dst = ptb[:, (un % 4) * 384:(un % 4) * 384 + nb * 128]
                        if eng == "vector":
                            ev = ph.run("vector", lambda v, pt=pt, dst=dst, nb=nb:
                                        v.tensor_copy(dst, pt[:, 0:nb * 128]), sem)
                        else:
                            ev = ph.run("scalar", lambda s, pt=pt, dst=dst, nb=nb:
                                        s.copy(dst, pt[:, 0:nb * 128]), sem)
                        tp_last[un % 2] = (ev, sem)
                        trp_last.append(tpt)
                        po = otp[un % 2]
                        if un >= 2:
                            ph.wait("tensor", sA, otev_t[un - 2])
                        ph.wait("tensor", sem, ev)
                        tav = None
                        for bi, kt in enumerate(kts):
                            tav = ph.run(
                                "tensor", lambda t, bi=bi, kt=kt, h=h, po=po,
                                un=un:
                                t.matmul(po[:],
                                         ptb[:, (un % 4) * 384 + bi * 128:
                                             (un % 4) * 384 + bi * 128 + 128],
                                         vr[:, (kt % 4) * D + h * 128:
                                            (kt % 4) * D + h * 128 + 128],
                                         start=(bi == 0), stop=(bi == nb - 1)),
                                sPE if bi == nb - 1 else None)
                        if h == 15:
                            av_last.append(tav)
                        ph.wait("scalar", sPE, tav)
                        if h == 0 and tt >= 2:
                            ph.wait("scalar", sPE, tail_pe[tt - 2])
                        ph.wait("scalar", sV, trc)
                        dst = ctm[:, (tt % 2) * D + h * 128:
                                  (tt % 2) * D + h * 128 + 128]
                        tev = ph.run("scalar", lambda s, po=po, dst=dst, rzc=rzc:
                                     s.activation(dst, po[:], AF.Copy,
                                                  scale=rzc), sA)
                        otev_t.append(tev)
                        un += 1
                    ph.wait("tensor", sA, otev_t[-1])
                    if tt >= 2:
                        ph.wait("scalar", sCs[tt % 2], st_t[tt - 2])
                    tails = None
                    for j in range(ND):
                        pt = tp[j % 2]
                        g2 = tp_last.get(j % 2)
                        if g2:
                            ph.wait("tensor", g2[1], g2[0])
                        tpt = ph.run(
                            "tensor", lambda t, j=j, pt=pt, tt=tt:
                            t.transpose(pt[:, 0:128],
                                        ctm[:, (tt % 2) * D + j * 128:
                                            (tt % 2) * D + j * 128 + 128],
                                        ident[:]), sPE)
                        eng, sem = (("vector", sV) if (j % 2 == 0)
                                    else ("scalar", sA))
                        ph.wait(eng, sPE, tpt)
                        dst = cstT[:, (tt % 2) * D + j * 128:
                                   (tt % 2) * D + j * 128 + 128]
                        if eng == "vector":
                            ev = ph.run("vector", lambda v, pt=pt, dst=dst:
                                        v.tensor_copy(dst, pt[:, 0:128]), sem)
                        else:
                            ev = ph.run("scalar", lambda s, pt=pt, dst=dst:
                                        s.copy(dst, pt[:, 0:128]), sem)
                        tp_last[j % 2] = (ev, sem)
                        tails = (ev, sem)
                        tail_tpt = tpt
                    tail_last.append(tails)
                    tail_pe.append(tail_tpt)
                    ph.wait("sync", tails[1], tails[0])
                    ph.wait("sync", sV if tails[1] is sA else sA,
                            (sV if tails[1] is sA else sA).n)
                    st = ph.dma("sync", ctxT[tt].rearrange("j p c -> p j c"),
                                cstT[:, (tt % 2) * D:(tt % 2) * D + D].rearrange(
                                    "p (j c) -> p j c", c=128), sCs[tt % 2])
                    st_t.append(st)
                ph.wait("sync", sCs[0], sCs[0].n)
                ph.wait("sync", sCs[1], sCs[1].n)
            ph.emit()

        if "CMB" in phase_set:
            ph = Phase(nc, "CMB")
            with ExitStack() as ps:
                EE = ps.enter_context
                ctxa = xnT
                omt = EE(nc.sbuf_tensor("CM_om", [128, ND * T], BF16))
                otile = EE(nc.sbuf_tensor("CM_ot", [128, 2 * D], BF16))
                wos = EE(nc.sbuf_tensor("CM_wo", [128, 2 * ND * 512], BF16))
                gall = EE(nc.sbuf_tensor("CM_g", [128, 16], F32))
                ols = [EE(nc.sbuf_tensor(f"CM_ol{i}", [128, 512], F32))
                       for i in range(2)]
                tmp = EE(nc.sbuf_tensor("CM_tmp", [128, 512], F32))
                xsb = [EE(nc.sbuf_tensor(f"CM_x{i}", [128, 512], F32))
                       for i in range(2)]
                xnw = [EE(nc.sbuf_tensor(f"CM_xn{i}", [128, 512], F32))
                       for i in range(2)]
                tp = [EE(nc.psum_tensor(f"CM_tp{i}", [128, 128], BF16))
                      for i in range(2)]
                gps = [EE(nc.psum_tensor(f"CM_gp{i}", [128, 1], F32))
                       for i in range(2)]
                pso = [EE(nc.psum_tensor(f"CM_po{i}", [128, 512], F32))
                       for i in range(2)]
                psm = [EE(nc.psum_tensor(f"CM_pm{i}", [128, 512], F32))
                       for i in range(2)]
                sCtx = SP[0]
                sOm = [SP[1], SP[2]]
                sWo = [SP[3], SP[8]]
                sXl2 = [SP[4], SP[5]]
                sYs = [SP[6], SP[7]]
                xnT_v = xnT.rearrange("p (j c) -> p j c", c=T)
                tctx = 0
                for tt in range(NT):
                    tctx = ph.dma("gpsimd",
                                  xnT_v[:, :, tt * 128:tt * 128 + 128],
                                  ctxT[tt].rearrange("j p c -> p j c"), sCtx)
                trv = []
                om_tr_last = []
                for tt in range(NT):
                    ob = otile[:, (tt % 2) * D:(tt % 2) * D + D]
                    if tt >= 2:
                        ph.wait("gpsimd", sPE, om_tr_last[tt - 2])
                    tl = ph.dma("gpsimd", ob, omemp[tt], sOm[tt % 2])
                    ph.wait("tensor", sOm[tt % 2], tl)
                    tpt = None
                    for j in range(ND):
                        pb = tp[j % 2]
                        if len(trv) >= 2:
                            ph.wait("tensor", sV, trv[-2])
                        tpt = ph.run("tensor", lambda t, j=j, pb=pb, ob=ob:
                                     t.transpose(pb[:], ob[:, j * 128:(j + 1) * 128],
                                                 ident[:]), sPE)
                        ph.wait("vector", sPE, tpt)
                        ev = ph.run("vector", lambda v, pb=pb, j=j, tt=tt:
                                    v.tensor_copy(
                                        omt[:, j * T + tt * 128:
                                            j * T + tt * 128 + 128], pb[:]), sV)
                        trv.append(ev)
                    om_tr_last.append(tpt)
                ph.wait("tensor", sCtx, tctx)
                gev = []
                for tt in range(NT):
                    gp = gps[tt % 2]
                    if tt >= 2:
                        ph.wait("tensor", sA, gev[tt - 2])
                    tg = None
                    for j in range(ND):
                        tg = ph.run("tensor", lambda t, j=j, tt=tt, gp=gp:
                                    t.matmul(gp[:],
                                             ctxa[:, j * T + tt * 128:
                                                  j * T + tt * 128 + 128],
                                             gcomb[:, j:j + 1],
                                             start=(j == 0), stop=(j == ND - 1)),
                                    sPE if j == ND - 1 else None)
                    ph.wait("scalar", sPE, tg)
                    ge = ph.run("scalar", lambda s, gp=gp, tt=tt: s.activation(
                        gall[:, tt:tt + 1], gp[:], AF.Sigmoid, bias=gbb[:]), sA)
                    gev.append(ge)
                ph.wait("tensor", sA, gev[-1])
                st_t, ev_t = [], []
                wo_pe = {}
                for oc in range(4):
                    half = (oc % 2) * ND * 512
                    if oc >= 2:
                        ph.wait("gpsimd", sPE, wo_pe[oc - 2])
                    twl = ph.dma("gpsimd",
                                 wos[:, half:half + ND * 512].rearrange(
                                     "p (j c) -> p j c", c=512),
                                 woT[:, oc * 512:(oc + 1) * 512].rearrange(
                                     "(j p) c -> p j c", p=128), sWo[oc % 2])
                    ph.wait("tensor", sWo[oc % 2], twl)
                    tpm = None
                    for tt in range(NT):
                        u = len(st_t)
                        po, pm = pso[u % 2], psm[u % 2]
                        if u >= 2:
                            ph.wait("tensor", sV, ev_t[u - 2])
                        for j in range(ND):
                            ph.run("tensor", lambda t, j=j, tt=tt, po=po, half=half:
                                   t.matmul(po[:],
                                            ctxa[:, j * T + tt * 128:
                                                 j * T + tt * 128 + 128],
                                            wos[:, half + j * 512:
                                                half + j * 512 + 512],
                                            start=(j == 0), stop=(j == ND - 1)))
                        for j in range(ND):
                            tpm = ph.run(
                                "tensor", lambda t, j=j, tt=tt, pm=pm, half=half:
                                t.matmul(pm[:],
                                         omt[:, j * T + tt * 128:
                                             j * T + tt * 128 + 128],
                                         wos[:, half + j * 512:
                                             half + j * 512 + 512],
                                         start=(j == 0), stop=(j == ND - 1)),
                                sPE if j == ND - 1 else None)
                        tx = ph.dma("sync", xsb[u % 2][:],
                                    x_in[tt * 128:(tt + 1) * 128,
                                         oc * 512:(oc + 1) * 512], sXl2[u % 2])
                        ph.wait("scalar", sPE, tpm)
                        if u >= 2:
                            ph.wait("scalar", sV, ev_t[u - 2])
                        tol = ph.run("scalar", lambda s, po=po, u=u:
                                     s.copy(ols[u % 2][:], po[:]), sA)
                        ph.wait("vector", sA, tol)
                        ph.run("vector", lambda v, pm=pm, u=u, tt=tt:
                               v.scalar_tensor_tensor(
                                   tmp[:], pm[:], gall[:, tt:tt + 1],
                                   ols[u % 2][:], op0=ALU.mult, op1=ALU.add))
                        ph.wait("vector", sXl2[u % 2], tx)
                        if u >= 2:
                            ph.wait("vector", sYs[u % 2], st_t[u - 2])
                        ev = ph.run("vector", lambda v, u=u:
                                    v.tensor_add(xnw[u % 2][:], tmp[:],
                                                 xsb[u % 2][:]), sV)
                        ev_t.append(ev)
                        ph.wait("sync", sV, ev)
                        st = ph.dma("sync", xres[tt][:, oc * 512:(oc + 1) * 512],
                                    xnw[u % 2][:], sYs[u % 2])
                        st_t.append(st)
                    wo_pe[oc] = tpm
                ph.wait("sync", sYs[0], sYs[0].n)
                ph.wait("sync", sYs[1], sYs[1].n)
            ph.emit()

        if "LN2" in phase_set:
            ln_phase("LN2", lambda tt: xres[tt])

        def f1_phase(name):
            ph = Phase(nc, name)
            with ExitStack() as ps:
                EE = ps.enter_context
                wg_sb = EE(nc.sbuf_tensor(name + "_wg", [128, 2 * ND * 512], BF16))
                wu_sb = EE(nc.sbuf_tensor(name + "_wu", [128, 2 * ND * 512], BF16))
                hg = [EE(nc.sbuf_tensor(name + f"_hg{i}", [128, 512], BF16))
                      for i in range(2)]
                fst = [EE(nc.sbuf_tensor(name + f"_f{i}", [128, 512], BF16))
                       for i in range(4)]
                psg = [EE(nc.psum_tensor(name + f"_pg{i}", [128, 512], F32))
                       for i in range(2)]
                psu = [EE(nc.psum_tensor(name + f"_pu{i}", [128, 512], F32))
                       for i in range(2)]
                st_t, ev_t, grp_pe = [], [], {}
                hg_done = []
                sWh = [SP[0], SP[1]]
                sSg = [SP[4], SP[5], SP[6], SP[7]]
                for ibg in range(16):
                    half = (ibg % 2) * ND * 512
                    if ibg >= 2:
                        ph.wait("gpsimd", sPE, grp_pe[ibg - 2])
                    ph.dma("gpsimd",
                           wg_sb[:, half:half + ND * 512].rearrange(
                               "p (j c) -> p j c", c=512),
                           wgR[ibg].rearrange("j p c -> p j c"), sWh[ibg % 2])
                    twl = ph.dma("gpsimd",
                                 wu_sb[:, half:half + ND * 512].rearrange(
                                     "p (j c) -> p j c", c=512),
                                 wuR[ibg].rearrange("j p c -> p j c"),
                                 sWh[ibg % 2])
                    ph.wait("tensor", sWh[ibg % 2], twl)
                    tpu = None
                    for ibl in range(4):
                        ib = ibg * 4 + ibl
                        for tc in range(4):
                            u = len(st_t)
                            pg, pu = psg[u % 2], psu[u % 2]
                            if u >= 2:
                                ph.wait("tensor", sA, hg_done[u - 2])
                                ph.wait("tensor", sV, ev_t[u - 2])
                            for j in range(ND):
                                ph.run("tensor",
                                       lambda t, j=j, ibl=ibl, tc=tc, pg=pg,
                                       half=half:
                                       t.matmul(
                                           pg[:],
                                           wg_sb[:, half + j * 512 + ibl * 128:
                                                 half + j * 512 + ibl * 128 + 128],
                                           xnT[:, j * T + tc * 512:
                                               j * T + tc * 512 + 512],
                                           start=(j == 0), stop=(j == ND - 1)))
                            for j in range(ND):
                                tpu = ph.run(
                                    "tensor",
                                    lambda t, j=j, ibl=ibl, tc=tc, pu=pu, half=half:
                                    t.matmul(
                                        pu[:],
                                        wu_sb[:, half + j * 512 + ibl * 128:
                                              half + j * 512 + ibl * 128 + 128],
                                        xnT[:, j * T + tc * 512:
                                            j * T + tc * 512 + 512],
                                        start=(j == 0), stop=(j == ND - 1)),
                                    sPE if j == ND - 1 else None)
                            ph.wait("scalar", sPE, tpu)
                            if u >= 2:
                                ph.wait("scalar", sV, ev_t[u - 2])
                            th = ph.run("scalar", lambda s, pg=pg, u=u, ib=ib:
                                        s.activation(hg[u % 2][:], pg[:], AF.Silu,
                                                     scale=sgc[:, ib:ib + 1]), sA)
                            hg_done.append(th)
                            ph.wait("vector", sA, th)
                            if u >= 4:
                                ph.wait("vector", sSg[u % 4], st_t[u - 4])
                            sb = fst[u % 4]
                            ev = ph.run("vector",
                                        lambda v, pu=pu, u=u, ib=ib, sb=sb:
                                        v.scalar_tensor_tensor(
                                            sb[:], pu[:], suc[:, ib:ib + 1],
                                            hg[u % 2][:], op0=ALU.mult,
                                            op1=ALU.mult), sV)
                            ev_t.append(ev)
                            ph.wait("sync", sV, ev)
                            st = ph.dma("sync",
                                        fuse[ib][:, tc * 512:(tc + 1) * 512],
                                        sb[:], sSg[u % 4])
                            st_t.append(st)
                    grp_pe[ibg] = tpu
                for ss in sSg:
                    ph.wait("sync", ss, ss.n)
            ph.emit()

        def f2_phase(name, dst_of):
            ph = Phase(nc, name)
            with ExitStack() as ps:
                EE = ps.enter_context
                wd_sb = EE(nc.sbuf_tensor(name + "_wd", [128, 64 * 512], BF16))
                sdbc = EE(nc.sbuf_tensor(name + "_sd", [128, D], F32))
                tsd = ph.dma("gpsimd", sdbc[:], sdbc_d, sG)
                ph.wait("vector", sG, tsd)
                fsb = [EE(nc.sbuf_tensor(name + f"_fs{i}", [128, 4 * 512], BF16))
                       for i in range(2)]
                xsb = [EE(nc.sbuf_tensor(name + f"_x{i}", [128, 512], F32))
                       for i in range(2)]
                tmp = EE(nc.sbuf_tensor(name + "_tmp", [128, 512], F32))
                ynw = [EE(nc.sbuf_tensor(name + f"_y{i}", [128, 512], F32))
                       for i in range(2)]
                psy = [EE(nc.psum_tensor(name + f"_py{i}", [128, 512], F32))
                       for i in range(8)]
                sWd = [SP[0], SP[1]]
                sFs = [SP[2], SP[3]]
                sXl2 = [SP[4], SP[5]]
                sYs = [SP[6], SP[7]]
                st_t = []
                ev_by_unit = {}
                fs_pe = {}
                wd_pe = {}
                nchunk = 0
                for oc in range(4):
                    if oc >= 1:
                        ph.wait("gpsimd", sPE, wd_pe.get(oc - 1))
                    twd = 0
                    for j8 in range(8):
                        twd = ph.dma(
                            "gpsimd",
                            wd_sb[:, j8 * 8 * 512:(j8 + 1) * 8 * 512].rearrange(
                                "p (j c) -> p j c", c=512),
                            wd_i8[j8 * 1024:(j8 + 1) * 1024,
                                  oc * 512:(oc + 1) * 512].rearrange(
                                      "(j p) c -> p j c", p=128), sWd[oc % 2])
                    ph.wait("tensor", sWd[oc % 2], twd)
                    for g4 in range(4):
                        pset = [psy[(g4 % 2) * 4 + i] for i in range(4)]
                        uid = oc * 4 + g4
                        if uid >= 2:
                            ph.wait("tensor", sV, ev_by_unit[uid - 2])
                        tmm = None
                        for ib4 in range(16):
                            fb = fsb[ib4 % 2]
                            ph.wait("gpsimd", sPE, fs_pe.get(nchunk - 2))
                            tf = ph.dma(
                                "gpsimd",
                                fb[:].rearrange("p (b c) -> p b c", c=512),
                                fuse[ib4 * 4:(ib4 + 1) * 4, :,
                                     g4 * 512:(g4 + 1) * 512].rearrange(
                                         "b p c -> p b c"), sFs[ib4 % 2])
                            ph.wait("tensor", sFs[ib4 % 2], tf)
                            for bl in range(4):
                                ib = ib4 * 4 + bl
                                for tl in range(4):
                                    tmm = ph.run(
                                        "tensor",
                                        lambda t, fb=fb, bl=bl, ib=ib, tl=tl,
                                        pset=pset:
                                        t.matmul(
                                            pset[tl][:],
                                            fb[:, bl * 512 + tl * 128:
                                               bl * 512 + tl * 128 + 128],
                                            wd_sb[:, ib * 512:(ib + 1) * 512],
                                            start=(ib == 0), stop=(ib == 63)),
                                        sPE if (bl == 3 and tl == 3) else None)
                            fs_pe[nchunk] = tmm
                            nchunk += 1
                        wd_pe[oc] = tmm
                        lastev = None
                        for tl in range(4):
                            tt = g4 * 4 + tl
                            u = len(st_t)
                            tx = ph.dma("sync", xsb[u % 2][:],
                                        xres[tt][:, oc * 512:(oc + 1) * 512],
                                        sXl2[u % 2])
                            ph.wait("vector", sPE, tmm)
                            ph.run("vector", lambda v, pset=pset, tl=tl, oc=oc:
                                   v.tensor_mul(tmp[:], pset[tl][:],
                                                sdbc[:, oc * 512:(oc + 1) * 512]))
                            ph.wait("vector", sXl2[u % 2], tx)
                            if u >= 2:
                                ph.wait("vector", sYs[u % 2], st_t[u - 2])
                            ev = ph.run("vector", lambda v, u=u:
                                        v.tensor_add(ynw[u % 2][:], tmp[:],
                                                     xsb[u % 2][:]), sV)
                            lastev = ev
                            ph.wait("sync", sV, ev)
                            st = ph.dma("sync", dst_of(tt, oc), ynw[u % 2][:],
                                        sYs[u % 2])
                            st_t.append(st)
                        ev_by_unit[uid] = lastev
                ph.wait("sync", sYs[0], sYs[0].n)
                ph.wait("sync", sYs[1], sYs[1].n)
            ph.emit()

        if "F1a" in phase_set:
            f1_phase("F1a")
        if "F2a" in phase_set:
            f2_phase("F2a", lambda tt, oc: xres[tt][:, oc * 512:(oc + 1) * 512])
        if "LN3" in phase_set:
            ln_phase("LN3", lambda tt: xres[tt])

        def btier(tier):
            nh = NHS[tier]
            dim = nh * 64
            ndt = IN_OFF[tier + 1] // 128
            o0 = OUT_OFF[tier]
            rbase = o0 // 128
            has_fb = tier < 2
            w = nh * 8
            with ExitStack() as ts:
                TE = ts.enter_context
                vts = TE(nc.sbuf_tensor(f"B{tier}_v", [128, NT * dim], BF16))
                h0 = o0 // 64
                if has_fb:
                    cT = TE(nc.sbuf_tensor(f"B{tier}_cT", [64, 2 * T], BF16))
                    ph = Phase(nc, f"B{tier}c")
                    with ExitStack() as ps:
                        EE = ps.enter_context
                        pck = [EE(nc.psum_tensor(f"B{tier}c_k{i}", [128, w], F32))
                               for i in range(2)]
                        pcv = [EE(nc.psum_tensor(f"B{tier}c_v{i}", [128, w], F32))
                               for i in range(2)]
                        tpc = [EE(nc.psum_tensor(f"B{tier}c_t{i}", [128, 128], BF16))
                               for i in range(2)]
                        csb = [EE(nc.sbuf_tensor(f"B{tier}c_s{i}", [128, 2 * w],
                                                 BF16)) for i in range(2)]
                        nhi = 14 if tier == 0 else 8
                        j0 = 2 if tier == 0 else 8
                        fkc, fvc = fk_sb[tier], fv_sb[tier]
                        ev_t, tr_ev, c_tr_last = [], [], []
                        for tt in range(NT):
                            pk_, pv_ = pck[tt % 2], pcv[tt % 2]
                            sb = csb[tt % 2]
                            if tt >= 2:
                                ph.wait("tensor", sV, ev_t[tt - 2])
                                ph.wait("vector", sPE, c_tr_last[tt - 2])
                            tpe = None
                            for jj in range(nhi):
                                j = j0 + jj
                                ph.run("tensor", lambda t, j=j, jj=jj, tt=tt,
                                       pk_=pk_:
                                       t.matmul(pk_[:],
                                                xnT[:, j * T + tt * 128:
                                                    j * T + tt * 128 + 128],
                                                fkc[:, jj * w:(jj + 1) * w],
                                                start=(jj == 0),
                                                stop=(jj == nhi - 1)))
                            for jj in range(nhi):
                                j = j0 + jj
                                tpe = ph.run("tensor", lambda t, j=j, jj=jj, tt=tt,
                                             pv_=pv_:
                                             t.matmul(pv_[:],
                                                      xnT[:, j * T + tt * 128:
                                                          j * T + tt * 128 + 128],
                                                      fvc[:, jj * w:(jj + 1) * w],
                                                      start=(jj == 0),
                                                      stop=(jj == nhi - 1)),
                                             sPE if jj == nhi - 1 else None)
                            ph.wait("vector", sPE, tpe)
                            ph.run("vector", lambda v, pk_=pk_, sb=sb:
                                   v.tensor_copy(sb[:, 0:w], pk_[:]))
                            ev = ph.run("vector", lambda v, pv_=pv_, sb=sb:
                                        v.tensor_copy(sb[:, w:2 * w], pv_[:]), sV)
                            ev_t.append(ev)
                            ph.wait("tensor", sV, ev)
                            tpt = None
                            for kv in range(2):
                                pt = tpc[kv]
                                if len(tr_ev) >= 2:
                                    ph.wait("tensor", sV, tr_ev[-2])
                                src = sb[:, kv * w:(kv + 1) * w]
                                tpt = ph.run("tensor",
                                             lambda t, src=src, pt=pt:
                                             t.transpose(pt[0:w, :], src,
                                                         ident[:]), sPE)
                                ph.wait("vector", sPE, tpt)
                                dstc = cT[0:w, kv * T + tt * 128:
                                          kv * T + tt * 128 + 128]
                                ev2 = ph.run("vector",
                                             lambda v, pt=pt, dstc=dstc:
                                             v.tensor_copy(dstc, pt[0:w, :]), sV)
                                tr_ev.append(ev2)
                            c_tr_last.append(tpt)
                    ph.emit()
                if "Bsub=c" in debug:
                    return

                ph = Phase(nc, f"B{tier}qkv")
                with ExitStack() as ps:
                    EE = ps.enter_context
                    wcol = EE(nc.sbuf_tensor(f"B{tier}_wc", [128, ndt * 64], BF16))
                    wvc = EE(nc.sbuf_tensor(
                        f"B{tier}_wvc",
                        [128, ndt * (512 if tier != 2 else 256)], BF16))
                    pps = [EE(nc.psum_tensor(f"B{tier}q_p{i}", [128, 512], F32))
                           for i in range(2)]
                    ppv = [EE(nc.psum_tensor(f"B{tier}v_p{i}", [128, 512], F32))
                           for i in range(2)]
                    qstg = [EE(nc.sbuf_tensor(f"B{tier}_qs{i}", [64, 512], BF16))
                            for i in range(4)]
                    sWc = SP[0]
                    sWv = SP[1]
                    sQs = [SP[4], SP[5], SP[6], SP[7]]
                    ev_t = []
                    st_t = []
                    wld = []

                    def head_units(w_dram, hh, dst_dram, with_fb):
                        col0 = o0 + hh * 64
                        if wld:
                            ph.wait("gpsimd", sPE, wld[-1])
                        twl = ph.dma("gpsimd",
                                     wcol[:, 0:ndt * 64].rearrange(
                                         "p (j c) -> p j c", c=64),
                                     w_dram[0:ndt * 128,
                                            col0:col0 + 64].rearrange(
                                         "(j p) c -> p j c", p=128), sWc)
                        ph.wait("tensor", sWc, twl)
                        tpe = None
                        for tc in range(4):
                            u = len(ev_t)
                            pb = pps[u % 2]
                            if u >= 2:
                                ph.wait("tensor", sA, ev_t[u - 2])
                            for j in range(ndt):
                                last = (j == ndt - 1 and not with_fb)
                                tpe = ph.run(
                                    "tensor", lambda t, j=j, tc=tc, pb=pb,
                                    last=last:
                                    t.matmul(pb[0:64, :],
                                             wcol[:, j * 64:(j + 1) * 64],
                                             xnT[:, j * T + tc * 512:
                                                 j * T + tc * 512 + 512],
                                             start=(j == 0), stop=last),
                                    sPE if last else None)
                            if with_fb:
                                tpe = ph.run(
                                    "tensor", lambda t, hh=hh, tc=tc, pb=pb:
                                    t.matmul(pb[0:64, :],
                                             pkbd[tier][0:w,
                                                        hh * 64:hh * 64 + 64],
                                             cT[0:w, tc * 512:tc * 512 + 512],
                                             start=False, stop=True), sPE)
                            ph.wait("vector", sPE, tpe)
                            u2 = len(st_t)
                            sb2 = qstg[u2 % 4]
                            if u2 >= 4:
                                ph.wait("vector", sQs[u2 % 4], st_t[u2 - 4])
                            if "Bsub=noev" not in debug:
                                ph.wait("scalar", sPE, tpe)
                                ev = ph.run("scalar", lambda s, pb=pb, sb2=sb2:
                                            s.copy(sb2[:], pb[0:64, :]), sA)
                                ev_t.append(ev)
                                ph.wait("sync", sA, ev)
                            if "Bsub=noev" in debug or "Bsub=nost" in debug:
                                st_t.append(sSt.n)
                            else:
                                st = ph.dma(
                                    "sync",
                                    dst_dram[h0 + hh][:,
                                                      tc * 512:tc * 512 + 512],
                                    sb2[:], sQs[u2 % 4])
                                st_t.append(st)
                        wld.append(tpe)

                    nqh = nh if "Bsub=q1" not in debug else 1
                    for hh in range(nqh):
                        head_units(WQd, hh, QTb, False)
                        if "Bsub=qonly" not in debug and "Bsub=q1" not in debug:
                            head_units(WKd, hh, KTb, has_fb)
                    chunks = {0: [(0, 256)], 1: [(0, 512)],
                              2: [(0, 256), (256, 256), (512, 256)]}[tier]
                    if any(x in debug for x in ("Bsub=qonly", "Bsub=q1", "Bsub=noV")):
                        chunks = []
                    for (c0, wch) in chunks:
                        if wld:
                            ph.wait("gpsimd", sPE, wld[-1])
                        twl = ph.dma("gpsimd",
                                     wvc[:, 0:ndt * wch].rearrange(
                                         "p (j c) -> p j c", c=wch),
                                     WVd[0:ndt * 128,
                                         o0 + c0:o0 + c0 + wch].rearrange(
                                         "(j p) c -> p j c", p=128), sWv)
                        ph.wait("tensor", sWv, twl)
                        tpe = None
                        for tt in range(NT):
                            u = len(ev_t)
                            pb = ppv[u % 2]
                            if u >= 2:
                                ph.wait("tensor", sV, ev_t[u - 2])
                            for j in range(ndt):
                                last = (j == ndt - 1 and not has_fb)
                                tpe = ph.run(
                                    "tensor", lambda t, j=j, tt=tt, pb=pb,
                                    wch=wch, last=last:
                                    t.matmul(pb[:, 0:wch],
                                             xnT[:, j * T + tt * 128:
                                                 j * T + tt * 128 + 128],
                                             wvc[:, j * wch:(j + 1) * wch],
                                             start=(j == 0), stop=last),
                                    sPE if last else None)
                            if has_fb:
                                tpe = ph.run(
                                    "tensor", lambda t, tt=tt, pb=pb, c0=c0,
                                    wch=wch:
                                    t.matmul(pb[:, 0:wch],
                                             cT[0:w, T + tt * 128:
                                                T + tt * 128 + 128],
                                             pvbd[tier][0:w, c0:c0 + wch],
                                             start=False, stop=True), sPE)
                            ph.wait("vector", sPE, tpe)
                            ev = ph.run("vector", lambda v, pb=pb, tt=tt, c0=c0,
                                        wch=wch:
                                        v.tensor_copy(
                                            vts[:, tt * dim + c0:
                                                tt * dim + c0 + wch],
                                            pb[:, 0:wch]), sV)
                            ev_t.append(ev)
                        wld.append(tpe)
                    for ss in sQs:
                        ph.wait("sync", ss, ss.n)
                ph.emit()
                if "Bsub=qkv" in debug:
                    return

                ph = Phase(nc, f"B{tier}att")
                with ExitStack() as ps:
                    EE = ps.enter_context
                    psb = EE(nc.sbuf_tensor(f"B{tier}_P", [128, 4 * 512], BF16))
                    ptb = EE(nc.sbuf_tensor(f"B{tier}_PT", [128, 2 * 512], BF16))
                    zz = EE(nc.sbuf_tensor(f"B{tier}_z", [128, 12], F32))
                    obh = [EE(nc.sbuf_tensor(f"B{tier}_oh{i}", [128, NT * 64],
                                             BF16)) for i in range(2)]
                    obsT = EE(nc.sbuf_tensor(f"B{tier}_obT", [64, T], BF16))
                    qkr = EE(nc.sbuf_tensor(f"B{tier}_qk", [64, 2 * 2 * T], BF16))
                    scp = [EE(nc.psum_tensor(f"B{tier}_sc{i}", [128, 512], F32))
                           for i in range(4)]
                    tp = [EE(nc.psum_tensor(f"B{tier}_tp{i}", [128, 512], BF16))
                          for i in range(2)]
                    otp = [EE(nc.psum_tensor(f"B{tier}_ot{i}", [128, 64], F32))
                           for i in range(2)]
                    un = 0
                    gchunk = 0
                    chunk_exp = []
                    chunk_tpt = {}
                    zred_t = []
                    otev_t = []
                    tp_last = {}
                    st_t = []
                    hd_tr = []
                    h_ring_last = []
                    sQk = [SP[0], SP[1]]
                    sOb = SP[2]
                    last_tsc = None
                    for hh in range(nh):
                        slot = (hh % 2) * 2 * T
                        if hh >= 2:
                            ph.wait("gpsimd", sPE, h_ring_last[hh - 2])
                        ph.dma("gpsimd", qkr[:, slot:slot + T],
                               QTb[h0 + hh], sQk[hh % 2])
                        tldq = ph.dma("gpsimd", qkr[:, slot + T:slot + 2 * T],
                                      KTb[h0 + hh], sQk[hh % 2])
                        ph.wait("tensor", sQk[hh % 2], tldq)
                        ob = obh[hh % 2]
                        if hh >= 2:
                            ph.wait("scalar", sPE, hd_tr[hh - 2])
                        for i in range(NT):
                            nk = i + 1
                            nkc = (nk * 128 + 511) // 512
                            po = otp[un % 2]
                            if un >= 2:
                                ph.wait("tensor", sA, otev_t[un - 2])
                                ph.wait("scalar", sV, zred_t[un - 2])
                            sc_t = []
                            for kc in range(nkc):
                                wv_ = min(512, nk * 128 - kc * 512)
                                pb = scp[gchunk % 4]
                                if gchunk >= 4:
                                    ph.wait("tensor", sA, chunk_exp[gchunk - 4])
                                tsc = ph.run(
                                    "tensor",
                                    lambda t, kc=kc, wv_=wv_, slot=slot, i=i,
                                    pb=pb:
                                    t.matmul(
                                        pb[:, 0:wv_],
                                        qkr[0:64, slot + i * 128:
                                            slot + i * 128 + 128],
                                        qkr[0:64, slot + T + kc * 512:
                                            slot + T + kc * 512 + wv_],
                                        start=True, stop=True), sPE)
                                last_tsc = tsc
                                diag = (i * 128) // 512 == kc
                                tm = None
                                if diag:
                                    ph.wait("vector", sPE, tsc)
                                    off = (i * 128) % 512
                                    tm = ph.run(
                                        "vector", lambda v, pb=pb, off=off:
                                        v.tensor_tensor(
                                            pb[:, off:off + 128],
                                            pb[:, off:off + 128],
                                            maskw[:, 256:384], op=ALU.add), sV)
                                    ph.wait("scalar", sV, tm)
                                else:
                                    ph.wait("scalar", sPE, tsc)
                                if gchunk >= 4:
                                    ph.wait("scalar", sPE,
                                            chunk_tpt[gchunk - 4])
                                pcur = psb[:, (gchunk % 4) * 512:
                                           (gchunk % 4) * 512 + wv_]
                                zc = zz[:, (un % 2) * 4 + kc:
                                        (un % 2) * 4 + kc + 1]
                                te = ph.run(
                                    "scalar", lambda s, pb=pb, wv_=wv_, zc=zc,
                                    pcur=pcur:
                                    s.activation(
                                        pcur, pb[:, 0:wv_], AF.Exp,
                                        bias=cmisc[:, C_NB32:C_NB32 + 1],
                                        scale=cmisc[:, C_S64:C_S64 + 1],
                                        accum_out=zc), sA)
                                chunk_exp.append(te)
                                sc_t.append((gchunk, te, wv_))
                                gchunk += 1
                            for (gc, te, wv_) in sc_t:
                                nbl = (wv_ + 127) // 128
                                ph.wait("tensor", sA, te)
                                pt = tp[gc % 2]
                                gg = tp_last.get(gc % 2)
                                if gg:
                                    ph.wait("tensor", gg[1], gg[0])
                                tpt = None
                                for bi in range(nbl):
                                    tpt = ph.run(
                                        "tensor",
                                        lambda t, pt=pt, gc=gc, bi=bi:
                                        t.transpose(
                                            pt[:, bi * 128:(bi + 1) * 128],
                                            psb[:, (gc % 4) * 512 + bi * 128:
                                                (gc % 4) * 512 + bi * 128 + 128],
                                            ident[:]),
                                        sPE if bi == nbl - 1 else None)
                                chunk_tpt[gc] = tpt
                                eng, sem = (("vector", sV)
                                            if gc % 2 == 0
                                            else ("scalar", sA))
                                ph.wait(eng, sPE, tpt)
                                dst = ptb[:, (gc % 2) * 512:
                                          (gc % 2) * 512 + nbl * 128]
                                if eng == "vector":
                                    ev = ph.run("vector",
                                                lambda v, pt=pt, dst=dst, nbl=nbl:
                                                v.tensor_copy(
                                                    dst, pt[:, 0:nbl * 128]), sem)
                                else:
                                    ev = ph.run("scalar",
                                                lambda s, pt=pt, dst=dst, nbl=nbl:
                                                s.copy(dst, pt[:, 0:nbl * 128]),
                                                sem)
                                tp_last[gc % 2] = (ev, sem)
                                ph.wait("tensor", sem, ev)
                                tav = None
                                for bi in range(nbl):
                                    kt = (gc - sc_t[0][0]) * 4 + bi
                                    last = (gc == sc_t[-1][0] and bi == nbl - 1)
                                    tav = ph.run(
                                        "tensor",
                                        lambda t, kt=kt, hh=hh, po=po, gc=gc,
                                        bi=bi, first=(gc == sc_t[0][0]
                                                      and bi == 0), last=last:
                                        t.matmul(
                                            po[:],
                                            ptb[:, (gc % 2) * 512 + bi * 128:
                                                (gc % 2) * 512 + bi * 128 + 128],
                                            vts[:, kt * dim + hh * 64:
                                                kt * dim + hh * 64 + 64],
                                            start=first, stop=last),
                                        sPE if (last or bi == nbl - 1) else None)
                            Zc = 8 + un % 2
                            RZc = 10 + un % 2
                            ph.wait("vector", sA, chunk_exp[-1])
                            ph.run("vector", lambda v, nkc=nkc, un=un, Zc=Zc:
                                   v.reduce_sum(
                                       zz[:, Zc:Zc + 1],
                                       zz[:, (un % 2) * 4:(un % 2) * 4 + nkc],
                                       axis=AX.X))
                            trc = ph.run("vector", lambda v, Zc=Zc, RZc=RZc:
                                         v.reciprocal(
                                             zz[:, RZc:RZc + 1],
                                             zz[:, Zc:Zc + 1]), sV)
                            zred_t.append(trc)
                            ph.wait("scalar", sV, trc)
                            ph.wait("scalar", sPE, tav)
                            dst = ob[:, i * 64:(i + 1) * 64]
                            tev = ph.run(
                                "scalar", lambda s, po=po, dst=dst, RZc=RZc:
                                s.activation(dst, po[:], AF.Copy,
                                             scale=zz[:, RZc:RZc + 1]), sA)
                            otev_t.append(tev)
                            un += 1
                        h_ring_last.append(last_tsc)
                        ph.wait("tensor", sA, otev_t[-1])
                        if st_t:
                            pass
                        tails = []
                        for tg in range(0, NT, 4):
                            pt = tp[(tg // 4) % 2]
                            gg = tp_last.get((tg // 4) % 2)
                            if gg:
                                ph.wait("tensor", gg[1], gg[0])
                            tpt = None
                            for tt2 in range(tg, tg + 4):
                                tpt = ph.run(
                                    "tensor", lambda t, pt=pt, ob=ob, tt2=tt2,
                                    tg=tg:
                                    t.transpose(
                                        pt[0:64, (tt2 - tg) * 128:
                                           (tt2 - tg) * 128 + 128],
                                        ob[:, tt2 * 64:(tt2 + 1) * 64],
                                        ident[:]), sPE if tt2 == tg + 3 else None)
                            if tg == 0 and st_t:
                                ph.wait("vector", sOb, st_t[-1])
                            ph.wait("vector", sPE, tpt)
                            ev = ph.run(
                                "vector", lambda v, pt=pt, tg=tg:
                                v.tensor_copy(
                                    obsT[0:64, tg * 128:(tg + 4) * 128],
                                    pt[0:64, 0:512]), sV)
                            tp_last[(tg // 4) % 2] = (ev, sV)
                            tails.append(ev)
                        hd_tr.append(tpt)
                        ph.wait("sync", sV, tails[-1])
                        st = ph.dma("sync",
                                    ObT[rbase + hh // 2][(hh % 2) * 64:
                                                         (hh % 2) * 64 + 64, :],
                                    obsT[0:64, :], sOb)
                        st_t.append(st)
                    ph.wait("sync", sOb, sOb.n)
                ph.emit()

        for tier, pn in ((0, "B0"), (1, "B1"), (2, "B2")):
            if pn in phase_set:
                btier(tier)

        if "BWO" in phase_set:
            ph = Phase(nc, "BWO")
            with ExitStack() as ps:
                EE = ps.enter_context
                oball = EE(nc.sbuf_tensor("BW_ob", [128, 12 * T], BF16))
                wos2 = EE(nc.sbuf_tensor("BW_w", [128, 2 * 12 * 512], BF16))
                xsb = [EE(nc.sbuf_tensor(f"BW_x{i}", [128, 512], F32))
                       for i in range(2)]
                ynw = [EE(nc.sbuf_tensor(f"BW_y{i}", [128, 512], F32))
                       for i in range(2)]
                psy = [EE(nc.psum_tensor(f"BW_p{i}", [128, 512], F32))
                       for i in range(2)]
                sObl = SP[0]
                sWo2 = [SP[1], SP[2]]
                sXl2 = [SP[4], SP[5]]
                sYs = [SP[6], SP[7]]
                tob = 0
                for rt in range(12):
                    tob = ph.dma("gpsimd", oball[:, rt * T:(rt + 1) * T],
                                 ObT[rt], sObl)
                ph.wait("tensor", sObl, tob)
                nrc_by_oc = [6, 6, 12, 12]
                ev_t, st_t, wo_pe = [], [], []
                for oc in range(4):
                    nrc = nrc_by_oc[oc]
                    half = (oc % 2) * 12 * 512
                    if oc >= 2:
                        ph.wait("gpsimd", sPE, wo_pe[oc - 2])
                    twl = ph.dma("gpsimd",
                                 wos2[:, half:half + nrc * 512].rearrange(
                                     "p (j c) -> p j c", c=512),
                                 WOd[0:nrc * 128, oc * 512:(oc + 1) * 512]
                                 .rearrange("(j p) c -> p j c", p=128),
                                 sWo2[oc % 2])
                    ph.wait("tensor", sWo2[oc % 2], twl)
                    tpe = None
                    for tt in range(NT):
                        u = len(ev_t)
                        pb = psy[u % 2]
                        if u >= 2:
                            ph.wait("tensor", sV, ev_t[u - 2])
                        for r in range(nrc):
                            tpe = ph.run(
                                "tensor", lambda t, r=r, tt=tt, pb=pb, half=half:
                                t.matmul(pb[:],
                                         oball[:, r * T + tt * 128:
                                               r * T + tt * 128 + 128],
                                         wos2[:, half + r * 512:
                                              half + r * 512 + 512],
                                         start=(r == 0), stop=(r == nrc - 1)),
                                sPE if r == nrc - 1 else None)
                        tx = ph.dma("sync", xsb[u % 2][:],
                                    xres[tt][:, oc * 512:(oc + 1) * 512],
                                    sXl2[u % 2])
                        ph.wait("vector", sPE, tpe)
                        ph.wait("vector", sXl2[u % 2], tx)
                        if u >= 2:
                            ph.wait("vector", sYs[u % 2], st_t[u - 2])
                        ev = ph.run("vector", lambda v, pb=pb, u=u:
                                    v.tensor_add(ynw[u % 2][:], pb[:],
                                                 xsb[u % 2][:]), sV)
                        ev_t.append(ev)
                        ph.wait("sync", sV, ev)
                        st = ph.dma("sync", xres[tt][:, oc * 512:(oc + 1) * 512],
                                    ynw[u % 2][:], sYs[u % 2])
                        st_t.append(st)
                    wo_pe.append(tpe)
                ph.wait("sync", sYs[0], sYs[0].n)
                ph.wait("sync", sYs[1], sYs[1].n)
            ph.emit()

        if "LN4" in phase_set:
            ln_phase("LN4", lambda tt: xres[tt])
        if "F1b" in phase_set:
            f1_phase("F1b")
        if "F2b" in phase_set:
            f2_phase("F2b", lambda tt, oc: out_d[tt * 128:(tt + 1) * 128,
                                                 oc * 512:(oc + 1) * 512])

        dnames = [n for n in debug if ":" not in n and n in scr]
        if dnames:
            ph = Phase(nc, "DBG")
            td = 0
            for name in dnames:
                s = scr[name]
                o = nc.dram_tensor("dbg_" + name, list(s.shape), s.dtype,
                                   kind="ExternalOutput").ap()
                dbg_outs[name] = o
                td = ph.dma("sync", o, s, sSt)
            ph.wait("sync", sSt, td)
            ph.emit()

    import os
    if os.environ.get("KDBG_SEMS"):
        names = ["sL", "sG", "sSt", "sPE", "sV", "sA"]
        for i, sm in enumerate(_dbg_sems):
            print(f"SEMTOTAL {names[i] if i < 6 else i}: {sm.n}")
    return nc, dbg_outs


def _blockdiag(P):
    P = np.asarray(P, np.float32)
    nh = P.shape[0]
    out = np.zeros((nh * 8, nh * 64), np.float32)
    for h in range(nh):
        out[h * 8:(h + 1) * 8, h * 64:(h + 1) * 64] = P[h]
    return out


def _prep_core_inputs(b, I):
    f32 = np.float32
    mask0 = np.where(np.arange(128)[None, :] > np.arange(128)[:, None],
                     0.0, NEG).astype(f32)
    mask2 = np.where(np.arange(128)[None, :] <= np.arange(128)[:, None],
                     0.0, NEG).astype(f32)
    maskw = np.concatenate([mask0, np.zeros((128, 128), f32), mask2], axis=1)
    d = {
        "x": I["L0_attn_a_hidden_states"][b],
        "wqT": I["L0_attn_a_wq"].T,
        "wkT": I["L0_attn_a_wk"].T,
        "wvT": I["L0_attn_a_wv"].T,
        "woT": I["L0_attn_a_wo"].T,
        "wo_nat": I["L0_attn_a_wo"],
        "gwT_rs": I["L0_attn_a_gate_w"].reshape(16, 128).T,
        "gb_bc": np.tile(np.asarray(I["L0_attn_a_gate_b"]).reshape(1, 1),
                         (128, 1)).astype(f32),
        "membT": I["L0_attn_a_mem_bank"].T,
        "WQ": I["L0_attn_b_W_Q"], "WK": I["L0_attn_b_W_K"],
        "WV": I["L0_attn_b_W_V"], "WO": I["L0_attn_b_W_O"],
        "FK0": I["L0_attn_b_FK0"], "FV0": I["L0_attn_b_FV0"],
        "FK1": I["L0_attn_b_FK1"], "FV1": I["L0_attn_b_FV1"],
        "PK0bd": _blockdiag(I["L0_attn_b_PK0"]),
        "PV0bd": _blockdiag(I["L0_attn_b_PV0"]),
        "PK1bd": _blockdiag(I["L0_attn_b_PK1"]),
        "PV1bd": _blockdiag(I["L0_attn_b_PV1"]),
        "wgR": np.asarray(I["L0_ffn_w_gate"]).reshape(
            16, 128, 16, 512).transpose(2, 0, 1, 3),
        "wuR": np.asarray(I["L0_ffn_w_up"]).reshape(
            16, 128, 16, 512).transpose(2, 0, 1, 3),
        "wd": I["L0_ffn_w_down"],
        "sg_rs": np.asarray(I["L0_ffn_s_gate"]).reshape(64, 128).T,
        "su_rs": np.asarray(I["L0_ffn_s_up"]).reshape(64, 128).T,
        "sdbc": np.tile(np.asarray(I["L0_ffn_s_down"]).reshape(1, D), (128, 1)),
        "ident": np.eye(128, dtype=ml_dtypes.bfloat16),
        "maskw": maskw,
    }
    return {k: np.ascontiguousarray(v) for k, v in d.items()}


def kernel(**inputs):
    nc, dbg = build_nc()
    in_maps = [_prep_core_inputs(b, inputs) for b in range(8)]
    res = run_bass_kernel_spmd(nc, in_maps, core_ids=list(range(8)))
    return np.stack([np.asarray(res.results[b]["out"]) for b in range(8)], axis=0)


# version 11 (final)
# speedup vs baseline: 1.0406x; 1.0115x over previous
import numpy as np
import ml_dtypes
from contextlib import ExitStack

import concourse.bass as bass
import concourse.mybir as mybir
from concourse.bass_utils import run_bass_kernel_spmd

F32 = mybir.dt.float32
BF16 = mybir.dt.bfloat16
I8 = mybir.dt.int8
AF = mybir.ActivationFunctionType
ALU = mybir.AluOpType
AX = mybir.AxisListType

T = 2048
D = 2048
NT = T // 128
ND = D // 128
M = 4096
INTER = 8192
NEG = -1e30
SB = -32.0

TIER_OWN = [256, 768, 1024]
HEAD_OUT = [256, 512, 768]
NHS = [4, 8, 12]
IN_OFF = [0, 256, 1024, 2048]
OUT_OFF = [0, 256, 768, 1536]

ENGINES = ("sync", "gpsimd", "tensor", "vector", "scalar")
import os
SYNC_DMA = bool(os.environ.get("KDBG_SYNCDMA"))

ALL_PHASES = ["P0", "LN1", "KMT", "VMT", "SIM", "OMEM", "QT", "CTX", "CMB",
              "LN2", "F1a", "F2a", "LN3", "B0", "B1", "B2", "BWO", "LN4",
              "F1b", "F2b"]


class Sem:
    def __init__(self, h):
        self.h = h
        self.n = 0


class Phase:
    def __init__(self, nc, name):
        self.nc = nc
        self.name = name
        self.q = {e: [] for e in ENGINES}

    def on(self, eng, fn):
        self.q[eng].append(fn)

    def emit(self):
        with self.nc.Block(self.name) as blk:
            for name in ENGINES:
                fns = self.q[name]
                if not fns:
                    continue
                def body(eng, fns=fns):
                    for f in fns:
                        f(eng)
                getattr(blk, name)(body)

    def dma(self, eng, dst, src, sem, **kw):
        sem.n += 16
        tgt = sem.n
        self.on(eng, lambda e, dst=dst, src=src, kw=kw:
                e.dma_start(dst, src, **kw).then_inc(sem.h, 16))
        if SYNC_DMA:
            self.on(eng, lambda e, s=sem.h, t=tgt: e.wait_ge(s, t))
        return tgt

    def wait(self, eng, sem, tgt):
        if tgt is None:
            return
        self.on(eng, lambda e, s=sem.h, t=tgt: e.wait_ge(s, t))

    def run(self, eng, fn, sem=None, k=1):
        if sem is None:
            self.on(eng, fn)
            return None
        sem.n += k
        tgt = sem.n
        self.on(eng, lambda e, fn=fn, s=sem.h, k=k: fn(e).then_inc(s, k))
        return tgt


def build_nc(debug=(), upto=None):
    nc = bass.Bass("TRN2", target_bir_lowering=False)
    phase_set = ALL_PHASES[:ALL_PHASES.index(upto) + 1] if upto else ALL_PHASES

    def inp(name, shape, dt):
        return nc.dram_tensor(name, list(shape), dt, kind="ExternalInput").ap()

    x_in = inp("x", (T, D), F32)
    wqT = inp("wqT", (D, D), F32)
    wkT = inp("wkT", (D, D), F32)
    wvT = inp("wvT", (D, D), F32)
    woT = inp("woT", (D, D), F32)
    wo_nat = inp("wo_nat", (D, D), F32)
    gwT_rs = inp("gwT_rs", (128, 16), F32)
    gb_bc = inp("gb_bc", (128, 1), F32)
    membT = inp("membT", (D, M), F32)
    WQd = inp("WQ", (D, 1536), F32)
    WKd = inp("WK", (D, 1536), F32)
    WVd = inp("WV", (D, 1536), F32)
    WOd = inp("WO", (1536, D), F32)
    FKd = {0: inp("FK0", (1792, 32), F32), 1: inp("FK1", (1024, 64), F32)}
    FVd = {0: inp("FV0", (1792, 32), F32), 1: inp("FV1", (1024, 64), F32)}
    PKbd = {0: inp("PK0bd", (32, 256), F32), 1: inp("PK1bd", (64, 512), F32)}
    PVbd = {0: inp("PV0bd", (32, 256), F32), 1: inp("PV1bd", (64, 512), F32)}
    wgR = inp("wgR", (16, 16, 128, 512), I8)
    wuR = inp("wuR", (16, 16, 128, 512), I8)
    wd_i8 = inp("wd", (INTER, D), I8)
    sg_rs = inp("sg_rs", (128, 64), F32)
    su_rs = inp("su_rs", (128, 64), F32)
    sdbc_d = inp("sdbc", (128, D), F32)
    ident_d = inp("ident", (128, 128), BF16)
    maskw_d = inp("maskw", (128, 384), F32)

    out_d = nc.dram_tensor("out", [T, D], F32, kind="ExternalOutput").ap()

    scr = {}
    def scratch(name, shape, dt):
        scr[name] = nc.dram_tensor("scr_" + name, list(shape), dt,
                                   kind="Internal").ap()
        return scr[name]

    xres = scratch("xres", (NT, 128, D), F32)
    KT = scratch("KT", (NT, ND, 128, 128), BF16)
    QT = scratch("QT", (NT, ND, 128, 128), BF16)
    Vh = scratch("Vh", (NT, 128, D), BF16)
    kmemT = scratch("kmemT", (8, ND, 128, 512), BF16)
    vmem2 = scratch("vmem2", (4, 32, 128, 512), BF16)
    simd = scratch("simd", (NT, 128, M), F32)
    omemp = scratch("omemp", (NT, 128, D), BF16)
    ctxT = scratch("ctxT", (NT, ND, 128, 128), BF16)
    fuse = scratch("fuse", (64, 128, T), BF16)
    ObT = scratch("ObT", (12, 128, T), BF16)
    QTb = scratch("QTb", (24, 64, T), BF16)
    KTb = scratch("KTb", (24, 64, T), BF16)
    xnloT = scratch("xnloT", (NT, ND, 128, 128), BF16)
    wklo = scratch("wklo", (ND, ND, 128, 128), BF16)
    kmemTlo = scratch("kmemTlo", (8, ND, 128, 512), BF16)

    dbg_outs = {}
    dbg_sbuf = []

    with ExitStack() as stk:
        E = stk.enter_context

        xnT = E(nc.sbuf_tensor("xnT", [128, ND * T], BF16))
        ident = E(nc.sbuf_tensor("c_ident", [128, 128], BF16))
        maskw = E(nc.sbuf_tensor("c_maskw", [128, 384], F32))
        sgc = E(nc.sbuf_tensor("c_sg", [128, 64], F32))
        suc = E(nc.sbuf_tensor("c_su", [128, 64], F32))
        gbb = E(nc.sbuf_tensor("c_gb", [128, 1], F32))
        gcomb = E(nc.sbuf_tensor("c_gcomb", [128, 16], BF16))
        gwcol = E(nc.sbuf_tensor("c_gwcol", [128, 16], BF16))
        fk_sb = {0: E(nc.sbuf_tensor("c_fk0", [128, 14 * 32], BF16)),
                 1: E(nc.sbuf_tensor("c_fk1", [128, 8 * 64], BF16))}
        fv_sb = {0: E(nc.sbuf_tensor("c_fv0", [128, 14 * 32], BF16)),
                 1: E(nc.sbuf_tensor("c_fv1", [128, 8 * 64], BF16))}
        pkbd = {0: E(nc.sbuf_tensor("c_pk0", [32, 256], BF16)),
                1: E(nc.sbuf_tensor("c_pk1", [64, 512], BF16))}
        pvbd = {0: E(nc.sbuf_tensor("c_pv0", [32, 256], BF16)),
                1: E(nc.sbuf_tensor("c_pv1", [64, 512], BF16))}
        cmisc = E(nc.sbuf_tensor("c_misc", [128, 4], F32))
        C_NB32, C_S128, C_S64 = 0, 1, 2

        sems = [Sem(E(nc.semaphore(f"s{i}"))) for i in range(16)]
        sL, sG, sSt, sPE, sV, sA = sems[0:6]
        SP = sems[6:16]
        global _dbg_sems
        _dbg_sems = sems

        if "P0" in phase_set:
            ph = Phase(nc, "P0")
            with ExitStack() as ps:
                wo_sb = ps.enter_context(nc.sbuf_tensor("p0_wo", [128, 1024], BF16))
                gps = [ps.enter_context(
                    nc.psum_tensor(f"p0_ps{j}", [128, 1], F32)) for j in range(8)]
                ph.dma("sync", ident[:], ident_d, sL)
                ph.dma("sync", maskw[:], maskw_d, sL)
                ph.dma("sync", sgc[:], sg_rs, sL)
                ph.dma("sync", suc[:], su_rs, sL)
                tc0 = ph.dma("sync", gbb[:], gb_bc, sL)
                tgw = ph.dma("gpsimd", gwcol[:], gwT_rs, sG)
                ph.run("vector", lambda v: v.memset(
                    cmisc[:, C_NB32:C_NB32 + 1], -32.0))
                ph.run("vector", lambda v: v.memset(
                    cmisc[:, C_S128:C_S128 + 1], 1.0 / float(np.sqrt(128.0))))
                ph.run("vector", lambda v: v.memset(
                    cmisc[:, C_S64:C_S64 + 1], 0.125))
                for tier, nh, nj, w in ((0, 4, 14, 32), (1, 8, 8, 64)):
                    for j in range(nj):
                        ph.dma("gpsimd", fk_sb[tier][:, j * w:(j + 1) * w],
                               FKd[tier][j * 128:(j + 1) * 128, :], sG)
                        ph.dma("gpsimd", fv_sb[tier][:, j * w:(j + 1) * w],
                               FVd[tier][j * 128:(j + 1) * 128, :], sG)
                    ph.dma("gpsimd", pkbd[tier][:], PKbd[tier], sG)
                    ph.dma("gpsimd", pvbd[tier][:], PVbd[tier], sG)
                tload = sG.n
                last_grp = None
                for jg in range(2):
                    for c in range(16):
                        ph.wait("gpsimd", sPE, last_grp)
                        tw = ph.dma("gpsimd", wo_sb[:],
                                    wo_nat[c * 128:(c + 1) * 128,
                                           jg * 1024:jg * 1024 + 1024], sG)
                        ph.wait("tensor", sG, tw)
                        if jg == 0 and c == 0:
                            ph.wait("tensor", sG, tload)
                        for j8 in range(8):
                            last_grp = ph.run(
                                "tensor", lambda t, c=c, j8=j8: t.matmul(
                                    gps[j8][:], wo_sb[:, j8 * 128:(j8 + 1) * 128],
                                    gwcol[:, c:c + 1],
                                    start=(c == 0), stop=(c == 15)),
                                sPE if j8 == 7 else None)
                    ph.wait("vector", sPE, last_grp)
                    for j8 in range(8):
                        ph.run("vector", lambda v, jg=jg, j8=j8: v.tensor_copy(
                            gcomb[:, jg * 8 + j8:jg * 8 + j8 + 1], gps[j8][:]), sV)
                ph.wait("vector", sL, tc0)
                ph.wait("vector", sG, tload)
            ph.emit()

        def ln_phase(name, src_tile, emit_lo=False):
            ph = Phase(nc, name)
            with ExitStack() as ps:
                EE = ps.enter_context
                xsb = EE(nc.sbuf_tensor(name + "_x", [128, 2 * D], F32))
                scr_a = EE(nc.sbuf_tensor(name + "_scr", [128, D], F32))
                stats = EE(nc.sbuf_tensor(name + "_st", [128, 8], F32))
                xn_tm = EE(nc.sbuf_tensor(name + "_xn", [128, 2 * D], BF16))
                if emit_lo:
                    xf = EE(nc.sbuf_tensor(name + "_xf", [128, D], F32))
                    lob = EE(nc.sbuf_tensor(name + "_lo", [128, 2 * D], BF16))
                    lostg = [EE(nc.sbuf_tensor(name + f"_ls{i}",
                                               [128, ND * 128], BF16))
                             for i in range(2)]
                tp = [EE(nc.psum_tensor(name + f"_tp{i}", [128, 128], BF16))
                      for i in range(2)]
                sX = [SP[0], SP[1]]
                sLo = [SP[2], SP[3]]
                SUM, SSQ, MEAN, MSQ, VAR, SD, RSTD = range(7)
                act_done = []
                tr_last = []
                xn_t = []
                ev_t = []
                lo_st = []
                for tt in range(NT):
                    buf = xsb[:, (tt % 2) * D:(tt % 2) * D + D]
                    nbuf = xn_tm[:, (tt % 2) * D:(tt % 2) * D + D]
                    if tt >= 2:
                        ph.wait("gpsimd", sA, act_done[tt - 2])
                        ph.wait("gpsimd", sV, xn_t[tt - 2])
                    tx = ph.dma("gpsimd", buf, src_tile(tt), sX[tt % 2])
                    ph.wait("scalar", sX[tt % 2], tx)
                    ph.run("scalar", lambda s, b=buf: s.activation(
                        scr_a[:], b, AF.Copy, accum_out=stats[:, SUM:SUM + 1]))
                    tsq = ph.run("scalar", lambda s, b=buf: s.activation(
                        scr_a[:], b, AF.Square, accum_out=stats[:, SSQ:SSQ + 1]),
                        sA)
                    act_done.append(tsq)
                    ph.wait("vector", sA, tsq)
                    ph.run("vector", lambda v: v.tensor_scalar_mul(
                        stats[:, MEAN:MEAN + 1], stats[:, SUM:SUM + 1], 1.0 / D))
                    ph.run("vector", lambda v: v.tensor_mul(
                        stats[:, MSQ:MSQ + 1], stats[:, MEAN:MEAN + 1],
                        stats[:, MEAN:MEAN + 1]))
                    ph.run("vector", lambda v: v.tensor_scalar(
                        stats[:, VAR:VAR + 1], stats[:, SSQ:SSQ + 1], 1.0 / D,
                        None, op0=ALU.mult))
                    ph.run("vector", lambda v: v.tensor_sub(
                        stats[:, VAR:VAR + 1], stats[:, VAR:VAR + 1],
                        stats[:, MSQ:MSQ + 1]))
                    tv = ph.run("vector", lambda v: v.tensor_scalar_add(
                        stats[:, VAR:VAR + 1], stats[:, VAR:VAR + 1], 1e-5), sV)
                    ph.wait("scalar", sV, tv)
                    tsd = ph.run("scalar", lambda s: s.activation(
                        stats[:, SD:SD + 1], stats[:, VAR:VAR + 1], AF.Sqrt), sA)
                    ph.wait("vector", sA, tsd)
                    trc = ph.run("vector", lambda v: v.reciprocal(
                        stats[:, RSTD:RSTD + 1], stats[:, SD:SD + 1]), sV)
                    ph.wait("vector", sV, trc)
                    if tt >= 2:
                        ph.wait("vector", sPE, tr_last[tt - 2])
                    if not emit_lo:
                        txn = ph.run("vector", lambda v, b=buf, nb=nbuf:
                                     v.tensor_scalar(
                                         nb, b, stats[:, MEAN:MEAN + 1],
                                         stats[:, RSTD:RSTD + 1],
                                         op0=ALU.subtract, op1=ALU.mult), sV)
                    else:
                        lbuf = lob[:, (tt % 2) * D:(tt % 2) * D + D]
                        ph.run("vector", lambda v, b=buf: v.tensor_scalar(
                            xf[:], b, stats[:, MEAN:MEAN + 1],
                            stats[:, RSTD:RSTD + 1],
                            op0=ALU.subtract, op1=ALU.mult))
                        ph.run("vector", lambda v, nb=nbuf:
                               v.tensor_copy(nb, xf[:]))
                        txn = ph.run("vector", lambda v, nb=nbuf, lb=lbuf:
                                     v.tensor_sub(lb, xf[:], nb), sV)
                    xn_t.append(txn)
                    ph.wait("tensor", sV, txn)
                    tpt = None
                    for j in range(ND):
                        pb = tp[j % 2]
                        if len(ev_t) >= 2:
                            ph.wait("tensor", sV, ev_t[-2])
                        tpt = ph.run("tensor", lambda t, j=j, pb=pb, nb=nbuf:
                                     t.transpose(pb[:], nb[:, j * 128:(j + 1) * 128],
                                                 ident[:]), sPE)
                        ph.wait("vector", sPE, tpt)
                        dst = xnT[:, j * T + tt * 128: j * T + tt * 128 + 128]
                        ev = ph.run("vector", lambda v, pb=pb, dst=dst:
                                    v.tensor_copy(dst, pb[:]), sV)
                        ev_t.append(ev)
                    if emit_lo:
                        ls = lostg[tt % 2]
                        if tt >= 2:
                            ph.wait("vector", sLo[tt % 2], lo_st[tt - 2])
                        for j in range(ND):
                            pb = tp[j % 2]
                            if len(ev_t) >= 2:
                                ph.wait("tensor", sV, ev_t[-2])
                            tpt = ph.run("tensor",
                                         lambda t, j=j, pb=pb, lb=lbuf:
                                         t.transpose(pb[:],
                                                     lb[:, j * 128:(j + 1) * 128],
                                                     ident[:]), sPE)
                            ph.wait("vector", sPE, tpt)
                            ev = ph.run("vector", lambda v, pb=pb, j=j, ls=ls:
                                        v.tensor_copy(
                                            ls[:, j * 128:(j + 1) * 128],
                                            pb[:]), sV)
                            ev_t.append(ev)
                        ph.wait("sync", sV, ev_t[-1])
                        tst = ph.dma("sync",
                                     xnloT[tt].rearrange("j p c -> p j c"),
                                     ls[:].rearrange("p (j c) -> p j c", c=128),
                                     sLo[tt % 2])
                        lo_st.append(tst)
                    tr_last.append(tpt)
                if emit_lo:
                    ph.wait("sync", sLo[0], sLo[0].n)
                    ph.wait("sync", sLo[1], sLo[1].n)
                if name + ":xnT" in debug:
                    ph.wait("sync", sV, ev_t[-1])
                    o = nc.dram_tensor("dbg_" + name + "_xnT", [128, ND * T],
                                       BF16, kind="ExternalOutput").ap()
                    dbg_outs[name + "_xnT"] = o
                    td = ph.dma("sync", o, xnT[:], sSt)
                    ph.wait("sync", sSt, td)
            ph.emit()

        if "LN1" in phase_set:
            ln_phase("LN1", lambda tt: x_in[tt * 128:(tt + 1) * 128, :],
                     emit_lo=True)

        def projT_phase(name, w_dram, dst_T, do_mem):
            ph = Phase(nc, name)
            with ExitStack() as ps:
                EE = ps.enter_context
                wsb = EE(nc.sbuf_tensor(name + "_w", [128, ND * D], BF16))
                stg = [EE(nc.sbuf_tensor(name + f"_s{i}", [128, 512], BF16))
                       for i in range(4)]
                pps = [EE(nc.psum_tensor(name + f"_p{i}", [128, 512], F32))
                       for i in range(2)]
                sW = SP[0]
                sM = SP[1]
                sWL = [SP[2], SP[3]]
                sSg = [SP[4], SP[5], SP[6], SP[7]]
                sWS = SP[8]
                if do_mem:
                    mstg = EE(nc.sbuf_tensor(name + "_mf", [128, ND * 256], F32))
                    kmh = EE(nc.sbuf_tensor(name + "_mh", [128, ND * 512], BF16))
                    kml = EE(nc.sbuf_tensor(name + "_ml", [128, ND * 512], BF16))
                    wstg = mstg[:, 0:D]
                    wlos = kml[:, 0:D]
                if not do_mem:
                    tw = 0
                    for j in range(ND):
                        tw = ph.dma("gpsimd", wsb[:, j * D:(j + 1) * D],
                                    w_dram[j * 128:(j + 1) * 128, :], sW)
                    ph.wait("tensor", sW, tw)
                else:
                    wl_prev = None
                    for j in range(ND):
                        twf = ph.dma("gpsimd", wstg[:],
                                     w_dram[j * 128:(j + 1) * 128, :], sW)
                        ph.wait("vector", sW, twf)
                        ph.run("vector", lambda v, j=j: v.tensor_copy(
                            wsb[:, j * D:(j + 1) * D], wstg[:]))
                        tlo = ph.run("vector", lambda v, j=j: v.tensor_sub(
                            wlos[:], wstg[:], wsb[:, j * D:(j + 1) * D]), sV)
                        ph.wait("sync", sV, tlo)
                        twl = ph.dma("sync",
                                     wklo[:, j].rearrange("o p c -> p o c"),
                                     wlos[:].rearrange("p (o c) -> p o c", c=128),
                                     sWS)
                        ph.wait("gpsimd", sWS, twl)
                        wl_prev = tlo
                    ph.wait("tensor", sV, wl_prev)
                ev_t, st_t = [], []

                def unit(lhs_list, rhs_list, dst, src_of):
                    u = len(ev_t)
                    pb = pps[u % 2]
                    sb = stg[u % 4]
                    if u >= 2:
                        ph.wait("tensor", sV, ev_t[u - 2])
                    n = len(lhs_list)
                    tpe = None
                    for i in range(n):
                        tpe = ph.run("tensor",
                                     lambda t, l=lhs_list[i], r=rhs_list[i], pb=pb,
                                     i=i, n=n:
                                     t.matmul(pb[:], l, r, start=(i == 0),
                                              stop=(i == n - 1)),
                                     sPE if i == n - 1 else None)
                    ph.wait("vector", sPE, tpe)
                    if u >= 4:
                        ph.wait("vector", sSg[u % 4], st_t[u - 4])
                    ev = ph.run("vector", lambda v, pb=pb, sb=sb:
                                v.tensor_copy(sb[:], pb[:]), sV)
                    ev_t.append(ev)
                    ph.wait("sync", sV, ev)
                    st = ph.dma("sync", dst, src_of(sb), sSg[u % 4])
                    st_t.append(st)
                    return tpe

                for ot in range(ND):
                    for tc in range(4):
                        lhs = [wsb[:, j * D + ot * 128: j * D + ot * 128 + 128]
                               for j in range(ND)]
                        rhs = [xnT[:, j * T + tc * 512: j * T + tc * 512 + 512]
                               for j in range(ND)]
                        dst = dst_T[tc * 4:(tc + 1) * 4, ot].rearrange(
                            "t p c -> p t c")
                        unit(lhs, rhs, dst,
                             lambda sb: sb.rearrange("p (t c) -> p t c", c=128))
                if do_mem:
                    wlr = EE(nc.sbuf_tensor(name + "_wlr", [128, 2 * ND * 128],
                                            BF16))
                    lstg = [EE(nc.sbuf_tensor(name + f"_lo{i}", [128, 512], BF16))
                            for i in range(4)]
                    mem_pe = []
                    wlr_t = {}
                    for mc in range(8):
                        if mc >= 1:
                            ph.wait("gpsimd", sPE, mem_pe[-1])
                        tsp = None
                        for jh in range(2):
                            tm = 0
                            for jj in range(8):
                                j = jh * 8 + jj
                                tm = ph.dma("gpsimd",
                                            mstg[:, jj * 512:(jj + 1) * 512],
                                            membT[j * 128:(j + 1) * 128,
                                                  mc * 512:(mc + 1) * 512], sM)
                            ph.wait("vector", sM, tm)
                            off = jh * 8 * 512
                            ph.run("vector", lambda v, off=off: v.tensor_copy(
                                kmh[:, off:off + 8 * 512], mstg[:]))
                            tsp = ph.run("vector", lambda v, off=off: v.tensor_sub(
                                kml[:, off:off + 8 * 512], mstg[:],
                                kmh[:, off:off + 8 * 512]), sV)
                            if jh == 0:
                                ph.wait("gpsimd", sV, tsp)
                        ph.wait("tensor", sV, tsp)
                        tpe = None
                        for ot in range(ND):
                            ring = (mc * ND + ot) % 2
                            gw = wlr_t.get((mc * ND + ot) - 2)
                            ph.wait("gpsimd", sPE, gw)
                            twl = ph.dma("gpsimd",
                                         wlr[:, ring * ND * 128:
                                             (ring + 1) * ND * 128].rearrange(
                                             "p (j c) -> p j c", c=128),
                                         wklo[ot].rearrange("j p c -> p j c"),
                                         sWL[ring])
                            u = len(ev_t)
                            pb = pps[u % 2]
                            sb = stg[u % 4]
                            lb = lstg[u % 4]
                            if u >= 2:
                                ph.wait("tensor", sV, ev_t[u - 2])
                            for j in range(ND):
                                ph.run("tensor", lambda t, j=j, ot=ot, pb=pb:
                                       t.matmul(
                                           pb[:],
                                           wsb[:, j * D + ot * 128:
                                               j * D + ot * 128 + 128],
                                           kmh[:, j * 512:(j + 1) * 512],
                                           start=(j == 0), stop=False))
                            ph.wait("tensor", sWL[ring], twl)
                            for j in range(ND):
                                ph.run("tensor", lambda t, j=j, ring=ring, pb=pb:
                                       t.matmul(
                                           pb[:],
                                           wlr[:, ring * ND * 128 + j * 128:
                                               ring * ND * 128 + j * 128 + 128],
                                           kmh[:, j * 512:(j + 1) * 512],
                                           start=False, stop=False))
                            tpe = None
                            for j in range(ND):
                                tpe = ph.run(
                                    "tensor", lambda t, j=j, ot=ot, pb=pb:
                                    t.matmul(
                                        pb[:],
                                        wsb[:, j * D + ot * 128:
                                            j * D + ot * 128 + 128],
                                        kml[:, j * 512:(j + 1) * 512],
                                        start=False, stop=(j == ND - 1)),
                                    sPE if j == ND - 1 else None)
                            wlr_t[mc * ND + ot] = tpe
                            ph.wait("vector", sPE, tpe)
                            if u >= 4:
                                ph.wait("vector", sSg[u % 4], st_t[u - 4])
                            ph.run("vector", lambda v, pb=pb, sb=sb:
                                   v.tensor_copy(sb[:], pb[:]))
                            ev = ph.run("vector", lambda v, pb=pb, sb=sb, lb=lb:
                                        v.tensor_sub(lb[:], pb[:], sb[:]), sV)
                            ev_t.append(ev)
                            ph.wait("sync", sV, ev)
                            st = ph.dma("sync", kmemT[mc, ot], sb[:], sSg[u % 4])
                            st = ph.dma("sync", kmemTlo[mc, ot], lb[:],
                                        sSg[u % 4])
                            st_t.append(st)
                        mem_pe.append(tpe)
                for ss in sSg:
                    ph.wait("sync", ss, ss.n)
                ph.wait("sync", sWS, sWS.n)
            ph.emit()

        if "KMT" in phase_set:
            projT_phase("KMT", wkT, KT, True)

        if "VMT" in phase_set:
            ph = Phase(nc, "VMT")
            with ExitStack() as ps:
                EE = ps.enter_context
                wsb = EE(nc.sbuf_tensor("VMT_w", [128, ND * D], BF16))
                stg = [EE(nc.sbuf_tensor(f"VMT_s{i}", [128, 512], BF16))
                       for i in range(4)]
                pps = [EE(nc.psum_tensor(f"VMT_p{i}", [128, 512], F32))
                       for i in range(2)]
                mts = EE(nc.sbuf_tensor("VMT_mt", [128, 2 * ND * 512], BF16))
                sW = SP[0]
                sMh = [SP[1], SP[2]]
                sSg = [SP[4], SP[5], SP[6], SP[7]]
                tw = 0
                for j in range(ND):
                    tw = ph.dma("gpsimd", wsb[:, j * D:(j + 1) * D],
                                wvT[j * 128:(j + 1) * 128, :], sW)
                ph.wait("tensor", sW, tw)
                ev_t, st_t = [], []

                def unit(lhs_list, rhs_list, dst):
                    u = len(ev_t)
                    pb = pps[u % 2]
                    sb = stg[u % 4]
                    if u >= 2:
                        ph.wait("tensor", sV, ev_t[u - 2])
                    n = len(lhs_list)
                    tpe = None
                    for i in range(n):
                        tpe = ph.run("tensor",
                                     lambda t, l=lhs_list[i], r=rhs_list[i], pb=pb,
                                     i=i, n=n:
                                     t.matmul(pb[:], l, r, start=(i == 0),
                                              stop=(i == n - 1)),
                                     sPE if i == n - 1 else None)
                    ph.wait("vector", sPE, tpe)
                    if u >= 4:
                        ph.wait("vector", sSg[u % 4], st_t[u - 4])
                    ev = ph.run("vector", lambda v, pb=pb, sb=sb:
                                v.tensor_copy(sb[:], pb[:]), sV)
                    ev_t.append(ev)
                    ph.wait("sync", sV, ev)
                    st = ph.dma("sync", dst, sb[:], sSg[u % 4])
                    st_t.append(st)
                    return tpe

                for tt in range(NT):
                    for oc in range(4):
                        lhs = [xnT[:, j * T + tt * 128: j * T + tt * 128 + 128]
                               for j in range(ND)]
                        rhs = [wsb[:, j * D + oc * 512: j * D + oc * 512 + 512]
                               for j in range(ND)]
                        unit(lhs, rhs, Vh[tt][:, oc * 512:(oc + 1) * 512])
                mem_pe = []
                for mc in range(8):
                    half = (mc % 2) * ND * 512
                    if mc >= 2:
                        ph.wait("gpsimd", sPE, mem_pe[mc - 2])
                    tm = 0
                    for j in range(ND):
                        tm = ph.dma("gpsimd",
                                    mts[:, half + j * 512: half + j * 512 + 512],
                                    membT[j * 128:(j + 1) * 128,
                                          mc * 512:(mc + 1) * 512], sMh[mc % 2])
                    ph.wait("tensor", sMh[mc % 2], tm)
                    tpe = None
                    for ml in range(4):
                        mt = mc * 4 + ml
                        for oc in range(4):
                            lhs = [mts[:, half + j * 512 + ml * 128:
                                       half + j * 512 + ml * 128 + 128]
                                   for j in range(ND)]
                            rhs = [wsb[:, j * D + oc * 512: j * D + oc * 512 + 512]
                                   for j in range(ND)]
                            tpe = unit(lhs, rhs, vmem2[oc, mt])
                    mem_pe.append(tpe)
                for ss in sSg:
                    ph.wait("sync", ss, ss.n)
            ph.emit()

        if "SIM" in phase_set:
            ph = Phase(nc, "SIM")
            with ExitStack() as ps:
                EE = ps.enter_context
                kms = EE(nc.sbuf_tensor("SIM_km", [128, 2 * ND * 512], BF16))
                kls = EE(nc.sbuf_tensor("SIM_kl", [128, 2 * ND * 512], BF16))
                xlr = EE(nc.sbuf_tensor("SIM_xl", [128, 2 * ND * 128], BF16))
                stg = [EE(nc.sbuf_tensor(f"SIM_s{i}", [128, 512], F32))
                       for i in range(4)]
                pps = [EE(nc.psum_tensor(f"SIM_p{i}", [128, 512], F32))
                       for i in range(2)]
                sMh = [SP[0], SP[1]]
                sXl = [SP[2], SP[3]]
                sSg = [SP[4], SP[5], SP[6], SP[7]]
                ev_t, st_t, mem_pe = [], [], []
                xlr_t = {}
                for mc in range(8):
                    half = (mc % 2) * ND * 512
                    if mc >= 2:
                        ph.wait("gpsimd", sPE, mem_pe[mc - 2])
                    ph.dma("gpsimd",
                           kms[:, half:half + ND * 512].rearrange(
                               "p (j c) -> p j c", c=512),
                           kmemT[mc].rearrange("j p c -> p j c"), sMh[mc % 2])
                    tm = ph.dma("gpsimd",
                                kls[:, half:half + ND * 512].rearrange(
                                    "p (j c) -> p j c", c=512),
                                kmemTlo[mc].rearrange("j p c -> p j c"),
                                sMh[mc % 2])
                    ph.wait("tensor", sMh[mc % 2], tm)
                    tpe = None
                    for tt in range(NT):
                        idx = mc * NT + tt
                        ring = idx % 2
                        gw = xlr_t.get(idx - 2)
                        ph.wait("gpsimd", sPE, gw)
                        txl = ph.dma("gpsimd",
                                     xlr[:, ring * ND * 128:
                                         (ring + 1) * ND * 128].rearrange(
                                         "p (j c) -> p j c", c=128),
                                     xnloT[tt].rearrange("j p c -> p j c"),
                                     sXl[ring])
                        u = len(ev_t)
                        pb = pps[u % 2]
                        sb = stg[u % 4]
                        if u >= 2:
                            ph.wait("tensor", sV, ev_t[u - 2])
                        for j in range(ND):
                            ph.run("tensor",
                                   lambda t, j=j, tt=tt, pb=pb, half=half:
                                   t.matmul(pb[:],
                                            xnT[:, j * T + tt * 128:
                                                j * T + tt * 128 + 128],
                                            kms[:, half + j * 512:
                                                half + j * 512 + 512],
                                            start=(j == 0), stop=False))
                        ph.wait("tensor", sXl[ring], txl)
                        for j in range(ND):
                            ph.run("tensor",
                                   lambda t, j=j, ring=ring, pb=pb, half=half:
                                   t.matmul(pb[:],
                                            xlr[:, ring * ND * 128 + j * 128:
                                                ring * ND * 128 + j * 128 + 128],
                                            kms[:, half + j * 512:
                                                half + j * 512 + 512],
                                            start=False, stop=False))
                        for j in range(ND):
                            tpe = ph.run(
                                "tensor",
                                lambda t, j=j, tt=tt, pb=pb, half=half:
                                t.matmul(pb[:],
                                         xnT[:, j * T + tt * 128:
                                             j * T + tt * 128 + 128],
                                         kls[:, half + j * 512:
                                             half + j * 512 + 512],
                                         start=False, stop=(j == ND - 1)),
                                sPE if j == ND - 1 else None)
                        xlr_t[idx] = tpe
                        ph.wait("vector", sPE, tpe)
                        if u >= 4:
                            ph.wait("vector", sSg[u % 4], st_t[u - 4])
                        ev = ph.run("vector", lambda v, pb=pb, sb=sb:
                                    v.tensor_scalar_mul(
                                        sb[:], pb[:],
                                        1.0 / float(np.sqrt(D))), sV)
                        ev_t.append(ev)
                        ph.wait("sync", sV, ev)
                        st = ph.dma("sync", simd[tt][:, mc * 512:(mc + 1) * 512],
                                    sb[:], sSg[u % 4])
                        st_t.append(st)
                    mem_pe.append(tpe)
                for ss in sSg:
                    ph.wait("sync", ss, ss.n)
            ph.emit()

        if "OMEM" in phase_set:
            ph = Phase(nc, "OMEM")
            GT = 2
            with ExitStack() as ps:
                EE = ps.enter_context
                simg = EE(nc.sbuf_tensor("OM_sim", [128, GT * M], F32))
                wfs = EE(nc.sbuf_tensor("OM_wf", [128, GT * M], BF16))
                wft = EE(nc.sbuf_tensor("OM_wft", [128, GT * M], BF16))
                wexp = EE(nc.sbuf_tensor("OM_we", [128, 2 * M], BF16))
                tk = EE(nc.sbuf_tensor("OM_tk", [128, 40], F32))
                vms = EE(nc.sbuf_tensor("OM_vm", [128, 2 * 16 * 512], BF16))
                stg = [EE(nc.sbuf_tensor(f"OM_s{i}", [128, 512], BF16))
                       for i in range(4)]
                tp = [EE(nc.psum_tensor(f"OM_tp{i}", [128, 128], BF16))
                      for i in range(2)]
                om = [EE(nc.psum_tensor(f"OM_p{i}", [128, 512], F32))
                      for i in range(GT)]
                sSim = SP[0]
                sVm = [SP[1], SP[2]]
                sSg = [SP[4], SP[5], SP[6], SP[7]]
                TA, TB, T8, NT1, Z8, LNZ, B2, TT8, E8 = (
                    0, 8, 16, 24, 25, 26, 27, 28, 32)
                st_t = []
                prev_group_ev = None
                half_pe = {}
                twf_all = []
                g_tr_last = []
                for g in range(NT // GT):
                    if g >= 1:
                        ph.wait("gpsimd", sPE, g_tr_last[g - 1])
                        ph.wait("gpsimd", sV, twf_all[-1])
                    tload = ph.dma("gpsimd",
                                   simg[:].rearrange("p (t c) -> p t c", c=M),
                                   simd[g * GT:(g + 1) * GT].rearrange(
                                       "t p c -> p t c"), sSim)
                    tr_ev = []
                    tpt = None
                    for tl in range(GT):
                        sl = simg[:, tl * M:(tl + 1) * M]
                        wf = wfs[:, tl * M:(tl + 1) * M]
                        if tl == 0:
                            ph.wait("vector", sSim, tload)
                        ph.run("vector", lambda v, sl=sl: v.max(
                            out=tk[:, TA:TA + 8], in_=sl[:, :M // 2]))
                        thm = ph.run("vector", lambda v, sl=sl: v.max(
                            out=tk[:, TB:TB + 8], in_=sl[:, M // 2:]), sV)
                        ph.wait("vector", sV, thm)
                        tmax = ph.run("vector", lambda v: v.max(
                            out=tk[:, T8:T8 + 8], in_=tk[:, TA:TA + 16]), sV)
                        ph.wait("vector", sV, tmax)
                        ph.run("vector", lambda v: v.tensor_scalar_mul(
                            tk[:, NT1:NT1 + 1], tk[:, T8:T8 + 1], -1.0))
                        tnt = ph.run("vector", lambda v: v.tensor_copy(
                            tk[:, TT8:TT8 + 1], tk[:, T8 + 7:T8 + 8]), sV)
                        ph.wait("scalar", sV, tnt)
                        tz = ph.run("scalar", lambda s: s.activation(
                            tk[:, E8:E8 + 8], tk[:, T8:T8 + 8], AF.Exp,
                            bias=tk[:, NT1:NT1 + 1],
                            accum_out=tk[:, Z8:Z8 + 1]), sA)
                        ph.wait("scalar", sA, tz)
                        tln = ph.run("scalar", lambda s: s.activation(
                            tk[:, LNZ:LNZ + 1], tk[:, Z8:Z8 + 1], AF.Ln), sA)
                        ph.wait("vector", sA, tln)
                        tb2 = ph.run("vector", lambda v: v.tensor_scalar(
                            tk[:, B2:B2 + 1], tk[:, LNZ:LNZ + 1],
                            tk[:, T8:T8 + 1], -1.0,
                            op0=ALU.add, op1=ALU.mult), sV)
                        ph.wait("scalar", sV, tb2)
                        we = wexp[:, (tl % 2) * M:(tl % 2) * M + M]
                        if len(twf_all) >= 2:
                            ph.wait("scalar", sV, twf_all[-2])
                        twe = ph.run("scalar", lambda s, sl=sl, we=we: s.activation(
                            we, sl, AF.Exp, bias=tk[:, B2:B2 + 1]), sA)
                        ph.wait("vector", sA, twe)
                        twf = ph.run("vector", lambda v, sl=sl, we=we, wf=wf:
                                     v.scalar_tensor_tensor(
                                         wf, sl, tk[:, TT8:TT8 + 1], we,
                                         op0=ALU.is_ge, op1=ALU.mult), sV)
                        twf_all.append(twf)
                        ph.wait("tensor", sV, twf)
                        for mb in range(32):
                            pb = tp[mb % 2]
                            if len(tr_ev) >= 2:
                                ph.wait("tensor", sV, tr_ev[-2])
                            tpt = ph.run("tensor", lambda t, wf=wf, mb=mb, pb=pb:
                                         t.transpose(
                                             pb[:], wf[:, mb * 128:(mb + 1) * 128],
                                             ident[:]), sPE)
                            ph.wait("vector", sPE, tpt)
                            dst = wft[:, tl * M + mb * 128:tl * M + mb * 128 + 128]
                            ev = ph.run("vector", lambda v, pb=pb, dst=dst:
                                        v.tensor_copy(dst, pb[:]), sV)
                            tr_ev.append(ev)
                    g_tr_last.append(tpt)
                    if g == 0 and "OM:wft" in debug:
                        o = nc.dram_tensor("dbg_OM_wft", [128, GT * M], BF16,
                                           kind="ExternalOutput").ap()
                        dbg_outs["OM_wft"] = o
                        ph.wait("sync", sV, tr_ev[-1])
                        tdw = ph.dma("sync", o, wft[:], sSt)
                        ph.wait("sync", sSt, tdw)
                    for oc in range(4):
                        for hf in range(2):
                            half = hf * 16 * 512
                            prev = half_pe.get((oc - 1, hf) if oc > 0
                                               else (3, hf))
                            ph.wait("gpsimd", sPE, prev)
                            tvm = ph.dma("gpsimd",
                                         vms[:, half:half + 16 * 512].rearrange(
                                             "p (m c) -> p m c", c=512),
                                         vmem2[oc, hf * 16:(hf + 1) * 16].rearrange(
                                             "m p c -> p m c"), sVm[hf])
                            ph.wait("tensor", sVm[hf], tvm)
                            if hf == 0:
                                ph.wait("tensor", sV, tr_ev[-1])
                                ph.wait("tensor", sV, prev_group_ev)
                            tmm = None
                            for tl in range(GT):
                                for ml in range(16):
                                    mb = hf * 16 + ml
                                    tmm = ph.run(
                                        "tensor",
                                        lambda t, tl=tl, ml=ml, mb=mb, half=half:
                                        t.matmul(
                                            om[tl][:],
                                            wft[:, tl * M + mb * 128:
                                                tl * M + mb * 128 + 128],
                                            vms[:, half + ml * 512:
                                                half + ml * 512 + 512],
                                            start=(mb == 0), stop=(mb == 31)),
                                        sPE if ml == 15 else None)
                            half_pe[(oc, hf)] = tmm
                        evs = []
                        for tl in range(GT):
                            u = len(st_t)
                            sb = stg[u % 4]
                            ph.wait("vector", sPE, half_pe[(oc, 1)] - (GT - 1 - tl))
                            if u >= 4:
                                ph.wait("vector", sSg[u % 4], st_t[u - 4])
                            ev = ph.run("vector", lambda v, tl=tl, sb=sb:
                                        v.tensor_copy(sb[:], om[tl][:]), sV)
                            evs.append(ev)
                            ph.wait("sync", sV, ev)
                            st = ph.dma("sync",
                                        omemp[g * GT + tl][:,
                                                           oc * 512:(oc + 1) * 512],
                                        sb[:], sSg[u % 4])
                            st_t.append(st)
                        prev_group_ev = evs[-1]
                for ss in sSg:
                    ph.wait("sync", ss, ss.n)
            ph.emit()

        if "QT" in phase_set:
            projT_phase("QT", wqT, QT, False)

        if "CTX" in phase_set:
            ph = Phase(nc, "CTX")
            with ExitStack() as ps:
                EE = ps.enter_context
                qsb = EE(nc.sbuf_tensor("CT_q", [128, 2 * D], BF16))
                ktr = EE(nc.sbuf_tensor("CT_kt", [128, 4 * D], BF16))
                vr = EE(nc.sbuf_tensor("CT_v", [128, 4 * D], BF16))
                psb = EE(nc.sbuf_tensor("CT_p", [128, 4 * 384], BF16))
                ptb = EE(nc.sbuf_tensor("CT_pt", [128, 4 * 384], BF16))
                zz = EE(nc.sbuf_tensor("CT_z", [128, 8], F32))
                cst = EE(nc.sbuf_tensor("CT_c", [128, 2 * D], BF16))
                sc = [EE(nc.psum_tensor(f"CT_sc{i}", [128, 384], F32))
                      for i in range(4)]
                tp = [EE(nc.psum_tensor(f"CT_tp{i}", [128, 384], BF16))
                      for i in range(2)]
                otp = [EE(nc.psum_tensor(f"CT_ot{i}", [128, 128], F32))
                       for i in range(2)]
                st_t = []
                exp_t = []
                score_last = []
                av_last = []
                otev_t = []
                trp_last = []
                tp_last = {}
                sTl = [SP[0], SP[1], SP[2], SP[3]]
                sCs = [SP[4], SP[5]]
                tl_hist = {}
                un = 0
                for tt in range(NT):
                    if tt >= 4:
                        ph.wait("gpsimd", sPE, av_last[tt - 2])
                    if tt >= 2:
                        ph.wait("gpsimd", sPE, score_last[tt - 2])
                    ph.dma("gpsimd",
                           ktr[:, (tt % 4) * D:(tt % 4) * D + D].rearrange(
                               "p (j c) -> p j c", c=128),
                           KT[tt].rearrange("j p c -> p j c"), sTl[tt % 4])
                    ph.dma("gpsimd", vr[:, (tt % 4) * D:(tt % 4) * D + D],
                           Vh[tt], sTl[tt % 4])
                    tld = ph.dma("gpsimd",
                                 qsb[:, (tt % 2) * D:(tt % 2) * D + D].rearrange(
                                     "p (j c) -> p j c", c=128),
                                 QT[tt].rearrange("j p c -> p j c"),
                                 sTl[tt % 4])
                    tl_hist[tt] = tld
                    nb = min(tt + 1, 3)
                    kts = list(range(tt - nb + 1, tt + 1))
                    q = qsb[:, (tt % 2) * D:(tt % 2) * D + D]
                    if tt >= 2:
                        ph.wait("scalar", sCs[tt % 2], st_t[tt - 2])
                    def backend(u, h, nb, kts, zc, pcur, te):
                        ph.wait("vector", sA, te)
                        trc = ph.run("vector", lambda v, zc=zc:
                                     v.reciprocal(zc, zc), sV)
                        ph.wait("vector", sV, trc)
                        tps = ph.run("vector", lambda v, zc=zc, pcur=pcur:
                                     v.tensor_scalar_mul(pcur, pcur, zc), sV)
                        ph.wait("tensor", sV, tps)
                        pt = tp[u % 2]
                        g = tp_last.get(u % 2)
                        if g:
                            ph.wait("tensor", g[1], g[0])
                        tpt = None
                        for bi in range(nb):
                            tpt = ph.run(
                                "tensor", lambda t, bi=bi, pt=pt, pcur=pcur:
                                t.transpose(
                                    pt[:, bi * 128:(bi + 1) * 128],
                                    pcur[:, bi * 128:(bi + 1) * 128],
                                    ident[:]), sPE if bi == nb - 1 else None)
                        eng, sem = (("vector", sV) if (u % 2 == 0)
                                    else ("scalar", sA))
                        ph.wait(eng, sPE, tpt)
                        dst = ptb[:, (u % 4) * 384:(u % 4) * 384 + nb * 128]
                        if eng == "vector":
                            ev = ph.run("vector", lambda v, pt=pt, dst=dst, nb=nb:
                                        v.tensor_copy(dst, pt[:, 0:nb * 128]), sem)
                        else:
                            ev = ph.run("scalar", lambda s, pt=pt, dst=dst, nb=nb:
                                        s.copy(dst, pt[:, 0:nb * 128]), sem)
                        tp_last[u % 2] = (ev, sem)
                        trp_last.append(tpt)
                        po = otp[u % 2]
                        if u >= 2:
                            ph.wait("tensor", sA, otev_t[u - 2])
                        ph.wait("tensor", sem, ev)
                        tav = None
                        for bi, kt in enumerate(kts):
                            tav = ph.run(
                                "tensor", lambda t, bi=bi, kt=kt, h=h, po=po,
                                u=u:
                                t.matmul(po[:],
                                         vr[:, (kt % 4) * D + h * 128:
                                            (kt % 4) * D + h * 128 + 128],
                                         ptb[:, (u % 4) * 384 + bi * 128:
                                             (u % 4) * 384 + bi * 128 + 128],
                                         start=(bi == 0), stop=(bi == nb - 1)),
                                sPE if bi == nb - 1 else None)
                        if h == 15:
                            av_last.append(tav)
                        ph.wait("scalar", sPE, tav)
                        dst = cst[:, (tt % 2) * D + h * 128:
                                  (tt % 2) * D + h * 128 + 128]
                        tev = ph.run("scalar", lambda s, po=po, dst=dst:
                                     s.copy(dst, po[:]), sA)
                        otev_t.append(tev)

                    pend = None
                    for h in range(16):
                        pb = sc[un % 4]
                        if un >= 4:
                            ph.wait("tensor", sA, exp_t[un - 4])
                        if h == 0:
                            ph.wait("tensor", sTl[tt % 4], tld)
                            if tt >= 1:
                                ph.wait("tensor", sTl[(tt - 1) % 4],
                                        tl_hist[tt - 1])
                            if tt >= 2:
                                ph.wait("tensor", sTl[(tt - 2) % 4],
                                        tl_hist[tt - 2])
                        tsc = None
                        for bi, kt in enumerate(kts):
                            tsc = ph.run(
                                "tensor", lambda t, h=h, bi=bi, kt=kt, pb=pb, q=q:
                                t.matmul(pb[:, bi * 128:(bi + 1) * 128],
                                         q[:, h * 128:(h + 1) * 128],
                                         ktr[:, (kt % 4) * D + h * 128:
                                             (kt % 4) * D + h * 128 + 128],
                                         start=True, stop=True),
                                sPE if bi == nb - 1 else None)
                        if h == 15:
                            score_last.append(tsc)
                        ph.wait("vector", sPE, tsc)
                        tm = ph.run("vector", lambda v, pb=pb, nb=nb:
                                    v.tensor_tensor(
                                        pb[:, 0:nb * 128], pb[:, 0:nb * 128],
                                        maskw[:, (3 - nb) * 128:384], op=ALU.add),
                                    sV)
                        ph.wait("scalar", sV, tm)
                        zc = zz[:, un % 8:un % 8 + 1]
                        pcur = psb[:, (un % 4) * 384:(un % 4) * 384 + nb * 128]
                        if un >= 4:
                            ph.wait("scalar", sPE, trp_last[un - 4])
                        te = ph.run("scalar", lambda s, pb=pb, nb=nb, zc=zc,
                                    pcur=pcur:
                                    s.activation(pcur, pb[:, 0:nb * 128], AF.Exp,
                                                 bias=cmisc[:, C_NB32:C_NB32 + 1],
                                                 scale=cmisc[:, C_S128:C_S128 + 1],
                                                 accum_out=zc), sA)
                        exp_t.append(te)
                        if pend is not None:
                            backend(*pend)
                        pend = (un, h, nb, kts, zc, pcur, te)
                        un += 1
                    backend(*pend)
                    ph.wait("sync", sA, otev_t[-1])
                    st = ph.dma("sync", ctxT[tt].rearrange("j p c -> p j c"),
                                cst[:, (tt % 2) * D:(tt % 2) * D + D].rearrange(
                                    "p (j c) -> p j c", c=128), sCs[tt % 2])
                    st_t.append(st)
                ph.wait("sync", sCs[0], sCs[0].n)
                ph.wait("sync", sCs[1], sCs[1].n)
            ph.emit()

        if "CMB" in phase_set:
            ph = Phase(nc, "CMB")
            with ExitStack() as ps:
                EE = ps.enter_context
                ctxa = xnT
                omt = EE(nc.sbuf_tensor("CM_om", [128, ND * T], BF16))
                otile = EE(nc.sbuf_tensor("CM_ot", [128, 2 * D], BF16))
                wos = EE(nc.sbuf_tensor("CM_wo", [128, 2 * ND * 512], BF16))
                gall = EE(nc.sbuf_tensor("CM_g", [128, 16], F32))
                ols = [EE(nc.sbuf_tensor(f"CM_ol{i}", [128, 512], F32))
                       for i in range(2)]
                tmp = EE(nc.sbuf_tensor("CM_tmp", [128, 512], F32))
                xsb = [EE(nc.sbuf_tensor(f"CM_x{i}", [128, 512], F32))
                       for i in range(2)]
                xnw = [EE(nc.sbuf_tensor(f"CM_xn{i}", [128, 512], F32))
                       for i in range(2)]
                tp = [EE(nc.psum_tensor(f"CM_tp{i}", [128, 128], BF16))
                      for i in range(2)]
                gps = [EE(nc.psum_tensor(f"CM_gp{i}", [128, 1], F32))
                       for i in range(2)]
                pso = [EE(nc.psum_tensor(f"CM_po{i}", [128, 512], F32))
                       for i in range(2)]
                psm = [EE(nc.psum_tensor(f"CM_pm{i}", [128, 512], F32))
                       for i in range(2)]
                sCtx = SP[0]
                sOm = [SP[1], SP[2]]
                sWo = [SP[3], SP[8]]
                sXl2 = [SP[4], SP[5]]
                sYs = [SP[6], SP[7]]
                xnT_v = xnT.rearrange("p (j c) -> p j c", c=T)
                tctx = 0
                for tt in range(NT):
                    tctx = ph.dma("gpsimd",
                                  xnT_v[:, :, tt * 128:tt * 128 + 128],
                                  ctxT[tt].rearrange("j p c -> p j c"), sCtx)
                trv = []
                om_tr_last = []
                for tt in range(NT):
                    ob = otile[:, (tt % 2) * D:(tt % 2) * D + D]
                    if tt >= 2:
                        ph.wait("gpsimd", sPE, om_tr_last[tt - 2])
                    tl = ph.dma("gpsimd", ob, omemp[tt], sOm[tt % 2])
                    ph.wait("tensor", sOm[tt % 2], tl)
                    tpt = None
                    for j in range(ND):
                        pb = tp[j % 2]
                        if len(trv) >= 2:
                            ph.wait("tensor", sV, trv[-2])
                        tpt = ph.run("tensor", lambda t, j=j, pb=pb, ob=ob:
                                     t.transpose(pb[:], ob[:, j * 128:(j + 1) * 128],
                                                 ident[:]), sPE)
                        ph.wait("vector", sPE, tpt)
                        ev = ph.run("vector", lambda v, pb=pb, j=j, tt=tt:
                                    v.tensor_copy(
                                        omt[:, j * T + tt * 128:
                                            j * T + tt * 128 + 128], pb[:]), sV)
                        trv.append(ev)
                    om_tr_last.append(tpt)
                ph.wait("tensor", sCtx, tctx)
                gev = []
                for tt in range(NT):
                    gp = gps[tt % 2]
                    if tt >= 2:
                        ph.wait("tensor", sA, gev[tt - 2])
                    tg = None
                    for j in range(ND):
                        tg = ph.run("tensor", lambda t, j=j, tt=tt, gp=gp:
                                    t.matmul(gp[:],
                                             ctxa[:, j * T + tt * 128:
                                                  j * T + tt * 128 + 128],
                                             gcomb[:, j:j + 1],
                                             start=(j == 0), stop=(j == ND - 1)),
                                    sPE if j == ND - 1 else None)
                    ph.wait("scalar", sPE, tg)
                    ge = ph.run("scalar", lambda s, gp=gp, tt=tt: s.activation(
                        gall[:, tt:tt + 1], gp[:], AF.Sigmoid, bias=gbb[:]), sA)
                    gev.append(ge)
                ph.wait("tensor", sA, gev[-1])
                st_t, ev_t = [], []
                wo_pe = {}
                for oc in range(4):
                    half = (oc % 2) * ND * 512
                    if oc >= 2:
                        ph.wait("gpsimd", sPE, wo_pe[oc - 2])
                    twl = ph.dma("gpsimd",
                                 wos[:, half:half + ND * 512].rearrange(
                                     "p (j c) -> p j c", c=512),
                                 woT[:, oc * 512:(oc + 1) * 512].rearrange(
                                     "(j p) c -> p j c", p=128), sWo[oc % 2])
                    ph.wait("tensor", sWo[oc % 2], twl)
                    tpm = None
                    for tt in range(NT):
                        u = len(st_t)
                        po, pm = pso[u % 2], psm[u % 2]
                        if u >= 2:
                            ph.wait("tensor", sV, ev_t[u - 2])
                        for j in range(ND):
                            ph.run("tensor", lambda t, j=j, tt=tt, po=po, half=half:
                                   t.matmul(po[:],
                                            ctxa[:, j * T + tt * 128:
                                                 j * T + tt * 128 + 128],
                                            wos[:, half + j * 512:
                                                half + j * 512 + 512],
                                            start=(j == 0), stop=(j == ND - 1)))
                        for j in range(ND):
                            tpm = ph.run(
                                "tensor", lambda t, j=j, tt=tt, pm=pm, half=half:
                                t.matmul(pm[:],
                                         omt[:, j * T + tt * 128:
                                             j * T + tt * 128 + 128],
                                         wos[:, half + j * 512:
                                             half + j * 512 + 512],
                                         start=(j == 0), stop=(j == ND - 1)),
                                sPE if j == ND - 1 else None)
                        tx = ph.dma("sync", xsb[u % 2][:],
                                    x_in[tt * 128:(tt + 1) * 128,
                                         oc * 512:(oc + 1) * 512], sXl2[u % 2])
                        ph.wait("scalar", sPE, tpm)
                        if u >= 2:
                            ph.wait("scalar", sV, ev_t[u - 2])
                        tol = ph.run("scalar", lambda s, po=po, u=u:
                                     s.copy(ols[u % 2][:], po[:]), sA)
                        ph.wait("vector", sA, tol)
                        ph.run("vector", lambda v, pm=pm, u=u, tt=tt:
                               v.scalar_tensor_tensor(
                                   tmp[:], pm[:], gall[:, tt:tt + 1],
                                   ols[u % 2][:], op0=ALU.mult, op1=ALU.add))
                        ph.wait("vector", sXl2[u % 2], tx)
                        if u >= 2:
                            ph.wait("vector", sYs[u % 2], st_t[u - 2])
                        ev = ph.run("vector", lambda v, u=u:
                                    v.tensor_add(xnw[u % 2][:], tmp[:],
                                                 xsb[u % 2][:]), sV)
                        ev_t.append(ev)
                        ph.wait("sync", sV, ev)
                        st = ph.dma("sync", xres[tt][:, oc * 512:(oc + 1) * 512],
                                    xnw[u % 2][:], sYs[u % 2])
                        st_t.append(st)
                    wo_pe[oc] = tpm
                ph.wait("sync", sYs[0], sYs[0].n)
                ph.wait("sync", sYs[1], sYs[1].n)
            ph.emit()

        if "LN2" in phase_set:
            ln_phase("LN2", lambda tt: xres[tt])

        def f1_phase(name):
            ph = Phase(nc, name)
            with ExitStack() as ps:
                EE = ps.enter_context
                wg_sb = EE(nc.sbuf_tensor(name + "_wg", [128, 2 * ND * 512], BF16))
                wu_sb = EE(nc.sbuf_tensor(name + "_wu", [128, 2 * ND * 512], BF16))
                hg = [EE(nc.sbuf_tensor(name + f"_hg{i}", [128, 512], BF16))
                      for i in range(2)]
                fst = [EE(nc.sbuf_tensor(name + f"_f{i}", [128, 512], BF16))
                       for i in range(4)]
                psg = [EE(nc.psum_tensor(name + f"_pg{i}", [128, 512], F32))
                       for i in range(2)]
                psu = [EE(nc.psum_tensor(name + f"_pu{i}", [128, 512], F32))
                       for i in range(2)]
                st_t, ev_t, grp_pe = [], [], {}
                hg_done = []
                sWh = [SP[0], SP[1]]
                sSg = [SP[4], SP[5], SP[6], SP[7]]
                for ibg in range(16):
                    half = (ibg % 2) * ND * 512
                    if ibg >= 2:
                        ph.wait("gpsimd", sPE, grp_pe[ibg - 2])
                    ph.dma("gpsimd",
                           wg_sb[:, half:half + ND * 512].rearrange(
                               "p (j c) -> p j c", c=512),
                           wgR[ibg].rearrange("j p c -> p j c"), sWh[ibg % 2])
                    twl = ph.dma("gpsimd",
                                 wu_sb[:, half:half + ND * 512].rearrange(
                                     "p (j c) -> p j c", c=512),
                                 wuR[ibg].rearrange("j p c -> p j c"),
                                 sWh[ibg % 2])
                    ph.wait("tensor", sWh[ibg % 2], twl)
                    tpu = None
                    for ibl in range(4):
                        ib = ibg * 4 + ibl
                        for tc in range(4):
                            u = len(st_t)
                            pg, pu = psg[u % 2], psu[u % 2]
                            if u >= 2:
                                ph.wait("tensor", sA, hg_done[u - 2])
                                ph.wait("tensor", sV, ev_t[u - 2])
                            for j in range(ND):
                                ph.run("tensor",
                                       lambda t, j=j, ibl=ibl, tc=tc, pg=pg,
                                       half=half:
                                       t.matmul(
                                           pg[:],
                                           wg_sb[:, half + j * 512 + ibl * 128:
                                                 half + j * 512 + ibl * 128 + 128],
                                           xnT[:, j * T + tc * 512:
                                               j * T + tc * 512 + 512],
                                           start=(j == 0), stop=(j == ND - 1)))
                            for j in range(ND):
                                tpu = ph.run(
                                    "tensor",
                                    lambda t, j=j, ibl=ibl, tc=tc, pu=pu, half=half:
                                    t.matmul(
                                        pu[:],
                                        wu_sb[:, half + j * 512 + ibl * 128:
                                              half + j * 512 + ibl * 128 + 128],
                                        xnT[:, j * T + tc * 512:
                                            j * T + tc * 512 + 512],
                                        start=(j == 0), stop=(j == ND - 1)),
                                    sPE if j == ND - 1 else None)
                            ph.wait("scalar", sPE, tpu)
                            if u >= 2:
                                ph.wait("scalar", sV, ev_t[u - 2])
                            th = ph.run("scalar", lambda s, pg=pg, u=u, ib=ib:
                                        s.activation(hg[u % 2][:], pg[:], AF.Silu,
                                                     scale=sgc[:, ib:ib + 1]), sA)
                            hg_done.append(th)
                            ph.wait("vector", sA, th)
                            if u >= 4:
                                ph.wait("vector", sSg[u % 4], st_t[u - 4])
                            sb = fst[u % 4]
                            ev = ph.run("vector",
                                        lambda v, pu=pu, u=u, ib=ib, sb=sb:
                                        v.scalar_tensor_tensor(
                                            sb[:], pu[:], suc[:, ib:ib + 1],
                                            hg[u % 2][:], op0=ALU.mult,
                                            op1=ALU.mult), sV)
                            ev_t.append(ev)
                            ph.wait("sync", sV, ev)
                            st = ph.dma("sync",
                                        fuse[ib][:, tc * 512:(tc + 1) * 512],
                                        sb[:], sSg[u % 4])
                            st_t.append(st)
                    grp_pe[ibg] = tpu
                for ss in sSg:
                    ph.wait("sync", ss, ss.n)
            ph.emit()

        def f2_phase(name, dst_of):
            ph = Phase(nc, name)
            with ExitStack() as ps:
                EE = ps.enter_context
                wd_sb = EE(nc.sbuf_tensor(name + "_wd", [128, 64 * 512], BF16))
                sdbc = EE(nc.sbuf_tensor(name + "_sd", [128, D], F32))
                tsd = ph.dma("gpsimd", sdbc[:], sdbc_d, sG)
                ph.wait("vector", sG, tsd)
                fsb = [EE(nc.sbuf_tensor(name + f"_fs{i}", [128, 4 * 512], BF16))
                       for i in range(2)]
                xsb = [EE(nc.sbuf_tensor(name + f"_x{i}", [128, 512], F32))
                       for i in range(2)]
                tmp = EE(nc.sbuf_tensor(name + "_tmp", [128, 512], F32))
                ynw = [EE(nc.sbuf_tensor(name + f"_y{i}", [128, 512], F32))
                       for i in range(2)]
                psy = [EE(nc.psum_tensor(name + f"_py{i}", [128, 512], F32))
                       for i in range(8)]
                sWd = [SP[0], SP[1]]
                sFs = [SP[2], SP[3]]
                sXl2 = [SP[4], SP[5]]
                sYs = [SP[6], SP[7]]
                st_t = []
                ev_by_unit = {}
                fs_pe = {}
                wd_pe = {}
                nchunk = 0
                for oc in range(4):
                    if oc >= 1:
                        ph.wait("gpsimd", sPE, wd_pe.get(oc - 1))
                    twd = 0
                    for j8 in range(8):
                        twd = ph.dma(
                            "gpsimd",
                            wd_sb[:, j8 * 8 * 512:(j8 + 1) * 8 * 512].rearrange(
                                "p (j c) -> p j c", c=512),
                            wd_i8[j8 * 1024:(j8 + 1) * 1024,
                                  oc * 512:(oc + 1) * 512].rearrange(
                                      "(j p) c -> p j c", p=128), sWd[oc % 2])
                    ph.wait("tensor", sWd[oc % 2], twd)
                    for g4 in range(4):
                        pset = [psy[(g4 % 2) * 4 + i] for i in range(4)]
                        uid = oc * 4 + g4
                        if uid >= 2:
                            ph.wait("tensor", sV, ev_by_unit[uid - 2])
                        tmm = None
                        for ib4 in range(16):
                            fb = fsb[ib4 % 2]
                            ph.wait("scalar", sPE, fs_pe.get(nchunk - 2))
                            tf = ph.dma(
                                "scalar",
                                fb[:].rearrange("p (b c) -> p b c", c=512),
                                fuse[ib4 * 4:(ib4 + 1) * 4, :,
                                     g4 * 512:(g4 + 1) * 512].rearrange(
                                         "b p c -> p b c"), sFs[ib4 % 2])
                            ph.wait("tensor", sFs[ib4 % 2], tf)
                            for bl in range(4):
                                ib = ib4 * 4 + bl
                                for tl in range(4):
                                    tmm = ph.run(
                                        "tensor",
                                        lambda t, fb=fb, bl=bl, ib=ib, tl=tl,
                                        pset=pset:
                                        t.matmul(
                                            pset[tl][:],
                                            fb[:, bl * 512 + tl * 128:
                                               bl * 512 + tl * 128 + 128],
                                            wd_sb[:, ib * 512:(ib + 1) * 512],
                                            start=(ib == 0), stop=(ib == 63)),
                                        sPE if (bl == 3 and tl == 3) else None)
                            fs_pe[nchunk] = tmm
                            nchunk += 1
                        wd_pe[oc] = tmm
                        lastev = None
                        for tl in range(4):
                            tt = g4 * 4 + tl
                            u = len(st_t)
                            tx = ph.dma("sync", xsb[u % 2][:],
                                        xres[tt][:, oc * 512:(oc + 1) * 512],
                                        sXl2[u % 2])
                            ph.wait("vector", sPE, tmm)
                            ph.run("vector", lambda v, pset=pset, tl=tl, oc=oc:
                                   v.tensor_mul(tmp[:], pset[tl][:],
                                                sdbc[:, oc * 512:(oc + 1) * 512]))
                            ph.wait("vector", sXl2[u % 2], tx)
                            if u >= 2:
                                ph.wait("vector", sYs[u % 2], st_t[u - 2])
                            ev = ph.run("vector", lambda v, u=u:
                                        v.tensor_add(ynw[u % 2][:], tmp[:],
                                                     xsb[u % 2][:]), sV)
                            lastev = ev
                            ph.wait("sync", sV, ev)
                            st = ph.dma("sync", dst_of(tt, oc), ynw[u % 2][:],
                                        sYs[u % 2])
                            st_t.append(st)
                        ev_by_unit[uid] = lastev
                ph.wait("sync", sYs[0], sYs[0].n)
                ph.wait("sync", sYs[1], sYs[1].n)
            ph.emit()

        if "F1a" in phase_set:
            f1_phase("F1a")
        if "F2a" in phase_set:
            f2_phase("F2a", lambda tt, oc: xres[tt][:, oc * 512:(oc + 1) * 512])
        if "LN3" in phase_set:
            ln_phase("LN3", lambda tt: xres[tt])

        def btier(tier):
            nh = NHS[tier]
            dim = nh * 64
            ndt = IN_OFF[tier + 1] // 128
            o0 = OUT_OFF[tier]
            rbase = o0 // 128
            has_fb = tier < 2
            w = nh * 8
            with ExitStack() as ts:
                TE = ts.enter_context
                vts = TE(nc.sbuf_tensor(f"B{tier}_v", [128, NT * dim], BF16))
                h0 = o0 // 64
                if has_fb:
                    cT = TE(nc.sbuf_tensor(f"B{tier}_cT", [64, 2 * T], BF16))
                    ph = Phase(nc, f"B{tier}c")
                    with ExitStack() as ps:
                        EE = ps.enter_context
                        pck = [EE(nc.psum_tensor(f"B{tier}c_k{i}", [128, w], F32))
                               for i in range(2)]
                        pcv = [EE(nc.psum_tensor(f"B{tier}c_v{i}", [128, w], F32))
                               for i in range(2)]
                        tpc = [EE(nc.psum_tensor(f"B{tier}c_t{i}", [128, 128], BF16))
                               for i in range(2)]
                        csb = [EE(nc.sbuf_tensor(f"B{tier}c_s{i}", [128, 2 * w],
                                                 BF16)) for i in range(2)]
                        nhi = 14 if tier == 0 else 8
                        j0 = 2 if tier == 0 else 8
                        fkc, fvc = fk_sb[tier], fv_sb[tier]
                        ev_t, tr_ev, c_tr_last = [], [], []
                        for tt in range(NT):
                            pk_, pv_ = pck[tt % 2], pcv[tt % 2]
                            sb = csb[tt % 2]
                            if tt >= 2:
                                ph.wait("tensor", sV, ev_t[tt - 2])
                                ph.wait("vector", sPE, c_tr_last[tt - 2])
                            tpe = None
                            for jj in range(nhi):
                                j = j0 + jj
                                ph.run("tensor", lambda t, j=j, jj=jj, tt=tt,
                                       pk_=pk_:
                                       t.matmul(pk_[:],
                                                xnT[:, j * T + tt * 128:
                                                    j * T + tt * 128 + 128],
                                                fkc[:, jj * w:(jj + 1) * w],
                                                start=(jj == 0),
                                                stop=(jj == nhi - 1)))
                            for jj in range(nhi):
                                j = j0 + jj
                                tpe = ph.run("tensor", lambda t, j=j, jj=jj, tt=tt,
                                             pv_=pv_:
                                             t.matmul(pv_[:],
                                                      xnT[:, j * T + tt * 128:
                                                          j * T + tt * 128 + 128],
                                                      fvc[:, jj * w:(jj + 1) * w],
                                                      start=(jj == 0),
                                                      stop=(jj == nhi - 1)),
                                             sPE if jj == nhi - 1 else None)
                            ph.wait("vector", sPE, tpe)
                            ph.run("vector", lambda v, pk_=pk_, sb=sb:
                                   v.tensor_copy(sb[:, 0:w], pk_[:]))
                            ev = ph.run("vector", lambda v, pv_=pv_, sb=sb:
                                        v.tensor_copy(sb[:, w:2 * w], pv_[:]), sV)
                            ev_t.append(ev)
                            ph.wait("tensor", sV, ev)
                            tpt = None
                            for kv in range(2):
                                pt = tpc[kv]
                                if len(tr_ev) >= 2:
                                    ph.wait("tensor", sV, tr_ev[-2])
                                src = sb[:, kv * w:(kv + 1) * w]
                                tpt = ph.run("tensor",
                                             lambda t, src=src, pt=pt:
                                             t.transpose(pt[0:w, :], src,
                                                         ident[:]), sPE)
                                ph.wait("vector", sPE, tpt)
                                dstc = cT[0:w, kv * T + tt * 128:
                                          kv * T + tt * 128 + 128]
                                ev2 = ph.run("vector",
                                             lambda v, pt=pt, dstc=dstc:
                                             v.tensor_copy(dstc, pt[0:w, :]), sV)
                                tr_ev.append(ev2)
                            c_tr_last.append(tpt)
                    ph.emit()
                if "Bsub=c" in debug:
                    return

                ph = Phase(nc, f"B{tier}qkv")
                with ExitStack() as ps:
                    EE = ps.enter_context
                    wcol = EE(nc.sbuf_tensor(f"B{tier}_wc", [128, ndt * 64], BF16))
                    wvc = EE(nc.sbuf_tensor(
                        f"B{tier}_wvc",
                        [128, ndt * (512 if tier != 2 else 256)], BF16))
                    pps = [EE(nc.psum_tensor(f"B{tier}q_p{i}", [128, 512], F32))
                           for i in range(2)]
                    ppv = [EE(nc.psum_tensor(f"B{tier}v_p{i}", [128, 512], F32))
                           for i in range(2)]
                    qstg = [EE(nc.sbuf_tensor(f"B{tier}_qs{i}", [64, 512], BF16))
                            for i in range(4)]
                    sWc = SP[0]
                    sWv = SP[1]
                    sQs = [SP[4], SP[5], SP[6], SP[7]]
                    ev_t = []
                    st_t = []
                    wld = []

                    def head_units(w_dram, hh, dst_dram, with_fb):
                        col0 = o0 + hh * 64
                        if wld:
                            ph.wait("gpsimd", sPE, wld[-1])
                        twl = ph.dma("gpsimd",
                                     wcol[:, 0:ndt * 64].rearrange(
                                         "p (j c) -> p j c", c=64),
                                     w_dram[0:ndt * 128,
                                            col0:col0 + 64].rearrange(
                                         "(j p) c -> p j c", p=128), sWc)
                        ph.wait("tensor", sWc, twl)
                        tpe = None
                        for tc in range(4):
                            u = len(ev_t)
                            pb = pps[u % 2]
                            if u >= 2:
                                ph.wait("tensor", sA, ev_t[u - 2])
                            for j in range(ndt):
                                last = (j == ndt - 1 and not with_fb)
                                tpe = ph.run(
                                    "tensor", lambda t, j=j, tc=tc, pb=pb,
                                    last=last:
                                    t.matmul(pb[0:64, :],
                                             wcol[:, j * 64:(j + 1) * 64],
                                             xnT[:, j * T + tc * 512:
                                                 j * T + tc * 512 + 512],
                                             start=(j == 0), stop=last),
                                    sPE if last else None)
                            if with_fb:
                                tpe = ph.run(
                                    "tensor", lambda t, hh=hh, tc=tc, pb=pb:
                                    t.matmul(pb[0:64, :],
                                             pkbd[tier][0:w,
                                                        hh * 64:hh * 64 + 64],
                                             cT[0:w, tc * 512:tc * 512 + 512],
                                             start=False, stop=True), sPE)
                            ph.wait("vector", sPE, tpe)
                            u2 = len(st_t)
                            sb2 = qstg[u2 % 4]
                            if u2 >= 4:
                                ph.wait("vector", sQs[u2 % 4], st_t[u2 - 4])
                            if "Bsub=noev" not in debug:
                                ph.wait("scalar", sPE, tpe)
                                ev = ph.run("scalar", lambda s, pb=pb, sb2=sb2:
                                            s.copy(sb2[:], pb[0:64, :]), sA)
                                ev_t.append(ev)
                                ph.wait("sync", sA, ev)
                            if "Bsub=noev" in debug or "Bsub=nost" in debug:
                                st_t.append(sSt.n)
                            else:
                                st = ph.dma(
                                    "sync",
                                    dst_dram[h0 + hh][:,
                                                      tc * 512:tc * 512 + 512],
                                    sb2[:], sQs[u2 % 4])
                                st_t.append(st)
                        wld.append(tpe)

                    nqh = nh if "Bsub=q1" not in debug else 1
                    for hh in range(nqh):
                        head_units(WQd, hh, QTb, False)
                        if "Bsub=qonly" not in debug and "Bsub=q1" not in debug:
                            head_units(WKd, hh, KTb, has_fb)
                    chunks = {0: [(0, 256)], 1: [(0, 512)],
                              2: [(0, 256), (256, 256), (512, 256)]}[tier]
                    if any(x in debug for x in ("Bsub=qonly", "Bsub=q1", "Bsub=noV")):
                        chunks = []
                    for (c0, wch) in chunks:
                        if wld:
                            ph.wait("gpsimd", sPE, wld[-1])
                        twl = ph.dma("gpsimd",
                                     wvc[:, 0:ndt * wch].rearrange(
                                         "p (j c) -> p j c", c=wch),
                                     WVd[0:ndt * 128,
                                         o0 + c0:o0 + c0 + wch].rearrange(
                                         "(j p) c -> p j c", p=128), sWv)
                        ph.wait("tensor", sWv, twl)
                        tpe = None
                        for tt in range(NT):
                            u = len(ev_t)
                            pb = ppv[u % 2]
                            if u >= 2:
                                ph.wait("tensor", sV, ev_t[u - 2])
                            for j in range(ndt):
                                last = (j == ndt - 1 and not has_fb)
                                tpe = ph.run(
                                    "tensor", lambda t, j=j, tt=tt, pb=pb,
                                    wch=wch, last=last:
                                    t.matmul(pb[:, 0:wch],
                                             xnT[:, j * T + tt * 128:
                                                 j * T + tt * 128 + 128],
                                             wvc[:, j * wch:(j + 1) * wch],
                                             start=(j == 0), stop=last),
                                    sPE if last else None)
                            if has_fb:
                                tpe = ph.run(
                                    "tensor", lambda t, tt=tt, pb=pb, c0=c0,
                                    wch=wch:
                                    t.matmul(pb[:, 0:wch],
                                             cT[0:w, T + tt * 128:
                                                T + tt * 128 + 128],
                                             pvbd[tier][0:w, c0:c0 + wch],
                                             start=False, stop=True), sPE)
                            ph.wait("vector", sPE, tpe)
                            ev = ph.run("vector", lambda v, pb=pb, tt=tt, c0=c0,
                                        wch=wch:
                                        v.tensor_copy(
                                            vts[:, tt * dim + c0:
                                                tt * dim + c0 + wch],
                                            pb[:, 0:wch]), sV)
                            ev_t.append(ev)
                        wld.append(tpe)
                    for ss in sQs:
                        ph.wait("sync", ss, ss.n)
                ph.emit()
                if "Bsub=qkv" in debug:
                    return

                ph = Phase(nc, f"B{tier}att")
                with ExitStack() as ps:
                    EE = ps.enter_context
                    psb = EE(nc.sbuf_tensor(f"B{tier}_P", [128, 4 * 512], BF16))
                    ptb = EE(nc.sbuf_tensor(f"B{tier}_PT", [128, 2 * 512], BF16))
                    zz = EE(nc.sbuf_tensor(f"B{tier}_z", [128, 12], F32))
                    obh = [EE(nc.sbuf_tensor(f"B{tier}_oh{i}", [128, NT * 64],
                                             BF16)) for i in range(2)]
                    obsT = EE(nc.sbuf_tensor(f"B{tier}_obT", [64, T], BF16))
                    qkr = EE(nc.sbuf_tensor(f"B{tier}_qk", [64, 2 * 2 * T], BF16))
                    scp = [EE(nc.psum_tensor(f"B{tier}_sc{i}", [128, 512], F32))
                           for i in range(4)]
                    tp = [EE(nc.psum_tensor(f"B{tier}_tp{i}", [128, 512], BF16))
                          for i in range(2)]
                    otp = [EE(nc.psum_tensor(f"B{tier}_ot{i}", [128, 64], F32))
                           for i in range(2)]
                    un = 0
                    gchunk = 0
                    chunk_exp = []
                    chunk_tpt = {}
                    zred_t = []
                    otev_t = []
                    tp_last = {}
                    st_t = []
                    hd_tr = []
                    h_ring_last = []
                    sQk = [SP[0], SP[1]]
                    sOb = SP[2]
                    last_tsc = None
                    for hh in range(nh):
                        slot = (hh % 2) * 2 * T
                        if hh >= 2:
                            ph.wait("gpsimd", sPE, h_ring_last[hh - 2])
                        ph.dma("gpsimd", qkr[:, slot:slot + T],
                               QTb[h0 + hh], sQk[hh % 2])
                        tldq = ph.dma("gpsimd", qkr[:, slot + T:slot + 2 * T],
                                      KTb[h0 + hh], sQk[hh % 2])
                        ph.wait("tensor", sQk[hh % 2], tldq)
                        ob = obh[hh % 2]
                        if hh >= 2:
                            ph.wait("scalar", sPE, hd_tr[hh - 2])
                        for i in range(NT):
                            nk = i + 1
                            nkc = (nk * 128 + 511) // 512
                            po = otp[un % 2]
                            if un >= 2:
                                ph.wait("tensor", sA, otev_t[un - 2])
                                ph.wait("scalar", sV, zred_t[un - 2])
                            sc_t = []
                            for kc in range(nkc):
                                wv_ = min(512, nk * 128 - kc * 512)
                                pb = scp[gchunk % 4]
                                if gchunk >= 4:
                                    ph.wait("tensor", sA, chunk_exp[gchunk - 4])
                                tsc = ph.run(
                                    "tensor",
                                    lambda t, kc=kc, wv_=wv_, slot=slot, i=i,
                                    pb=pb:
                                    t.matmul(
                                        pb[:, 0:wv_],
                                        qkr[0:64, slot + i * 128:
                                            slot + i * 128 + 128],
                                        qkr[0:64, slot + T + kc * 512:
                                            slot + T + kc * 512 + wv_],
                                        start=True, stop=True), sPE)
                                last_tsc = tsc
                                diag = (i * 128) // 512 == kc
                                tm = None
                                if diag:
                                    ph.wait("vector", sPE, tsc)
                                    off = (i * 128) % 512
                                    tm = ph.run(
                                        "vector", lambda v, pb=pb, off=off:
                                        v.tensor_tensor(
                                            pb[:, off:off + 128],
                                            pb[:, off:off + 128],
                                            maskw[:, 256:384], op=ALU.add), sV)
                                    ph.wait("scalar", sV, tm)
                                else:
                                    ph.wait("scalar", sPE, tsc)
                                if gchunk >= 4:
                                    ph.wait("scalar", sPE,
                                            chunk_tpt[gchunk - 4])
                                pcur = psb[:, (gchunk % 4) * 512:
                                           (gchunk % 4) * 512 + wv_]
                                zc = zz[:, (un % 2) * 4 + kc:
                                        (un % 2) * 4 + kc + 1]
                                te = ph.run(
                                    "scalar", lambda s, pb=pb, wv_=wv_, zc=zc,
                                    pcur=pcur:
                                    s.activation(
                                        pcur, pb[:, 0:wv_], AF.Exp,
                                        bias=cmisc[:, C_NB32:C_NB32 + 1],
                                        scale=cmisc[:, C_S64:C_S64 + 1],
                                        accum_out=zc), sA)
                                chunk_exp.append(te)
                                sc_t.append((gchunk, te, wv_))
                                gchunk += 1
                            for (gc, te, wv_) in sc_t:
                                nbl = (wv_ + 127) // 128
                                ph.wait("tensor", sA, te)
                                pt = tp[gc % 2]
                                gg = tp_last.get(gc % 2)
                                if gg:
                                    ph.wait("tensor", gg[1], gg[0])
                                tpt = None
                                for bi in range(nbl):
                                    tpt = ph.run(
                                        "tensor",
                                        lambda t, pt=pt, gc=gc, bi=bi:
                                        t.transpose(
                                            pt[:, bi * 128:(bi + 1) * 128],
                                            psb[:, (gc % 4) * 512 + bi * 128:
                                                (gc % 4) * 512 + bi * 128 + 128],
                                            ident[:]),
                                        sPE if bi == nbl - 1 else None)
                                chunk_tpt[gc] = tpt
                                eng, sem = (("vector", sV)
                                            if gc % 2 == 0
                                            else ("scalar", sA))
                                ph.wait(eng, sPE, tpt)
                                dst = ptb[:, (gc % 2) * 512:
                                          (gc % 2) * 512 + nbl * 128]
                                if eng == "vector":
                                    ev = ph.run("vector",
                                                lambda v, pt=pt, dst=dst, nbl=nbl:
                                                v.tensor_copy(
                                                    dst, pt[:, 0:nbl * 128]), sem)
                                else:
                                    ev = ph.run("scalar",
                                                lambda s, pt=pt, dst=dst, nbl=nbl:
                                                s.copy(dst, pt[:, 0:nbl * 128]),
                                                sem)
                                tp_last[gc % 2] = (ev, sem)
                                ph.wait("tensor", sem, ev)
                                tav = None
                                for bi in range(nbl):
                                    kt = (gc - sc_t[0][0]) * 4 + bi
                                    last = (gc == sc_t[-1][0] and bi == nbl - 1)
                                    tav = ph.run(
                                        "tensor",
                                        lambda t, kt=kt, hh=hh, po=po, gc=gc,
                                        bi=bi, first=(gc == sc_t[0][0]
                                                      and bi == 0), last=last:
                                        t.matmul(
                                            po[:],
                                            ptb[:, (gc % 2) * 512 + bi * 128:
                                                (gc % 2) * 512 + bi * 128 + 128],
                                            vts[:, kt * dim + hh * 64:
                                                kt * dim + hh * 64 + 64],
                                            start=first, stop=last),
                                        sPE if (last or bi == nbl - 1) else None)
                            Zc = 8 + un % 2
                            RZc = 10 + un % 2
                            ph.wait("vector", sA, chunk_exp[-1])
                            ph.run("vector", lambda v, nkc=nkc, un=un, Zc=Zc:
                                   v.reduce_sum(
                                       zz[:, Zc:Zc + 1],
                                       zz[:, (un % 2) * 4:(un % 2) * 4 + nkc],
                                       axis=AX.X))
                            trc = ph.run("vector", lambda v, Zc=Zc, RZc=RZc:
                                         v.reciprocal(
                                             zz[:, RZc:RZc + 1],
                                             zz[:, Zc:Zc + 1]), sV)
                            zred_t.append(trc)
                            ph.wait("scalar", sV, trc)
                            ph.wait("scalar", sPE, tav)
                            dst = ob[:, i * 64:(i + 1) * 64]
                            tev = ph.run(
                                "scalar", lambda s, po=po, dst=dst, RZc=RZc:
                                s.activation(dst, po[:], AF.Copy,
                                             scale=zz[:, RZc:RZc + 1]), sA)
                            otev_t.append(tev)
                            un += 1
                        h_ring_last.append(last_tsc)
                        ph.wait("tensor", sA, otev_t[-1])
                        if st_t:
                            pass
                        tails = []
                        for tg in range(0, NT, 4):
                            pt = tp[(tg // 4) % 2]
                            gg = tp_last.get((tg // 4) % 2)
                            if gg:
                                ph.wait("tensor", gg[1], gg[0])
                            tpt = None
                            for tt2 in range(tg, tg + 4):
                                tpt = ph.run(
                                    "tensor", lambda t, pt=pt, ob=ob, tt2=tt2,
                                    tg=tg:
                                    t.transpose(
                                        pt[0:64, (tt2 - tg) * 128:
                                           (tt2 - tg) * 128 + 128],
                                        ob[:, tt2 * 64:(tt2 + 1) * 64],
                                        ident[:]), sPE if tt2 == tg + 3 else None)
                            if tg == 0 and st_t:
                                ph.wait("vector", sOb, st_t[-1])
                            ph.wait("vector", sPE, tpt)
                            ev = ph.run(
                                "vector", lambda v, pt=pt, tg=tg:
                                v.tensor_copy(
                                    obsT[0:64, tg * 128:(tg + 4) * 128],
                                    pt[0:64, 0:512]), sV)
                            tp_last[(tg // 4) % 2] = (ev, sV)
                            tails.append(ev)
                        hd_tr.append(tpt)
                        ph.wait("sync", sV, tails[-1])
                        st = ph.dma("sync",
                                    ObT[rbase + hh // 2][(hh % 2) * 64:
                                                         (hh % 2) * 64 + 64, :],
                                    obsT[0:64, :], sOb)
                        st_t.append(st)
                    ph.wait("sync", sOb, sOb.n)
                ph.emit()

        for tier, pn in ((0, "B0"), (1, "B1"), (2, "B2")):
            if pn in phase_set:
                btier(tier)

        if "BWO" in phase_set:
            ph = Phase(nc, "BWO")
            with ExitStack() as ps:
                EE = ps.enter_context
                oball = EE(nc.sbuf_tensor("BW_ob", [128, 12 * T], BF16))
                wos2 = EE(nc.sbuf_tensor("BW_w", [128, 2 * 12 * 512], BF16))
                xsb = [EE(nc.sbuf_tensor(f"BW_x{i}", [128, 512], F32))
                       for i in range(2)]
                ynw = [EE(nc.sbuf_tensor(f"BW_y{i}", [128, 512], F32))
                       for i in range(2)]
                psy = [EE(nc.psum_tensor(f"BW_p{i}", [128, 512], F32))
                       for i in range(2)]
                sObl = SP[0]
                sWo2 = [SP[1], SP[2]]
                sXl2 = [SP[4], SP[5]]
                sYs = [SP[6], SP[7]]
                tob = 0
                for rt in range(12):
                    tob = ph.dma("gpsimd", oball[:, rt * T:(rt + 1) * T],
                                 ObT[rt], sObl)
                ph.wait("tensor", sObl, tob)
                nrc_by_oc = [6, 6, 12, 12]
                ev_t, st_t, wo_pe = [], [], []
                for oc in range(4):
                    nrc = nrc_by_oc[oc]
                    half = (oc % 2) * 12 * 512
                    if oc >= 2:
                        ph.wait("gpsimd", sPE, wo_pe[oc - 2])
                    twl = ph.dma("gpsimd",
                                 wos2[:, half:half + nrc * 512].rearrange(
                                     "p (j c) -> p j c", c=512),
                                 WOd[0:nrc * 128, oc * 512:(oc + 1) * 512]
                                 .rearrange("(j p) c -> p j c", p=128),
                                 sWo2[oc % 2])
                    ph.wait("tensor", sWo2[oc % 2], twl)
                    tpe = None
                    for tt in range(NT):
                        u = len(ev_t)
                        pb = psy[u % 2]
                        if u >= 2:
                            ph.wait("tensor", sV, ev_t[u - 2])
                        for r in range(nrc):
                            tpe = ph.run(
                                "tensor", lambda t, r=r, tt=tt, pb=pb, half=half:
                                t.matmul(pb[:],
                                         oball[:, r * T + tt * 128:
                                               r * T + tt * 128 + 128],
                                         wos2[:, half + r * 512:
                                              half + r * 512 + 512],
                                         start=(r == 0), stop=(r == nrc - 1)),
                                sPE if r == nrc - 1 else None)
                        tx = ph.dma("sync", xsb[u % 2][:],
                                    xres[tt][:, oc * 512:(oc + 1) * 512],
                                    sXl2[u % 2])
                        ph.wait("vector", sPE, tpe)
                        ph.wait("vector", sXl2[u % 2], tx)
                        if u >= 2:
                            ph.wait("vector", sYs[u % 2], st_t[u - 2])
                        ev = ph.run("vector", lambda v, pb=pb, u=u:
                                    v.tensor_add(ynw[u % 2][:], pb[:],
                                                 xsb[u % 2][:]), sV)
                        ev_t.append(ev)
                        ph.wait("sync", sV, ev)
                        st = ph.dma("sync", xres[tt][:, oc * 512:(oc + 1) * 512],
                                    ynw[u % 2][:], sYs[u % 2])
                        st_t.append(st)
                    wo_pe.append(tpe)
                ph.wait("sync", sYs[0], sYs[0].n)
                ph.wait("sync", sYs[1], sYs[1].n)
            ph.emit()

        if "LN4" in phase_set:
            ln_phase("LN4", lambda tt: xres[tt])
        if "F1b" in phase_set:
            f1_phase("F1b")
        if "F2b" in phase_set:
            f2_phase("F2b", lambda tt, oc: out_d[tt * 128:(tt + 1) * 128,
                                                 oc * 512:(oc + 1) * 512])

        dnames = [n for n in debug if ":" not in n and n in scr]
        if dnames:
            ph = Phase(nc, "DBG")
            td = 0
            for name in dnames:
                s = scr[name]
                o = nc.dram_tensor("dbg_" + name, list(s.shape), s.dtype,
                                   kind="ExternalOutput").ap()
                dbg_outs[name] = o
                td = ph.dma("sync", o, s, sSt)
            ph.wait("sync", sSt, td)
            ph.emit()

    import os
    if os.environ.get("KDBG_SEMS"):
        names = ["sL", "sG", "sSt", "sPE", "sV", "sA"]
        for i, sm in enumerate(_dbg_sems):
            print(f"SEMTOTAL {names[i] if i < 6 else i}: {sm.n}")
    return nc, dbg_outs


def _blockdiag(P):
    P = np.asarray(P, np.float32)
    nh = P.shape[0]
    out = np.zeros((nh * 8, nh * 64), np.float32)
    for h in range(nh):
        out[h * 8:(h + 1) * 8, h * 64:(h + 1) * 64] = P[h]
    return out


def _prep_core_inputs(b, I):
    f32 = np.float32
    mask0 = np.where(np.arange(128)[None, :] > np.arange(128)[:, None],
                     0.0, NEG).astype(f32)
    mask2 = np.where(np.arange(128)[None, :] <= np.arange(128)[:, None],
                     0.0, NEG).astype(f32)
    maskw = np.concatenate([mask0, np.zeros((128, 128), f32), mask2], axis=1)
    d = {
        "x": I["L0_attn_a_hidden_states"][b],
        "wqT": I["L0_attn_a_wq"].T,
        "wkT": I["L0_attn_a_wk"].T,
        "wvT": I["L0_attn_a_wv"].T,
        "woT": I["L0_attn_a_wo"].T,
        "wo_nat": I["L0_attn_a_wo"],
        "gwT_rs": I["L0_attn_a_gate_w"].reshape(16, 128).T,
        "gb_bc": np.tile(np.asarray(I["L0_attn_a_gate_b"]).reshape(1, 1),
                         (128, 1)).astype(f32),
        "membT": I["L0_attn_a_mem_bank"].T,
        "WQ": I["L0_attn_b_W_Q"], "WK": I["L0_attn_b_W_K"],
        "WV": I["L0_attn_b_W_V"], "WO": I["L0_attn_b_W_O"],
        "FK0": I["L0_attn_b_FK0"], "FV0": I["L0_attn_b_FV0"],
        "FK1": I["L0_attn_b_FK1"], "FV1": I["L0_attn_b_FV1"],
        "PK0bd": _blockdiag(I["L0_attn_b_PK0"]),
        "PV0bd": _blockdiag(I["L0_attn_b_PV0"]),
        "PK1bd": _blockdiag(I["L0_attn_b_PK1"]),
        "PV1bd": _blockdiag(I["L0_attn_b_PV1"]),
        "wgR": np.asarray(I["L0_ffn_w_gate"]).reshape(
            16, 128, 16, 512).transpose(2, 0, 1, 3),
        "wuR": np.asarray(I["L0_ffn_w_up"]).reshape(
            16, 128, 16, 512).transpose(2, 0, 1, 3),
        "wd": I["L0_ffn_w_down"],
        "sg_rs": np.asarray(I["L0_ffn_s_gate"]).reshape(64, 128).T,
        "su_rs": np.asarray(I["L0_ffn_s_up"]).reshape(64, 128).T,
        "sdbc": np.tile(np.asarray(I["L0_ffn_s_down"]).reshape(1, D), (128, 1)),
        "ident": np.eye(128, dtype=ml_dtypes.bfloat16),
        "maskw": maskw,
    }
    return {k: np.ascontiguousarray(v) for k, v in d.items()}


def kernel(**inputs):
    nc, dbg = build_nc()
    in_maps = [_prep_core_inputs(b, inputs) for b in range(8)]
    res = run_bass_kernel_spmd(nc, in_maps, core_ids=list(range(8)))
    return np.stack([np.asarray(res.results[b]["out"]) for b in range(8)], axis=0)
